# version 13
# baseline (speedup 1.0000x reference)
import contextlib
import numpy as np
import ml_dtypes
import concourse.bass as bass
import concourse.mybir as mybir
from concourse.bass_utils import run_bass_kernel_spmd

F32 = mybir.dt.float32
BF16 = mybir.dt.bfloat16
AF = mybir.ActivationFunctionType
ALU = mybir.AluOpType
AX = mybir.AxisListType

NCORE = 8
S, E, H, D = 4096, 1024, 16, 64
TC, C = 128, 32
G, NG = 8, 4
NF = 8
HID, NE, NHC = 4096, 8, 32
TT, NTILE, TS = 1024, 4, 8
HF = 512
EPS = 1e-5
SCL = 0.125
SLAB = 2
NSLAB = NHC // SLAB
BIG = 1.0e30
B_HEADS = True
B_TAILS = True
B_EXPERTS = True
B_W2 = True
B_GATE = True
SPARSE = False
CAP = 384
NSUB = CAP // 128

ENG = ("pe", "act", "dve", "sy", "pl")
SEMS = ("pe", "act", "dve", "io", "wl", "plc", "w1s0", "w1s1", "w1s2")


class _Dummy:
    def then_inc(self, *a, **k):
        return self


_DUM = _Dummy()


class Eng:

    def __init__(self, em, name):
        object.__setattr__(self, "_em", em)
        object.__setattr__(self, "_name", name)

    def __getattr__(self, op):
        em = self._em
        if getattr(em, "sched", False):
            name = self._name
            def rec(*a, **k):
                item = ["op", op, []]
                em.prog[name].append(item)
                em._last_op = item
                return _DUM
            return rec
        if em.active == self._name and em.engine is not None:
            return getattr(em.engine, op)
        return lambda *a, **k: _DUM


class Em:

    def __init__(self, active, sems):
        self.active = active
        self.engine = None
        self.sems = sems
        self.ct = {k: 0 for k in SEMS}
        self.ms = {}
        self._lastwait = {}

    def inc(self, sem, inst, n=1):
        self.ct[sem] += n
        if getattr(self, "sched", False):
            self._last_op[2].append((sem, n))
        elif not isinstance(inst, _Dummy):
            inst.then_inc(self.sems[sem], n)
        return self.ct[sem]

    def mark(self, key, sem):
        self.ms[key] = (sem, self.ct[sem])

    def wait(self, eng_name, key):
        sem, val = self.ms[key]
        self._wait_val(eng_name, sem, val)

    def wait_ct(self, eng_name, sem):
        self._wait_val(eng_name, sem, self.ct[sem])

    def _wait_val(self, eng_name, sem, val):
        if val <= 0:
            return
        if getattr(self, "sched", False):
            last = self._lastwait.get((eng_name, sem), -1)
            if val <= last:
                return
            self._lastwait[(eng_name, sem)] = val
            self.prog[eng_name].append(["wait", sem, val, "wait_ct"])
            return
        if self.active == eng_name and self.engine is not None:
            last = self._lastwait.get(sem, -1)
            if val <= last:
                return
            self._lastwait[sem] = val
            self.engine.wait_ge(self.sems[sem], val)


def _simulate(prog):
    counts = {k: 0 for k in SEMS}
    idx = {e: 0 for e in ENG}
    progress = True
    while progress:
        progress = False
        for e in ENG:
            stream = prog[e]
            while idx[e] < len(stream):
                it = stream[idx[e]]
                if it[0] == "wait":
                    if counts[it[1]] >= it[2]:
                        idx[e] += 1
                        progress = True
                    else:
                        break
                else:
                    for sem, n in it[2]:
                        counts[sem] += n
                    idx[e] += 1
                    progress = True
    stuck = {e: idx[e] for e in ENG if idx[e] < len(prog[e])}
    if stuck:
        msg = []
        for e, i in stuck.items():
            msg.append(f"{e}@{i}/{len(prog[e])}: {prog[e][i]} counts_now={counts[prog[e][i][1]] if prog[e][i][0]=='wait' else ''}")
        raise RuntimeError("DEADLOCK in schedule: " + " | ".join(msg))


class Bank:

    def __init__(self, em, name, t):
        self.em, self.name, self.t = em, name, t
        self.fills = 0
        self.frees = 0

    def producer_wait(self):
        assert self.frees == self.fills, (self.name, self.frees, self.fills)
        if self.fills > 0:
            self.em.wait("pe", f"{self.name}.fr{self.fills}")

    def fill(self, inst):
        self.em.inc("pe", inst)
        self.fills += 1
        self.em.mark(f"{self.name}.fi{self.fills}", "pe")

    def consumer_wait(self, eng):
        self.em.wait(eng, f"{self.name}.fi{self.fills}")

    def free(self, inst, sem):
        self.em.inc(sem, inst)
        self.frees += 1
        assert self.frees == self.fills, (self.name, self.frees, self.fills)
        self.em.mark(f"{self.name}.fr{self.frees}", sem)


class NS:
    pass


def declare_dram(nc):
    t = NS()
    def di(name, shape, dt=F32):
        return nc.dram_tensor(name, shape, dt, kind="ExternalInput").ap()
    t.x = di("x", [S, E])
    t.wqT = di("wqT", [E, E]); t.wkT = di("wkT", [E, E])
    t.wvT = di("wvT", [E, E]); t.woT = di("woT", [E, E])
    t.bqT = di("bqT", [128, NF]); t.bkT = di("bkT", [128, NF])
    t.bv = di("bv", [1, E]); t.bo = di("bo", [1, E])
    t.gw = di("gw", [E, NE]); t.gb = di("gb", [NE, 1])
    t.w1 = di("w1", [NE, E, HID]); t.b1T = di("b1T", [NE, 128, NHC])
    t.w2 = di("w2", [NE, HID, E]); t.b2 = di("b2", [NE, E])
    t.ws1 = di("ws1", [E, HID]); t.bs1T = di("bs1T", [128, NHC])
    t.ws2 = di("ws2", [HID, E]); t.bs2 = di("bs2", [1, E])
    t.ident = di("ident", [128, 128], BF16)
    t.ltri = di("ltri", [128, 128], BF16)
    t.evec = di("evec", [128, NE])
    t.tokid = di("tokid", [128, C], mybir.dt.int32)
    t.ln2d = nc.dram_tensor("ln2d", [S, E], F32).ap()
    t.combd = nc.dram_tensor("combd", [S, NE], F32).ap()
    t.idxd = nc.dram_tensor("idxd", [CAP * NE, 1], mybir.dt.int32).ap()
    t.out = nc.dram_tensor("out", [S, E], F32, kind="ExternalOutput").ap()
    t.x2d = nc.dram_tensor("x2d", [S, E], F32, kind="ExternalOutput").ap()
    return t


def emit_init(em, t):
    pe, act, dve, sy, pl = (Eng(em, n) for n in ENG)
    em.inc("plc", pl.memset(t.ones_bf[:], 1.0), 1)
    em.inc("plc", pl.memset(t.ones_col[:], 1.0), 1)
    em.inc("plc", pl.memset(t.eps_col[:], EPS), 1)
    em.inc("plc", pl.memset(t.ones_f[:], 1.0), 1)
    em.mark("consts", "plc")
    for dst, src in ((t.ident_sb, t.ident), (t.ltri_sb, t.ltri), (t.evec_sb, t.evec),
                     (t.tokid_sb, t.tokid), (t.bqTs, t.bqT), (t.bkTs, t.bkT),
                     (t.bs1Ts, t.bs1T)):
        em.wait_ct("sy", "io")
        em.inc("io", sy.dma_start(out=dst[:], in_=src), 16)
    em.wait_ct("sy", "io")
    em.wait_ct("sy", "io")
    em.inc("io", sy.dma_start(out=t.gb_col[:], in_=t.gb), 16)
    em.wait_ct("sy", "io")
    em.inc("io", sy.dma_start(out=t.b1Ts[:], in_=t.b1T.rearrange("e p h -> p e h")), 16)
    em.mark("persistf32", "io")
    for dst, src in ((t.bs2_bf, t.bs2), (t.b2_bf, t.b2)):
        em.wait_ct("pl", "wl")
        em.inc("wl", pl.dma_start(out=dst[:], in_=src), 16)
    em.wait_ct("pl", "wl")
    em.inc("wl", pl.dma_start(out=t.gw_sb[:], in_=t.gw.rearrange("(fc p) e -> p fc e", p=128)), 16)
    em.mark("persistbf", "wl")
    em.wait("dve", "persistf32")
    em.inc("dve", dve.tensor_scalar_mul(t.bqTs[:], t.bqTs[:], SCL))
    em.mark("bqscaled", "dve")


def emit_phase_a(em, t, a):
    pe, act, dve, sy, pl = (Eng(em, n) for n in ENG)
    b4 = [a.ps_mm0, a.ps_mm1, a.ps_sA, a.ps_sB]
    scuse = {}
    trb = [Bank(em, "Atr0", a.ps_trA), Bank(em, "Atr1", a.ps_trB)]
    atb = [Bank(em, "Aat0", a.ps_at0), Bank(em, "Aat1", a.ps_at1)]
    mmi = [0]

    def bank_acquire(j):
        if j in scuse:
            em.wait("pe", scuse[j])

    def bank_release(j, key):
        scuse[j] = key

    em.wait_ct("pl", "wl")
    em.inc("wl", pl.dma_start(out=a.bv_bf[:], in_=t.bv), 16)
    em.wait_ct("pl", "wl")
    em.inc("wl", pl.dma_start(out=a.bo_bf[:], in_=t.bo), 16)
    for dst, src in ((a.wq_sb, t.wqT), (a.wk_sb, t.wkT), (a.wv_sb, t.wvT),
                     (a.wo_sb, t.woT)):
        em.wait_ct("pl", "wl")
        em.inc("wl", pl.dma_start(out=dst[:], in_=src.rearrange("(fi p) o -> p fi o", p=128)), 16)
    em.mark("attnw", "wl")

    x_ch = t.x.rearrange("(c p) f -> c p f", p=128)
    x2_ch = t.x2d.rearrange("(c p) f -> c p f", p=128)

    def ln_chunk(c):
        cl4 = c % 4
        st = a.stat
        s0 = (c % 2) * 10
        if c >= 4:
            em.wait("sy", f"xn{c - 4}")
        em.wait_ct("sy", "io")
        em.inc("io", sy.dma_start(out=a.x_ln[:, cl4, :], in_=x_ch[c]), 16)
        em.mark(f"xl{c}", "io")
        xs = a.x_ln[:, cl4, :]
        em.wait("act", f"xl{c}")
        em.inc("act", act.activation(a.sq_scr[:], xs, AF.Square,
                                     accum_out=st[:, s0 + 1:s0 + 2]))
        em.mark(f"sq{c}", "act")
        em.wait("dve", f"xl{c}")
        em.inc("dve", dve.reduce_sum(st[:, s0 + 0:s0 + 1], xs, axis=AX.X))
        dve.drain()
        em.wait("dve", f"sq{c}")
        dve.tensor_scalar_mul(st[:, s0 + 2:s0 + 3], st[:, s0 + 0:s0 + 1], 1.0 / E)
        dve.tensor_scalar_mul(st[:, s0 + 3:s0 + 4], st[:, s0 + 1:s0 + 2], 1.0 / E)
        dve.drain()
        dve.tensor_mul(st[:, s0 + 4:s0 + 5], st[:, s0 + 2:s0 + 3], st[:, s0 + 2:s0 + 3])
        dve.drain()
        em.inc("dve", dve.tensor_sub(st[:, s0 + 5:s0 + 6], st[:, s0 + 3:s0 + 4], st[:, s0 + 4:s0 + 5]))
        em.mark(f"var{c}", "dve")
        em.wait("act", f"var{c}")
        em.wait("act", "persistbf")
        em.wait("act", "consts")
        act.activation(st[:, s0 + 6:s0 + 7], st[:, s0 + 5:s0 + 6], AF.Ln, bias=t.eps_col[:, 0:1], scale=1.0)
        act.drain()
        em.inc("act", act.activation(st[:, s0 + 7:s0 + 8], st[:, s0 + 6:s0 + 7], AF.Exp, bias=0.0, scale=-0.5))
        em.mark(f"rstd{c}", "act")
        em.wait("dve", f"rstd{c}")
        dve.drain()
        em.inc("dve", dve.scalar_tensor_tensor(st[:, s0 + 8:s0 + 9], st[:, s0 + 2:s0 + 3], -1.0,
                                               st[:, s0 + 7:s0 + 8],
                                               op0=ALU.mult, op1=ALU.mult))
        em.mark(f"nmr{c}", "dve")
        em.wait("act", f"nmr{c}")
        if c >= 2:
            em.wait("act", f"xnused{c - 2}")
        em.inc("act", act.activation(a.xn_sb[:, c % 2, :], xs, AF.Identity,
                                     bias=st[:, s0 + 8:s0 + 9], scale=st[:, s0 + 7:s0 + 8]))
        em.mark(f"xn{c}", "act")
        for fc in range(NF):
            b = trb[fc % 2]
            b.producer_wait()
            em.wait("pe", f"xn{c}")
            em.wait("pe", "persistf32")
            ins = pe.transpose(b.t[:, 0:128], a.xn_sb[:, c % 2, fc * 128:(fc + 1) * 128], t.ident_sb[:])
            b.fill(ins)
            b.consumer_wait("act")
            b.free(act.copy(a.ln1T[:, fc, (c % G) * 128:(c % G + 1) * 128], b.t[:, 0:128]), "act")
        em.mark(f"xnused{c}", "pe")
        em.mark(f"lnTc{c}", "act")

    for cl in range(G):
        ln_chunk(cl)

    for g in range(NG):
        em.wait("pe", "attnw")
        for which, wsb, outq in (("q", a.wq_sb, a.qT), ("k", a.wk_sb, a.kT)):
            for fo in range(NF):
                for hf in range(2):
                    j = mmi[0] % 2
                    mmi[0] += 1
                    bank_acquire(j)
                    em.wait("pe", f"lnTc{g * G + 4 * hf + 3}")
                    for fi in range(NF):
                        ins = pe.matmul(b4[j][:], wsb[:, fi, fo * 128:(fo + 1) * 128],
                                        a.ln1T[:, fi, hf * HF:(hf + 1) * HF],
                                        start=(fi == 0), stop=(fi == NF - 1))
                    em.inc("pe", ins)
                    em.mark(f"pj{g}_{which}{fo}_{hf}", "pe")
                    em.wait("act", f"pj{g}_{which}{fo}_{hf}")
                    if which == "q":
                        em.wait("act", "bqscaled")
                        ins = act.activation(outq[:, fo, hf * HF:(hf + 1) * HF], b4[j][:],
                                             AF.Identity, bias=t.bqTs[:, fo:fo + 1], scale=SCL)
                    else:
                        em.wait("act", "persistf32")
                        ins = act.activation(outq[:, fo, hf * HF:(hf + 1) * HF], b4[j][:],
                                             AF.Identity, bias=t.bkTs[:, fo:fo + 1], scale=1.0)
                    em.inc("act", ins)
                    em.mark(f"pjev{g}_{which}{fo}_{hf}", "act")
                    bank_release(j, f"pjev{g}_{which}{fo}_{hf}")
            em.mark(f"{which}T{g}", "act")

        for cl in range(G):
            c = g * G + cl
            for hf in range(2):
                j = mmi[0] % 2
                mmi[0] += 1
                bank_acquire(j)
                em.wait("pe", f"lnTc{c}")
                em.wait("pe", "attnw")
                em.wait("pe", "consts")
                for fi in range(NF):
                    pe.matmul(b4[j][:], a.ln1T[:, fi, cl * 128:(cl + 1) * 128],
                              a.wv_sb[:, fi, hf * HF:(hf + 1) * HF],
                              start=(fi == 0), stop=False)
                ins = pe.matmul(b4[j][:], t.ones_bf[0:1, 0:128], a.bv_bf[0:1, hf * HF:(hf + 1) * HF],
                                start=False, stop=True)
                em.inc("pe", ins)
                em.mark(f"vf{c}_{hf}", "pe")
                em.wait("act", f"vf{c}_{hf}")
                em.inc("act", act.copy(a.v_sb[:, cl, hf * HF:(hf + 1) * HF], b4[j][:]))
                em.mark(f"v{c}_{hf}", "act")
                bank_release(j, f"v{c}_{hf}")
            em.wait("dve", f"kT{g}")
            if c > 0:
                em.wait("dve", f"km{c - 1}")
            for fo in range(NF):
                ins = dve.reduce_sum(a.km_scr[:, fo:fo + 1], a.kT[:, fo, cl * 128:(cl + 1) * 128], axis=AX.X)
            em.inc("dve", ins)
            em.mark(f"kmraw{c}", "dve")
            em.wait("act", f"kmraw{c}")
            em.inc("act", act.activation(a.kmeansT[:, :, c:c + 1],
                                         a.km_scr[:].rearrange("p (e o) -> p e o", o=1),
                                         AF.Copy, bias=0.0, scale=1.0 / TC))
            em.mark(f"km{c}", "act")
            for hf in range(2):
                j = 2 + hf
                bank_acquire(j)
                em.wait("pe", f"v{c}_{hf}")
                em.wait("pe", "consts")
                ins = pe.matmul(b4[j][0:1, 0:HF], t.ones_col[:, 0:1],
                                a.v_sb[:, cl, hf * HF:(hf + 1) * HF], start=True, stop=True)
                em.inc("pe", ins)
                em.mark(f"vmf{c}_{hf}", "pe")
                em.wait("act", f"vmf{c}_{hf}")
                if c > 0:
                    em.wait("act", f"vmdma{c - 1}")
                em.inc("act", act.activation(a.vm_stage[0:1, hf * HF:(hf + 1) * HF], b4[j][0:1, 0:HF],
                                             AF.Copy, bias=0.0, scale=1.0 / TC))
                em.mark(f"vmc{c}_{hf}", "act")
                bank_release(j, f"vmc{c}_{hf}")
            em.mark(f"vmstage{c}", "act")
            em.wait("pl", f"vmstage{c}")
            em.wait_ct("pl", "wl")
            em.inc("wl", pl.dma_start(out=a.vmeans[c:c + 1, :], in_=a.vm_stage[0:1, :]), 16)
            em.mark(f"vmdma{c}", "wl")

        def chunk_pro(cl):
            c = g * G + cl
            if c >= 2:
                em.wait("sy", f"resid{c - 2}")
            em.wait_ct("sy", "io")
            em.inc("io", sy.dma_start(out=a.x_res[:, c % 2, :], in_=x_ch[c]), 16)
            em.mark(f"xresl{c}", "io")

        def head_scores(cl, h):
            c = g * G + cl
            nj = c + 1
            ghi = c * H + h
            hs = h % 16
            ji = h % 2
            jj = 2 + h % 2
            cp, hp = divmod(ghi - 16, H)
            qs = a.qT[(h % 2) * 64:(h % 2) * 64 + 64, h // 2, cl * 128:(cl + 1) * 128]
            bank_acquire(ji)
            em.wait("pe", f"qT{g}")
            em.wait("pe", f"kT{g}")
            ins = pe.matmul(b4[ji][:, 0:128], qs,
                            a.kT[(h % 2) * 64:(h % 2) * 64 + 64, h // 2, cl * 128:(cl + 1) * 128],
                            start=True, stop=True)
            em.inc("pe", ins)
            em.mark(f"sci{c}_{h}", "pe")
            bank_acquire(jj)
            em.wait("pe", f"km{c}")
            ins = pe.matmul(b4[jj][:, 0:nj], qs,
                            a.kmeansT[(h % 2) * 64:(h % 2) * 64 + 64, h // 2, 0:nj],
                            start=True, stop=True)
            em.inc("pe", ins)
            em.mark(f"scj{c}_{h}", "pe")
            em.wait("dve", f"sci{c}_{h}")
            em.wait("dve", f"scj{c}_{h}")
            if ghi - 16 in a.seen_exp:
                em.wait("dve", f"expi{cp}_{hp}")
                em.wait("dve", f"expj{cp}_{hp}")
            dve.reduce_max(a.negmax[:, hs:hs + 1], b4[ji][:, 0:128], axis=AX.X, negate=True)
            ins = dve.reduce_max(a.negmax[:, 16 + hs:17 + hs], b4[jj][:, 0:nj],
                                 axis=AX.X, negate=True)
            em.inc("dve", ins)
            em.mark(f"nm{c}_{h}", "dve")
            em.wait("act", f"nm{c}_{h}")
            if ghi - 16 in a.seen_an:
                em.wait("act", f"an{cp}_{hp}")
            em.inc("act", act.activation(a.ascr[:, hs, :], b4[ji][:, 0:128], AF.Exp,
                                         bias=a.negmax[:, hs:hs + 1], scale=1.0,
                                         accum_out=a.sums[:, hs:hs + 1]))
            em.mark(f"expi{c}_{h}", "act")
            bank_release(ji, f"expi{c}_{h}")
            ins = act.activation(a.aiscr[:, hs, 0:nj], b4[jj][:, 0:nj], AF.Exp,
                                 bias=a.negmax[:, 16 + hs:17 + hs], scale=1.0,
                                 accum_out=a.sums[:, 16 + hs:17 + hs])
            em.inc("act", ins)
            em.mark(f"expj{c}_{h}", "act")
            bank_release(jj, f"expj{c}_{h}")
            a.seen_exp.add(ghi)

        def head_norm(cl, h):
            c = g * G + cl
            nj = c + 1
            ghi = c * H + h
            hs = h % 16
            cp, hp = divmod(ghi - 16, H)
            em.wait("dve", f"expi{c}_{h}")
            em.wait("dve", f"expj{c}_{h}")
            if ghi - 16 in a.seen_an:
                em.wait("dve", f"an{cp}_{hp}")
            dve.reciprocal(a.recip[:, hs:hs + 1], a.sums[:, hs:hs + 1])
            ins = dve.reciprocal(a.recip[:, 16 + hs:17 + hs], a.sums[:, 16 + hs:17 + hs])
            em.inc("dve", ins)
            em.mark(f"recip{c}_{h}", "dve")
            if ghi - 16 in a.seen_tra:
                em.wait("dve", f"tra{cp}_{hp}")
            dve.drain()
            dve.tensor_scalar_mul(a.an_scr[:, hs, :], a.ascr[:, hs, :], a.recip[:, hs:hs + 1])
            ins = dve.tensor_scalar_mul(a.ain_scr[:, hs, 0:nj], a.aiscr[:, hs, 0:nj],
                                        a.recip[:, 16 + hs:17 + hs])
            em.inc("dve", ins)
            em.mark(f"an{c}_{h}", "dve")
            a.seen_an.add(ghi)

        def head_tail(cl, h):
            c = g * G + cl
            nj = c + 1
            ghi = c * H + h
            hs = h % 16
            cp, hp = divmod(ghi - 16, H)
            bA = trb[0]
            bA.producer_wait()
            em.wait("pe", f"an{c}_{h}")
            ins = pe.transpose(bA.t[:, 0:128], a.an_scr[:, hs, :], t.ident_sb[:])
            bA.fill(ins)
            bB = trb[1]
            bB.producer_wait()
            ins = pe.transpose(bB.t[0:nj, 0:128], a.ain_scr[:, hs, 0:nj], t.ident_sb[:])
            bB.fill(ins)
            em.mark(f"tra{c}_{h}", "pe")
            a.seen_tra.add(ghi)
            bA.consumer_wait("act")
            if ghi - 16 in a.seen_atm:
                em.wait("act", f"atmm{cp}_{hp}")
            bA.free(act.copy(a.aT_sb[:, hs, :], bA.t[:, 0:128]), "act")
            em.mark(f"aT{c}_{h}", "act")
            bB.consumer_wait("act")
            bB.free(act.copy(a.aiT_sb[0:nj, hs, :], bB.t[0:nj, 0:128]), "act")
            em.mark(f"aiT{c}_{h}", "act")
            pair = h // 2
            bk = atb[(ghi // 2) % 2]
            if h % 2 == 0:
                bk.producer_wait()
            em.wait("pe", f"aT{c}_{h}")
            pe.matmul(bk.t[(h % 2) * 64:(h % 2) * 64 + 64, 0:128],
                      a.v_sb[:, cl, h * 64:(h + 1) * 64],
                      a.aT_sb[:, hs, :], start=True, stop=False)
            em.wait("pe", f"aiT{c}_{h}")
            em.wait("pe", f"vmdma{c}")
            ins = pe.matmul(bk.t[(h % 2) * 64:(h % 2) * 64 + 64, 0:128],
                            a.vmeans[0:nj, h * 64:(h + 1) * 64],
                            a.aiT_sb[0:nj, hs, :], start=False, stop=True)
            if h % 2 == 1:
                bk.fill(ins)
            else:
                em.inc("pe", ins)
            em.mark(f"atmm{c}_{h}", "pe")
            a.seen_atm.add(ghi)
            if h % 2 == 1:
                bk.consumer_wait("act")
                if c > 0:
                    em.wait("act", f"oproj{c - 1}")
                bk.free(act.copy(a.attnT[:, pair, :], bk.t[:, 0:128]), "act")
                em.mark(f"atevac{c}_{pair}", "act")

        def chunk_epi(cl):
            c = g * G + cl
            for hf in range(2):
                j = mmi[0] % 2
                mmi[0] += 1
                bank_acquire(j)
                if B_HEADS and B_TAILS:
                    em.wait("pe", f"atevac{c}_{H // 2 - 1}")
                for fc in range(NF):
                    pe.matmul(b4[j][:], a.attnT[:, fc, :],
                              a.wo_sb[:, fc, hf * HF:(hf + 1) * HF],
                              start=(fc == 0), stop=False)
                ins = pe.matmul(b4[j][:], t.ones_bf[0:1, 0:128], a.bo_bf[0:1, hf * HF:(hf + 1) * HF],
                                start=False, stop=True)
                em.inc("pe", ins)
                em.mark(f"opf{c}_{hf}", "pe")
                em.wait("dve", f"opf{c}_{hf}")
                em.wait("dve", f"xresl{c}")
                if c >= 2:
                    em.wait("dve", f"x2w{c - 2}")
                ins = dve.tensor_add(a.xo_sb[:, cl % 2, hf * HF:(hf + 1) * HF], b4[j][:],
                                     a.x_res[:, c % 2, hf * HF:(hf + 1) * HF])
                em.inc("dve", ins)
                em.mark(f"opr{c}_{hf}", "dve")
                bank_release(j, f"opr{c}_{hf}")
            em.mark(f"oproj{c}", "pe")
            em.mark(f"resid{c}", "dve")
            em.wait("sy", f"resid{c}")
            em.wait_ct("sy", "io")
            em.inc("io", sy.dma_start(out=x2_ch[c], in_=a.xo_sb[:, cl % 2, :]), 16)
            em.mark(f"x2w{c}", "io")
            if g + 1 < NG:
                ln_chunk((g + 1) * G + cl)

        LAG = 12
        NQ = G * H
        if B_HEADS:
            for idx in range(NQ + LAG):
                if idx < NQ:
                    cl, h = divmod(idx, H)
                    if h == 0:
                        chunk_pro(cl)
                    head_scores(cl, h)
                    head_norm(cl, h)
                if idx >= LAG and B_TAILS:
                    cl2, h2 = divmod(idx - LAG, H)
                    head_tail(cl2, h2)
                    if h2 == H - 1:
                        chunk_epi(cl2)
            if not B_TAILS:
                for cl in range(G):
                    chunk_epi(cl)
        else:
            for cl in range(G):
                chunk_pro(cl)
                chunk_epi(cl)


def emit_phase_b(em, t, b):
    pe, act, dve, sy, pl = (Eng(em, n) for n in ENG)
    ph = [Bank(em, "Bh0", b.ph0), Bank(em, "Bh1", b.ph1)]
    po = [Bank(em, "Bo0", b.po0), Bank(em, "Bo1", b.po1)]

    x2_tile = t.x2d.rearrange("(ti ts p) f -> ti p ts f", ts=TS, p=128)
    out_tile = t.out.rearrange("(ti ts p) f -> ti p ts f", ts=TS, p=128)

    for k in ("endA_pe", "endA_act", "endA_dve", "endA_io", "endA_wl", "endA_plc"):
        for e in ENG:
            em.wait(e, k)

    prev_w2used = [None]
    w1_slot_used = [None, None, None]
    if SPARSE:
        b.bnd_s = pl.alloc_register("bnd_s")
        pl.reg_mov(b.bnd_s, S - 1)
        b.bnd_i = pl.alloc_register("bnd_i")
        pl.reg_mov(b.bnd_i, CAP * NE - 1)
        em.inc("plc", pl.memset(b.idx_fill[:], 65535), 1)
        em.inc("plc", pl.memset(b.out_g[:], 0.0), 1)
        em.mark("idxfill", "plc")
        for k in ("endA_pe", "endA_act", "endA_dve", "endA_io", "endA_wl", "endA_plc"):
            em.wait("sy", k)
        em.wait("sy", "idxfill")
        for c in range(C):
            em.inc("io", sy.dma_start(out=b.ln2d_r[:, c, :], in_=b.out_g[:]), 16)
        em.inc("io", sy.dma_start(out=t.combd.rearrange("(c p) e -> p c e", p=128),
                                  in_=b.out_g[:, 0:C * NE].rearrange("p (c e) -> p c e", e=NE)), 16)
        em.wait_ct("sy", "io")
        em.mark("zinit", "io")

    for ti in range(NTILE):
        for ts in range(TS):
            em.wait("sy", f"x2w{ti * G + ts}")
            if ti > 0:
                em.wait("sy", f"outdma{ti - 1}_{ts}")
            em.wait_ct("sy", "io")
            em.inc("io", sy.dma_start(out=b.x2_sb[:, ts, :], in_=x2_tile[ti][:, ts, :]), 16)
            em.mark(f"x2l{ti}_{ts}", "io")

        for ts in range(TS):
            st = b.stat2
            s0 = (ts % 2) * 10
            sl = ts % 2
            em.wait("act", f"x2l{ti}_{ts}")
            if f"xn2free{sl}" in em.ms:
                em.wait("act", f"xn2free{sl}")
            if SPARSE:
                kk = f"ln2d{ti}_{ts - 1}" if ts >= 1 else f"ln2d{ti - 1}_{TS - 1}"
                if kk in em.ms:
                    em.wait("act", kk)
            if sl == 0 and ti > 0 and B_GATE:
                em.wait("act", f"ltmtr{ti - 1}")
            em.inc("act", act.activation(b.xn2[:, sl, :], b.x2_sb[:, ts, :], AF.Square,
                                         accum_out=st[:, s0 + 1:s0 + 2]))
            em.mark(f"bsq{ti}_{ts}", "act")
            em.wait("dve", f"x2l{ti}_{ts}")
            em.inc("dve", dve.reduce_sum(st[:, s0 + 0:s0 + 1], b.x2_sb[:, ts, :], axis=AX.X))
            dve.drain()
            em.wait("dve", f"bsq{ti}_{ts}")
            dve.tensor_scalar_mul(st[:, s0 + 2:s0 + 3], st[:, s0 + 0:s0 + 1], 1.0 / E)
            dve.tensor_scalar_mul(st[:, s0 + 3:s0 + 4], st[:, s0 + 1:s0 + 2], 1.0 / E)
            dve.drain()
            dve.tensor_mul(st[:, s0 + 4:s0 + 5], st[:, s0 + 2:s0 + 3], st[:, s0 + 2:s0 + 3])
            dve.drain()
            em.inc("dve", dve.tensor_sub(st[:, s0 + 5:s0 + 6], st[:, s0 + 3:s0 + 4], st[:, s0 + 4:s0 + 5]))
            em.mark(f"bvar{ti}_{ts}", "dve")
            em.wait("act", f"bvar{ti}_{ts}")
            em.wait("act", "persistbf")
            em.wait("act", "consts")
            act.activation(st[:, s0 + 6:s0 + 7], st[:, s0 + 5:s0 + 6], AF.Ln, bias=t.eps_col[:, 0:1], scale=1.0)
            act.drain()
            em.inc("act", act.activation(st[:, s0 + 7:s0 + 8], st[:, s0 + 6:s0 + 7], AF.Exp, bias=0.0, scale=-0.5))
            em.mark(f"brstd{ti}_{ts}", "act")
            em.wait("dve", f"brstd{ti}_{ts}")
            dve.drain()
            em.inc("dve", dve.scalar_tensor_tensor(st[:, s0 + 8:s0 + 9], st[:, s0 + 2:s0 + 3], -1.0,
                                                   st[:, s0 + 7:s0 + 8],
                                                   op0=ALU.mult, op1=ALU.mult))
            em.mark(f"bnmr{ti}_{ts}", "dve")
            em.wait("act", f"bnmr{ti}_{ts}")
            em.inc("act", act.activation(b.xn2[:, sl, :], b.x2_sb[:, ts, :], AF.Identity,
                                         bias=st[:, s0 + 8:s0 + 9], scale=st[:, s0 + 7:s0 + 8]))
            em.mark(f"bxn{ti}_{ts}", "act")
            if SPARSE:
                em.wait("sy", f"bxn{ti}_{ts}")
                em.wait_ct("sy", "io")
                em.inc("io", sy.dma_start(out=b.ln2d_r[:, ti * TS + ts, :], in_=b.xn2[:, sl, :]), 16)
                em.mark(f"ln2d{ti}_{ts}", "io")
            for fc in range(NF):
                bk = ph[fc % 2]
                bk.producer_wait()
                em.wait("pe", f"bxn{ti}_{ts}")
                em.wait("pe", "identf")
                if ti > 0 and B_EXPERTS:
                    em.wait("pe", f"lnT2used{ti - 1}")
                ins = pe.transpose(bk.t[:, 0:128], b.xn2[:, sl, fc * 128:(fc + 1) * 128], b.identf[:])
                bk.fill(ins)
                bk.consumer_wait("act")
                bk.free(act.copy(b.ln2T[:, fc, ts * 128:(ts + 1) * 128], bk.t[:, 0:128]), "act")
            em.mark(f"xn2free{sl}", "pe")
        em.mark(f"lnT2{ti}", "act")

        if B_GATE:
            bk = ph[0]
            bk.producer_wait()
            em.wait("pe", f"lnT2{ti}")
            em.wait("pe", "persistbf")
            em.wait("pe", "consts")
            for hf in range(2):
                for fc in range(NF):
                    ins = pe.matmul(bk.t[0:NE, hf * HF:(hf + 1) * HF], b.gw_sb2[:, fc, :],
                                    b.ln2T[:, fc, hf * HF:(hf + 1) * HF], start=(fc == 0),
                                    stop=(fc == NF - 1))
            bk.fill(ins)
            bk.consumer_wait("act")
            em.wait("act", "xn2free0")
            if SPARSE:
                em.wait("act", f"ln2d{ti}_{TS - 1}")
            em.wait("act", "persistf32")
            bk.free(act.activation(b.xn2[0:NE, 0, :], bk.t[0:NE, :], AF.Identity,
                                   bias=t.gb_col[0:NE, 0:1], scale=1.0), "act")
            em.mark(f"logitsT{ti}", "act")

            for ts in range(TS):
                bk = ph[1 - ts % 2]
                bk.producer_wait()
                em.wait("pe", f"logitsT{ti}")
                ins = pe.transpose(bk.t[0:128, 0:NE], b.xn2[0:NE, 0, ts * 128:(ts + 1) * 128],
                                   b.identf[0:NE, 0:NE])
                bk.fill(ins)
                bk.consumer_wait("act")
                if ti > 0:
                    em.wait("act", f"combine{ti - 1}_{TS - 1}")
                bk.free(act.copy(b.logits_tm[:, ts, :], bk.t[0:128, 0:NE]), "act")
                em.mark(f"ltm{ti}_{ts}", "act")
            em.mark(f"ltmtr{ti}", "pe")

            for ts in range(TS):
                gg = b.gg
                g0 = (ts % 2) * 10
                lg = b.logits_tm[:, ts, :]
                em.wait("dve", f"ltm{ti}_{ts}")
                dve.reduce_max(gg[:, g0 + 0:g0 + 1], lg, axis=AX.X)
                dve.drain()
                dve.tensor_single_scalar(b.maskA[:], lg, gg[:, g0 + 0:g0 + 1], op=ALU.is_ge)
                dve.drain()
                dve.scalar_tensor_tensor(b.xm[:], b.maskA[:], -BIG, lg, op0=ALU.mult, op1=ALU.add)
                dve.drain()
                dve.reduce_max(gg[:, g0 + 1:g0 + 2], b.xm[:], axis=AX.X)
                ins = dve.tensor_scalar_mul(gg[:, g0 + 2:g0 + 3], gg[:, g0 + 0:g0 + 1], -1.0)
                em.inc("dve", ins)
                em.mark(f"gd1_{ti}_{ts}", "dve")
                em.wait("act", f"gd1_{ti}_{ts}")
                em.inc("act", act.activation(gg[:, g0 + 3:g0 + 4], gg[:, g0 + 1:g0 + 2], AF.Exp,
                                             bias=gg[:, g0 + 2:g0 + 3], scale=1.0))
                em.mark(f"gexp{ti}_{ts}", "act")
                em.wait("dve", f"gexp{ti}_{ts}")
                dve.tensor_scalar_add(gg[:, g0 + 4:g0 + 5], gg[:, g0 + 3:g0 + 4], 1.0)
                dve.drain()
                dve.reciprocal(gg[:, g0 + 5:g0 + 6], gg[:, g0 + 4:g0 + 5])
                dve.drain()
                dve.tensor_mul(gg[:, g0 + 6:g0 + 7], gg[:, g0 + 3:g0 + 4], gg[:, g0 + 5:g0 + 6])
                dve.tensor_single_scalar(b.maskAB[:], lg, gg[:, g0 + 1:g0 + 2], op=ALU.is_ge)
                dve.drain()
                dve.tensor_sub(gg[:, g0 + 7:g0 + 8], gg[:, g0 + 5:g0 + 6], gg[:, g0 + 6:g0 + 7])
                dve.drain()
                dve.tensor_scalar_mul(b.xm[:], b.maskA[:], gg[:, g0 + 7:g0 + 8])
                dve.drain()
                em.inc("dve", dve.scalar_tensor_tensor(b.combine[:, ts, :], b.maskAB[:], gg[:, g0 + 6:g0 + 7],
                                                       b.xm[:], op0=ALU.mult, op1=ALU.add))
                em.mark(f"combine{ti}_{ts}", "dve")
                if SPARSE:
                    dve.drain()
                    dve.tensor_single_scalar(b.mask_all[:, ts, :], b.combine[:, ts, :], 0.0,
                                             op=ALU.is_gt)
                    ins = dve.tensor_copy(b.maskA_all[:, ts, :], b.maskA[:])
                    em.inc("dve", ins)
                    em.mark(f"maskp{ti}_{ts}", "dve")
                    em.wait("pl", f"combine{ti}_{ts}")
                    em.wait("pl", "zinit")
                    em.wait_ct("pl", "wl")
                    em.inc("wl", pl.dma_start(out=b.combd_r[:, ti * TS + ts, :], in_=b.combine[:, ts, :]), 16)
                bk = ph[1 - ts % 2]
                bk.producer_wait()
                em.wait("pe", f"combine{ti}_{ts}")
                ins = pe.transpose(bk.t[0:NE, 0:128], b.combine[:, ts, :], b.identf[:])
                bk.fill(ins)
                bk.consumer_wait("act")
                if ti > 0 and B_EXPERTS and B_W2:
                    em.wait("act", f"w2used{(ti - 1) * 9 + (0 if SPARSE else 8)}")
                bk.free(act.copy(b.combineT[0:NE, ts, :], bk.t[0:NE, 0:128]), "act")
                em.mark(f"combT{ti}_{ts}", "act")

            if SPARSE:
                em.wait("act", f"maskp{ti}_{TS - 1}")
                em.inc("act", act.copy(b.mask_bf[:], b.mask_all[:].rearrange("p ts e -> p (ts e)")))
                em.mark(f"maskbf{ti}", "act")
                bk = ph[0]
                bk.producer_wait()
                em.wait("pe", f"maskbf{ti}")
                em.wait("pe", "persistf32")
                ins = pe.matmul(bk.t[:, 0:TS * NE], t.ltri_sb[:], b.mask_bf[:], start=True, stop=True)
                bk.fill(ins)
                bk.consumer_wait("act")
                bk.free(act.copy(b.rankw[:], bk.t[:, 0:TS * NE]), "act")
                em.mark(f"rankw{ti}", "act")
                bk = ph[1]
                bk.producer_wait()
                ins = pe.matmul(bk.t[0:1, 0:TS * NE], t.ones_col[:, 0:1], b.mask_bf[:],
                                start=True, stop=True)
                bk.fill(ins)
                bk.consumer_wait("act")
                bk.free(act.copy(b.cnt[0:1, :], bk.t[0:1, 0:TS * NE]), "act")
                em.mark(f"cnt{ti}", "act")
                em.wait("dve", f"cnt{ti}")
                dve.memset(b.base[0:1, 0:NE], 0.0)
                for ts in range(1, TS):
                    dve.drain()
                    dve.tensor_add(b.base[0:1, ts * NE:(ts + 1) * NE],
                                   b.base[0:1, (ts - 1) * NE:ts * NE],
                                   b.cnt[0:1, (ts - 1) * NE:ts * NE])
                ins = dve.drain()
                em.inc("dve", dve.engine_nop() if em.active == "dve" and not getattr(em, "sched", False) else _DUM)
                em.mark(f"base{ti}", "dve")
                bk = po[0]
                bk.producer_wait()
                em.wait("pe", f"base{ti}")
                em.wait("pe", "consts")
                ins = pe.matmul(bk.t[:, 0:TS * NE], t.ones_f[0:1, :], b.base[0:1, :],
                                start=True, stop=True)
                bk.fill(ins)
                bk.consumer_wait("dve")
                em.wait("dve", f"rankw{ti}")
                bk.free(dve.tensor_add(b.rankw[:], b.rankw[:], bk.t[:, 0:TS * NE]), "dve")
                em.mark(f"rankg{ti}", "dve")
                em.wait("pl", "idxfill")
                em.wait_ct("pl", "wl")
                em.inc("wl", pl.dma_start(out=t.idxd.rearrange("(s p e) one -> p s (e one)", p=128, e=NE),
                                          in_=b.idx_fill[:]), 16)
                for ts in range(TS):
                    sp = (ts % 2)
                    em.wait("dve", f"rankg{ti}")
                    if ts >= 2:
                        em.wait("dve", f"rsc{ti}_{ts - 2}")
                    mA = b.maskA_all[:, ts, :]
                    rk = b.rankw[:, ts * NE:(ts + 1) * NE]
                    dve.tensor_sub(b.maskAB[:], b.mask_all[:, ts, :], mA)
                    dve.tensor_mul(b.xm[:], mA, t.evec_sb[:])
                    dve.drain()
                    dve.reduce_sum(b.maskA[:, 0:1], b.xm[:], axis=AX.X)
                    dve.drain()
                    dve.tensor_mul(b.xm[:], b.maskAB[:], t.evec_sb[:])
                    dve.drain()
                    dve.reduce_sum(b.maskA[:, 1:2], b.xm[:], axis=AX.X)
                    dve.drain()
                    dve.tensor_mul(b.xm[:], mA, rk)
                    dve.drain()
                    dve.reduce_sum(b.maskA[:, 2:3], b.xm[:], axis=AX.X)
                    dve.drain()
                    dve.tensor_mul(b.xm[:], b.maskAB[:], rk)
                    dve.drain()
                    dve.reduce_sum(b.maskA[:, 3:4], b.xm[:], axis=AX.X)
                    dve.reduce_sum(b.maskA[:, 4:5], b.maskAB[:], axis=AX.X)
                    dve.drain()
                    dve.scalar_tensor_tensor(b.maskA[:, 5:6], b.maskA[:, 2:3], float(NE), b.maskA[:, 0:1],
                                             op0=ALU.mult, op1=ALU.add)
                    dve.scalar_tensor_tensor(b.maskA[:, 6:7], b.maskA[:, 3:4], float(NE), b.maskA[:, 1:2],
                                             op0=ALU.mult, op1=ALU.add)
                    dve.drain()
                    dve.scalar_tensor_tensor(b.maskA[:, 7:8], b.maskA[:, 4:5], -8192.0, b.maskA[:, 6:7],
                                             op0=ALU.mult, op1=ALU.add)
                    dve.drain()
                    dve.tensor_scalar_add(b.maskA[:, 7:8], b.maskA[:, 7:8], 8192.0)
                    dve.drain()
                    dve.tensor_copy(b.roff[:, sp, 0:1], b.maskA[:, 5:6])
                    ins = dve.tensor_copy(b.roff[:, sp, 1:2], b.maskA[:, 7:8])
                    em.inc("dve", ins)
                    em.mark(f"roff{ti}_{ts}", "dve")
                    em.wait("pl", f"roff{ti}_{ts}")
                    em.wait_ct("pl", "wl")
                    em.inc("wl", pl.indirect_dma_start(
                        out=t.idxd, out_offset=bass.IndirectOffsetOnAxis(b.roff[:, sp, 0:1], axis=0),
                        in_=t.tokid_sb[:, ti * TS + ts:ti * TS + ts + 1], in_offset=None,
                        bounds_check=b.bnd_i, oob_is_err=False), 16)
                    em.wait_ct("pl", "wl")
                    em.inc("wl", pl.indirect_dma_start(
                        out=t.idxd, out_offset=bass.IndirectOffsetOnAxis(b.roff[:, sp, 1:2], axis=0),
                        in_=t.tokid_sb[:, ti * TS + ts:ti * TS + ts + 1], in_offset=None,
                        bounds_check=b.bnd_i, oob_is_err=False), 16)
                    em.mark(f"rsc{ti}_{ts}", "wl")
                em.wait_ct("pl", "wl")
                em.inc("wl", pl.dma_start(
                    out=b.idxg[:],
                    in_=t.idxd.rearrange("(s p e) one -> p s (e one)", p=128, e=NE)), 16)
                em.mark(f"idxg{ti}", "wl")

        prev_lnTg = [None]
        prev_flushg = [None]
        last_osc = [None]

        def emit_stores():
            for ts in range(TS):
                if B_EXPERTS and B_W2:
                    se = 0 if SPARSE else 8
                    em.wait("sy", f"flush{ti * 9 + se}_{ts}")
                elif B_GATE:
                    em.wait("sy", f"combT{ti}_{ts}")
                else:
                    em.wait("sy", f"lnT2{ti}")
                em.wait_ct("sy", "io")
                em.inc("io", sy.dma_start(out=out_tile[ti][:, ts, :], in_=b.x2_sb[:, ts, :]), 16)
                em.mark(f"outdma{ti}_{ts}", "io")

        def emit_expert(pos, e):
            ei = ti * 9 + pos
            dense = (not SPARSE) or e == 8
            nsub = TS if dense else NSUB
            ncol = TT if dense else CAP
            w1src = t.w1[e] if e < NE else t.ws1
            w2src = t.w2[e] if e < NE else t.ws2

            def w1_dma(sbi):
                slot = sbi % 3
                if w1_slot_used[slot] is not None:
                    em.wait("pl", w1_slot_used[slot])
                em.inc(f"w1s{slot}", pl.dma_start(
                    out=b.w1b[:, slot, :, :],
                    in_=w1src[:, sbi * SLAB * 128:(sbi + 1) * SLAB * 128].rearrange(
                        "(fc p) h -> p fc h", p=128)), 16)
                em.mark(f"w1ld{ei}_{sbi}", f"w1s{slot}")
                w1_slot_used[slot] = f"w1used{ei}_{sbi}"

            def w2_dma(hb):
                if hb == 0 and prev_w2used[0] is not None:
                    em.wait("pl", prev_w2used[0])
                em.wait_ct("pl", "wl")
                em.inc("wl", pl.dma_start(
                    out=b.w2b[:, hb * 4:(hb + 1) * 4, :],
                    in_=w2src[hb * 512:(hb + 1) * 512, :].rearrange("(hc p) f -> p hc f", p=128)), 16)
                em.mark(f"w2ld{ei}_{hb}", "wl")

            if not dense:
                em.wait("pl", f"idxg{ti}")
                if prev_lnTg[0] is not None:
                    em.wait("pl", prev_lnTg[0])
                if prev_flushg[0] is not None:
                    em.wait("pl", prev_flushg[0])
                for sub in range(NSUB):
                    if f"lnTgp{ei}_{sub - 1}" in em.ms:
                        em.wait("pl", f"lnTgp{ei}_{sub - 1}")
                    em.wait_ct("pl", "wl")
                    em.inc("wl", pl.indirect_dma_start(
                        out=b.ln2g[:, 0, :], out_offset=None,
                        in_=t.ln2d, in_offset=bass.IndirectOffsetOnAxis(b.idxg[:, sub, e:e + 1], axis=0),
                        bounds_check=b.bnd_s, oob_is_err=False), 16)
                    em.wait_ct("pl", "wl")
                    em.inc("wl", pl.indirect_dma_start(
                        out=b.cbg[:, sub, :], out_offset=None,
                        in_=t.combd, in_offset=bass.IndirectOffsetOnAxis(b.idxg[:, sub, e:e + 1], axis=0),
                        bounds_check=b.bnd_s, oob_is_err=False), 16)
                    em.mark(f"gat{ei}_{sub}", "wl")
                    for fc in range(NF):
                        bk = ph[fc % 2]
                        bk.producer_wait()
                        em.wait("pe", f"gat{ei}_{sub}")
                        ins = pe.transpose(bk.t[:, 0:128], b.ln2g[:, 0, fc * 128:(fc + 1) * 128],
                                           b.identf[:])
                        bk.fill(ins)
                        bk.consumer_wait("act")
                        bk.free(act.copy(b.ln2Tg[:, fc, sub * 128:(sub + 1) * 128], bk.t[:, 0:128]), "act")
                    em.mark(f"lnTgp{ei}_{sub}", "pe")
                em.mark(f"gat{ei}", "wl")
                em.mark(f"lnTg{ei}", "act")
                prev_lnTg[0] = f"lnTgp{ei}_{NSUB - 1}"

            w1_dma(0)
            w1_dma(1)
            w1_dma(2)

            for hc in range(NHC):
                sbi = hc // SLAB
                bk = ph[hc % 2]
                em.wait("pe", f"w1ld{ei}_{sbi}")
                bk.producer_wait()
                if dense:
                    for hf in range(2):
                        for fc in range(NF):
                            ins = pe.matmul(bk.t[:, hf * HF:(hf + 1) * HF],
                                            b.w1b[:, sbi % 3, fc, (hc % SLAB) * 128:(hc % SLAB + 1) * 128],
                                            b.ln2T[:, fc, hf * HF:(hf + 1) * HF],
                                            start=(fc == 0), stop=(fc == NF - 1))
                else:
                    em.wait("pe", f"lnTg{ei}")
                    for fc in range(NF):
                        ins = pe.matmul(bk.t[:, 0:CAP],
                                        b.w1b[:, sbi % 3, fc, (hc % SLAB) * 128:(hc % SLAB + 1) * 128],
                                        b.ln2Tg[:, fc, 0:CAP],
                                        start=(fc == 0), stop=(fc == NF - 1))
                bk.fill(ins)
                if hc == NHC - 1 and e == 8:
                    em.mark(f"lnT2used{ti}", "pe")
                bk.consumer_wait("act")
                em.wait("act", "persistf32")
                bias = t.b1Ts[:, e, hc:hc + 1] if e < NE else t.bs1Ts[:, hc:hc + 1]
                bk.free(act.activation(b.h_sb[:, hc, 0:ncol], bk.t[:, 0:ncol],
                                       AF.Gelu, bias=bias, scale=1.0), "act")
                em.mark(f"h{ei}_{hc}", "act")
                if hc % SLAB == SLAB - 1:
                    em.mark(f"w1used{ei}_{sbi}", "pe")
                    if sbi + 3 < NSLAB:
                        w1_dma(sbi + 3)
                    if sbi % 2 == 0 and B_W2:
                        w2_dma(sbi // 2)
            prev_w2used[0] = f"w2used{ei}" if B_W2 else prev_w2used[0]

            for ts in range(nsub if B_W2 else 0):
                bk = po[ts % 2]
                bk.producer_wait()
                for hf in range(2):
                    for hc in range(NHC):
                        em.wait("pe", f"h{ei}_{hc}")
                        em.wait("pe", f"w2ld{ei}_{hc // 4}")
                        ins = pe.matmul(bk.t[:, hf * HF:(hf + 1) * HF],
                                        b.h_sb[:, hc, ts * 128:(ts + 1) * 128],
                                        b.w2b[:, hc, hf * HF:(hf + 1) * HF],
                                        start=(hc == 0), stop=(hc == NHC - 1 and e != 8))
                    if e == 8:
                        em.wait("pe", f"combT{ti}_{ts}")
                        pe.matmul(bk.t[:, hf * HF:(hf + 1) * HF], b.combineT[0:NE, ts, :],
                                  t.b2_bf[0:NE, hf * HF:(hf + 1) * HF], start=False, stop=False)
                        ins = pe.matmul(bk.t[:, hf * HF:(hf + 1) * HF], t.ones_bf[0:1, 0:128],
                                        t.bs2_bf[0:1, hf * HF:(hf + 1) * HF], start=False, stop=True)
                bk.fill(ins)
                if dense:
                    bk.consumer_wait("dve")
                    if SPARSE:
                        pass
                    else:
                        em.wait("dve", f"combine{ti}_{ts}")
                    if b.lastflush[ts] is not None:
                        em.wait("dve", b.lastflush[ts])
                    x2s = b.x2_sb[:, ts, :]
                    if e < NE:
                        em.wait("dve", f"combine{ti}_{ts}")
                        ins = dve.scalar_tensor_tensor(x2s, bk.t[:], b.combine[:, ts, e:e + 1],
                                                       x2s, op0=ALU.mult, op1=ALU.add)
                    else:
                        ins = dve.tensor_add(x2s, bk.t[:], x2s)
                    bk.free(ins, "dve")
                    em.mark(f"flush{ei}_{ts}", "dve")
                    b.lastflush[ts] = f"flush{ei}_{ts}"
                else:
                    bk.consumer_wait("dve")
                    em.wait("dve", f"gat{ei}")
                    if last_osc[0] is not None:
                        em.wait("dve", last_osc[0])
                    ins = dve.tensor_scalar_mul(b.out_g[:], bk.t[:], b.cbg[:, ts, e:e + 1])
                    bk.free(ins, "dve")
                    em.mark(f"flushg{ei}_{ts}", "dve")
                    em.wait("pl", f"flushg{ei}_{ts}")
                    em.wait("pl", f"outdma{ti}_{TS - 1}")
                    em.wait_ct("pl", "wl")
                    em.inc("wl", pl.indirect_dma_start(
                        out=t.out,
                        out_offset=bass.IndirectOffsetOnAxis(b.idxg[:, ts, e:e + 1], axis=0),
                        in_=b.out_g[:], in_offset=None,
                        bounds_check=b.bnd_s, oob_is_err=False,
                        compute_op=ALU.add), 16)
                    em.mark(f"osc{ei}_{ts}", "wl")
                    last_osc[0] = f"osc{ei}_{ts}"
                    prev_flushg[0] = f"flushg{ei}_{ts}"
            if B_W2:
                em.mark(f"w2used{ei}", "pe")

        order = ([8] + list(range(NE))) if SPARSE else list(range(9))
        for pos, e in enumerate(order if B_EXPERTS else []):
            emit_expert(pos, e)
            if B_EXPERTS and SPARSE and pos == 0:
                emit_stores()
        if not (B_EXPERTS and SPARSE):
            emit_stores()
        b.lastflush = [None for _ in range(TS)]


def build_nc():
    nc = bass.Bass("TRN2", target_bir_lowering=False)
    t = declare_dram(nc)
    with contextlib.ExitStack() as stack:
        en = stack.enter_context
        t.ones_bf = en(nc.sbuf_tensor("ones_bf", [1, 128], BF16))
        t.ones_col = en(nc.sbuf_tensor("ones_col", [128, 1], BF16))
        t.eps_col = en(nc.sbuf_tensor("eps_col", [128, 1], F32))
        t.ident_sb = en(nc.sbuf_tensor("ident_sb", [128, 128], BF16))
        t.ltri_sb = en(nc.sbuf_tensor("ltri_sb", [128, 128], BF16))
        t.evec_sb = en(nc.sbuf_tensor("evec_sb", [128, NE], F32))
        t.tokid_sb = en(nc.sbuf_tensor("tokid_sb", [128, C], mybir.dt.int32))
        t.ones_f = en(nc.sbuf_tensor("ones_f", [1, 128], F32))
        t.bqTs = en(nc.sbuf_tensor("bqTs", [128, NF], F32))
        t.bkTs = en(nc.sbuf_tensor("bkTs", [128, NF], F32))
        t.b1Ts = en(nc.sbuf_tensor("b1Ts", [128, NE, NHC], F32))
        t.bs1Ts = en(nc.sbuf_tensor("bs1Ts", [128, NHC], F32))
        t.bs2_bf = en(nc.sbuf_tensor("bs2_bf", [1, E], BF16))
        t.b2_bf = en(nc.sbuf_tensor("b2_bf", [NE, E], BF16))
        t.gb_col = en(nc.sbuf_tensor("gb_col", [NE, 1], F32))
        t.gw_sb = en(nc.sbuf_tensor("gw_sb", [128, NF, NE], BF16))

        sems = {k: en(nc.semaphore(f"s_{k}")) for k in SEMS}
        block = en(nc.Block())
        ems = {name: Em(name, sems) for name in ENG}
        emx = Em("__sched__", sems)
        emx.sched = True
        emx.prog = {e: [] for e in ENG}
        emx._last_op = None

        def run(name, fn, *args):
            em = ems[name]
            def body(engobj):
                em.engine = engobj
                fn(em, *args)
                em.engine = None
            getattr(block, {"pe": "tensor", "act": "scalar", "dve": "vector",
                            "sy": "sync", "pl": "gpsimd"}[name])(body)

        with contextlib.ExitStack() as sa:
            ea = sa.enter_context
            a = NS()
            a.x_ln = ea(nc.sbuf_tensor("x_ln", [128, 4, E], F32))
            a.x_res = ea(nc.sbuf_tensor("x_res", [128, 2, E], F32))
            a.xn_sb = ea(nc.sbuf_tensor("xn_sb", [128, 2, E], BF16))
            a.ln1T = ea(nc.sbuf_tensor("ln1T", [128, NF, G * TC], BF16))
            a.qT = ea(nc.sbuf_tensor("qT", [128, NF, G * TC], BF16))
            a.kT = ea(nc.sbuf_tensor("kT", [128, NF, G * TC], BF16))
            a.v_sb = ea(nc.sbuf_tensor("v_sb", [128, G, E], BF16))
            a.attnT = ea(nc.sbuf_tensor("attnT", [128, NF, TC], BF16))
            a.sq_scr = ea(nc.sbuf_tensor("sq_scr", [128, E], F32))
            a.stat = ea(nc.sbuf_tensor("statA", [128, 20], F32))
            a.km_scr = ea(nc.sbuf_tensor("km_scr", [128, NF], F32))
            a.kmeansT = ea(nc.sbuf_tensor("kmeansT", [128, NF, C], BF16))
            a.vmeans = ea(nc.sbuf_tensor("vmeans", [C, E], BF16))
            a.vm_stage = ea(nc.sbuf_tensor("vm_stage", [1, E], F32))
            a.negmax = ea(nc.sbuf_tensor("negmax", [128, 32], F32))
            a.sums = ea(nc.sbuf_tensor("sumsA", [128, 32], F32))
            a.recip = ea(nc.sbuf_tensor("recipA", [128, 32], F32))
            a.ascr = ea(nc.sbuf_tensor("ascr", [128, 16, TC], BF16))
            a.an_scr = ea(nc.sbuf_tensor("an_scr", [128, 16, TC], BF16))
            a.aiscr = ea(nc.sbuf_tensor("aiscr", [128, 16, C], BF16))
            a.ain_scr = ea(nc.sbuf_tensor("ain_scr", [128, 16, C], BF16))
            a.aT_sb = ea(nc.sbuf_tensor("aT_sb", [128, 16, TC], BF16))
            a.aiT_sb = ea(nc.sbuf_tensor("aiT_sb", [C, 16, TC], BF16))
            a.xo_sb = ea(nc.sbuf_tensor("xo_sb", [128, 2, E], F32))
            a.bv_bf = ea(nc.sbuf_tensor("bv_bf", [1, E], BF16))
            a.bo_bf = ea(nc.sbuf_tensor("bo_bf", [1, E], BF16))
            a.wq_sb = ea(nc.sbuf_tensor("wq_sb", [128, NF, E], BF16))
            a.wk_sb = ea(nc.sbuf_tensor("wk_sb", [128, NF, E], BF16))
            a.wv_sb = ea(nc.sbuf_tensor("wv_sb", [128, NF, E], BF16))
            a.wo_sb = ea(nc.sbuf_tensor("wo_sb", [128, NF, E], BF16))
            a.ps_mm0 = ea(nc.psum_tensor("ps_mm0", [128, HF], F32))
            a.ps_mm1 = ea(nc.psum_tensor("ps_mm1", [128, HF], F32))
            a.ps_sA = ea(nc.psum_tensor("ps_sA", [128, HF], F32))
            a.ps_sB = ea(nc.psum_tensor("ps_sB", [128, HF], F32))
            a.ps_trA = ea(nc.psum_tensor("ps_trA", [128, 128], BF16))
            a.ps_trB = ea(nc.psum_tensor("ps_trB", [128, 128], BF16))
            a.ps_at0 = ea(nc.psum_tensor("ps_at0", [128, 128], F32))
            a.ps_at1 = ea(nc.psum_tensor("ps_at1", [128, 128], F32))
            a.seen_exp = set()
            a.seen_an = set()
            a.seen_tra = set()
            a.seen_atm = set()

            def phA(em):
                a.seen_exp = set()
                a.seen_an = set()
                a.seen_tra = set()
                a.seen_atm = set()
                emit_init(em, t)
                emit_phase_a(em, t, a)
                em.mark("endA_pe", "pe")
                em.mark("endA_act", "act")
                em.mark("endA_dve", "dve")
                em.mark("endA_io", "io")
                em.mark("endA_wl", "wl")
                em.mark("endA_plc", "plc")

            phA(emx)
            for name in ENG:
                run(name, phA)

        with contextlib.ExitStack() as sb:
            eb = sb.enter_context
            b = NS()
            b.x2_sb = eb(nc.sbuf_tensor("x2_sb", [128, TS, E], F32))
            b.xn2 = eb(nc.sbuf_tensor("xn2", [128, 2, E], F32))
            b.ln2T = eb(nc.sbuf_tensor("ln2T", [128, NF, TT], BF16))
            b.h_sb = eb(nc.sbuf_tensor("h_sb", [128, NHC, TT], BF16))
            b.w1b = eb(nc.sbuf_tensor("w1b", [128, 3, NF, SLAB * 128], BF16))
            b.w2b = eb(nc.sbuf_tensor("w2b", [128, NHC, E], BF16))
            b.stat2 = eb(nc.sbuf_tensor("stat2", [128, 20], F32))
            b.logits_tm = eb(nc.sbuf_tensor("logits_tm", [128, TS, NE], F32))
            b.gg = eb(nc.sbuf_tensor("gg", [128, 20], F32))
            b.maskA = eb(nc.sbuf_tensor("maskA", [128, NE], F32))
            b.maskAB = eb(nc.sbuf_tensor("maskAB", [128, NE], F32))
            b.xm = eb(nc.sbuf_tensor("xm", [128, NE], F32))
            b.combine = eb(nc.sbuf_tensor("combine", [128, TS, NE], F32))
            b.combineT = eb(nc.sbuf_tensor("combineT", [NE, TS, TC], BF16))
            b.identf = eb(nc.sbuf_tensor("identf", [128, 128], F32))
            b.ph0 = eb(nc.psum_tensor("ph0", [128, TT], F32))
            b.ph1 = eb(nc.psum_tensor("ph1", [128, TT], F32))
            b.po0 = eb(nc.psum_tensor("po0", [128, TT], F32))
            b.po1 = eb(nc.psum_tensor("po1", [128, TT], F32))
            if SPARSE:
                b.ln2g = eb(nc.sbuf_tensor("ln2g", [128, 1, E], F32))
                b.ln2Tg = b.h_sb[:][:, 0:NF, 512:512 + CAP]
                b.cbg = eb(nc.sbuf_tensor("cbg", [128, NSUB, NE], F32))
                b.idxg = eb(nc.sbuf_tensor("idxg", [128, NSUB, NE], mybir.dt.int32))
                b.out_g = eb(nc.sbuf_tensor("out_g", [128, E], F32))
                b.mask_all = eb(nc.sbuf_tensor("mask_all", [128, TS, NE], F32))
                b.maskA_all = eb(nc.sbuf_tensor("maskA_all", [128, TS, NE], F32))
                b.mask_bf = eb(nc.sbuf_tensor("mask_bf", [128, TS * NE], BF16))
                b.rankw = eb(nc.sbuf_tensor("rankw", [128, TS * NE], F32))
                b.cnt = eb(nc.sbuf_tensor("cntB", [1, TS * NE], F32))
                b.base = eb(nc.sbuf_tensor("baseB", [1, TS * NE], F32))
                b.idx_fill = eb(nc.sbuf_tensor("idx_fill", [128, NSUB, NE], mybir.dt.int32))
                b.roff = eb(nc.sbuf_tensor("roff", [128, 2, 2], mybir.dt.int32))
            b.ln2d_r = t.ln2d.rearrange("(c p) f -> p c f", p=128)
            b.combd_r = t.combd.rearrange("(c p) e -> p c e", p=128)
            b.gw_sb2 = t.gw_sb
            b.lastflush = [None for _ in range(TS)]

            def phB(em):
                b.lastflush = [None for _ in range(TS)]
                dveB = Eng(em, "dve")
                em.wait("dve", "persistf32")
                for k in ("endA_pe", "endA_act", "endA_dve", "endA_io", "endA_wl", "endA_plc"):
                    em.wait("dve", k)
                em.inc("dve", dveB.tensor_copy(b.identf[:], t.ident_sb[:]))
                em.mark("identf", "dve")
                emit_phase_b(em, t, b)
                em.mark("final_io", "io")
                em.wait("sy", "final_io")

            phB(emx)
            _simulate(emx.prog)
            for name in ENG:
                run(name, phB)

    return nc


def kernel(**inputs):
    xin = np.ascontiguousarray(np.asarray(inputs["L0_attn_hidden_states"], dtype=np.float32))
    f32 = lambda v: np.ascontiguousarray(np.asarray(v, dtype=np.float32))
    wq = f32(inputs["L0_attn_wq"]); wk = f32(inputs["L0_attn_wk"])
    wv = f32(inputs["L0_attn_wv"]); wo = f32(inputs["L0_attn_wo"])
    base = {
        "wqT": np.ascontiguousarray(wq.T), "wkT": np.ascontiguousarray(wk.T),
        "wvT": np.ascontiguousarray(wv.T), "woT": np.ascontiguousarray(wo.T),
        "bqT": np.ascontiguousarray(f32(inputs["L0_attn_bq"]).reshape(NF, 128).T),
        "bkT": np.ascontiguousarray(f32(inputs["L0_attn_bk"]).reshape(NF, 128).T),
        "bv": f32(inputs["L0_attn_bv"]).reshape(1, E),
        "bo": f32(inputs["L0_attn_bo"]).reshape(1, E),
        "gw": f32(inputs["L0_ffn_gate_w"]),
        "gb": f32(inputs["L0_ffn_gate_b"]).reshape(NE, 1),
        "w1": f32(inputs["L0_ffn_w1"]),
        "b1T": np.ascontiguousarray(f32(inputs["L0_ffn_b1"]).reshape(NE, NHC, 128).transpose(0, 2, 1)),
        "w2": f32(inputs["L0_ffn_w2"]),
        "b2": f32(inputs["L0_ffn_b2"]),
        "ws1": f32(inputs["L0_ffn_ws1"]),
        "bs1T": np.ascontiguousarray(f32(inputs["L0_ffn_bs1"]).reshape(NHC, 128).T),
        "ws2": f32(inputs["L0_ffn_ws2"]),
        "bs2": f32(inputs["L0_ffn_bs2"]).reshape(1, E),
        "ident": np.eye(128, dtype=ml_dtypes.bfloat16),
        "ltri": np.triu(np.ones((128, 128), dtype=ml_dtypes.bfloat16), 1),
        "evec": np.tile(np.arange(NE, dtype=np.float32), (128, 1)),
        "tokid": (np.arange(C, dtype=np.int32)[None, :] * 128
                  + np.arange(128, dtype=np.int32)[:, None]).astype(np.int32),
    }
    in_maps = []
    for core in range(NCORE):
        m = dict(base)
        m["x"] = np.ascontiguousarray(xin[core])
        in_maps.append(m)
    nc = build_nc()
    res = run_bass_kernel_spmd(nc, in_maps, core_ids=list(range(NCORE)))
    out = np.stack([np.asarray(res.results[i]["out"]) for i in range(NCORE)], axis=0)
    return out.astype(np.float32)


# version 14
# speedup vs baseline: 1.0418x; 1.0418x over previous
import contextlib
import numpy as np
import ml_dtypes
import concourse.bass as bass
import concourse.mybir as mybir
from concourse.bass_utils import run_bass_kernel_spmd

F32 = mybir.dt.float32
BF16 = mybir.dt.bfloat16
AF = mybir.ActivationFunctionType
ALU = mybir.AluOpType
AX = mybir.AxisListType

NCORE = 8
S, E, H, D = 4096, 1024, 16, 64
TC, C = 128, 32
G, NG = 8, 4
NF = 8
HID, NE, NHC = 4096, 8, 32
TT, NTILE, TS = 1024, 4, 8
HF = 512
EPS = 1e-5
SCL = 0.125
SLAB = 2
NSLAB = NHC // SLAB
BIG = 1.0e30
B_HEADS = True
B_TAILS = True
B_EXPERTS = True
B_W2 = True
B_GATE = True
SPARSE = False
CAP = 384
NSUB = CAP // 128

ENG = ("pe", "act", "dve", "sy", "pl")
SEMS = ("pe", "act", "dve", "io", "wl", "plc", "w1s0", "w1s1", "w1s2")


class _Dummy:
    def then_inc(self, *a, **k):
        return self


_DUM = _Dummy()


class Eng:

    def __init__(self, em, name):
        object.__setattr__(self, "_em", em)
        object.__setattr__(self, "_name", name)

    def __getattr__(self, op):
        em = self._em
        if getattr(em, "sched", False):
            name = self._name
            def rec(*a, **k):
                item = ["op", op, []]
                em.prog[name].append(item)
                em._last_op = item
                return _DUM
            return rec
        if em.active == self._name and em.engine is not None:
            return getattr(em.engine, op)
        return lambda *a, **k: _DUM


class Em:

    def __init__(self, active, sems):
        self.active = active
        self.engine = None
        self.sems = sems
        self.ct = {k: 0 for k in SEMS}
        self.ms = {}
        self._lastwait = {}

    def inc(self, sem, inst, n=1):
        self.ct[sem] += n
        if getattr(self, "sched", False):
            self._last_op[2].append((sem, n))
        elif not isinstance(inst, _Dummy):
            inst.then_inc(self.sems[sem], n)
        return self.ct[sem]

    def mark(self, key, sem):
        self.ms[key] = (sem, self.ct[sem])

    def wait(self, eng_name, key):
        sem, val = self.ms[key]
        self._wait_val(eng_name, sem, val)

    def wait_ct(self, eng_name, sem):
        self._wait_val(eng_name, sem, self.ct[sem])

    def _wait_val(self, eng_name, sem, val):
        if val <= 0:
            return
        if getattr(self, "sched", False):
            last = self._lastwait.get((eng_name, sem), -1)
            if val <= last:
                return
            self._lastwait[(eng_name, sem)] = val
            self.prog[eng_name].append(["wait", sem, val, "wait_ct"])
            return
        if self.active == eng_name and self.engine is not None:
            last = self._lastwait.get(sem, -1)
            if val <= last:
                return
            self._lastwait[sem] = val
            self.engine.wait_ge(self.sems[sem], val)


def _simulate(prog):
    counts = {k: 0 for k in SEMS}
    idx = {e: 0 for e in ENG}
    progress = True
    while progress:
        progress = False
        for e in ENG:
            stream = prog[e]
            while idx[e] < len(stream):
                it = stream[idx[e]]
                if it[0] == "wait":
                    if counts[it[1]] >= it[2]:
                        idx[e] += 1
                        progress = True
                    else:
                        break
                else:
                    for sem, n in it[2]:
                        counts[sem] += n
                    idx[e] += 1
                    progress = True
    stuck = {e: idx[e] for e in ENG if idx[e] < len(prog[e])}
    if stuck:
        msg = []
        for e, i in stuck.items():
            msg.append(f"{e}@{i}/{len(prog[e])}: {prog[e][i]} counts_now={counts[prog[e][i][1]] if prog[e][i][0]=='wait' else ''}")
        raise RuntimeError("DEADLOCK in schedule: " + " | ".join(msg))


class Bank:

    def __init__(self, em, name, t):
        self.em, self.name, self.t = em, name, t
        self.fills = 0
        self.frees = 0

    def producer_wait(self):
        assert self.frees == self.fills, (self.name, self.frees, self.fills)
        if self.fills > 0:
            self.em.wait("pe", f"{self.name}.fr{self.fills}")

    def fill(self, inst):
        self.em.inc("pe", inst)
        self.fills += 1
        self.em.mark(f"{self.name}.fi{self.fills}", "pe")

    def consumer_wait(self, eng):
        self.em.wait(eng, f"{self.name}.fi{self.fills}")

    def free(self, inst, sem):
        self.em.inc(sem, inst)
        self.frees += 1
        assert self.frees == self.fills, (self.name, self.frees, self.fills)
        self.em.mark(f"{self.name}.fr{self.frees}", sem)


class NS:
    pass


def declare_dram(nc):
    t = NS()
    def di(name, shape, dt=F32):
        return nc.dram_tensor(name, shape, dt, kind="ExternalInput").ap()
    t.x = di("x", [S, E])
    t.wqT = di("wqT", [E, E]); t.wkT = di("wkT", [E, E])
    t.wvT = di("wvT", [E, E]); t.woT = di("woT", [E, E])
    t.bqT = di("bqT", [128, NF]); t.bkT = di("bkT", [128, NF])
    t.bv = di("bv", [1, E]); t.bo = di("bo", [1, E])
    t.gw = di("gw", [E, NE]); t.gb = di("gb", [NE, 1])
    t.w1 = di("w1", [NE, E, HID]); t.b1T = di("b1T", [NE, 128, NHC])
    t.w2 = di("w2", [NE, HID, E]); t.b2 = di("b2", [NE, E])
    t.ws1 = di("ws1", [E, HID]); t.bs1T = di("bs1T", [128, NHC])
    t.ws2 = di("ws2", [HID, E]); t.bs2 = di("bs2", [1, E])
    t.ident = di("ident", [128, 128], BF16)
    t.ltri = di("ltri", [128, 128], BF16)
    t.evec = di("evec", [128, NE])
    t.tokid = di("tokid", [128, C], mybir.dt.int32)
    t.ln2d = nc.dram_tensor("ln2d", [S, E], F32).ap()
    t.combd = nc.dram_tensor("combd", [S, NE], F32).ap()
    t.idxd = nc.dram_tensor("idxd", [CAP * NE, 1], mybir.dt.int32).ap()
    t.out = nc.dram_tensor("out", [S, E], F32, kind="ExternalOutput").ap()
    t.x2d = nc.dram_tensor("x2d", [S, E], F32, kind="ExternalOutput").ap()
    return t


def emit_init(em, t):
    pe, act, dve, sy, pl = (Eng(em, n) for n in ENG)
    em.inc("plc", pl.memset(t.ones_bf[:], 1.0), 1)
    em.inc("plc", pl.memset(t.ones_col[:], 1.0), 1)
    em.inc("plc", pl.memset(t.eps_col[:], EPS), 1)
    em.inc("plc", pl.memset(t.ones_f[:], 1.0), 1)
    em.mark("consts", "plc")
    for dst, src in ((t.ident_sb, t.ident), (t.ltri_sb, t.ltri), (t.evec_sb, t.evec),
                     (t.tokid_sb, t.tokid), (t.bqTs, t.bqT), (t.bkTs, t.bkT),
                     (t.bs1Ts, t.bs1T)):
        em.wait_ct("sy", "io")
        em.inc("io", sy.dma_start(out=dst[:], in_=src), 16)
    em.wait_ct("sy", "io")
    em.wait_ct("sy", "io")
    em.inc("io", sy.dma_start(out=t.gb_col[:], in_=t.gb), 16)
    em.wait_ct("sy", "io")
    em.inc("io", sy.dma_start(out=t.b1Ts[:], in_=t.b1T.rearrange("e p h -> p e h")), 16)
    em.mark("persistf32", "io")
    for dst, src in ((t.bs2_bf, t.bs2), (t.b2_bf, t.b2)):
        em.wait_ct("pl", "wl")
        em.inc("wl", pl.dma_start(out=dst[:], in_=src), 16)
    em.wait_ct("pl", "wl")
    em.inc("wl", pl.dma_start(out=t.gw_sb[:], in_=t.gw.rearrange("(fc p) e -> p fc e", p=128)), 16)
    em.mark("persistbf", "wl")
    em.wait("dve", "persistf32")
    em.inc("dve", dve.tensor_scalar_mul(t.bqTs[:], t.bqTs[:], SCL))
    em.mark("bqscaled", "dve")


def emit_phase_a(em, t, a):
    pe, act, dve, sy, pl = (Eng(em, n) for n in ENG)
    b4 = [a.ps_mm0, a.ps_mm1, a.ps_sA, a.ps_sB]
    scuse = {}
    trb = [Bank(em, "Atr0", a.ps_trA), Bank(em, "Atr1", a.ps_trB)]
    atb = [Bank(em, "Aat0", a.ps_at0), Bank(em, "Aat1", a.ps_at1)]
    mmi = [0]

    def bank_acquire(j):
        if j in scuse:
            em.wait("pe", scuse[j])

    def bank_release(j, key):
        scuse[j] = key

    em.wait_ct("pl", "wl")
    em.inc("wl", pl.dma_start(out=a.bv_bf[:], in_=t.bv), 16)
    em.wait_ct("pl", "wl")
    em.inc("wl", pl.dma_start(out=a.bo_bf[:], in_=t.bo), 16)
    for dst, src in ((a.wq_sb, t.wqT), (a.wk_sb, t.wkT), (a.wv_sb, t.wvT),
                     (a.wo_sb, t.woT)):
        em.wait_ct("pl", "wl")
        em.inc("wl", pl.dma_start(out=dst[:], in_=src.rearrange("(fi p) o -> p fi o", p=128)), 16)
    em.mark("attnw", "wl")

    x_ch = t.x.rearrange("(c p) f -> c p f", p=128)
    x2_ch = t.x2d.rearrange("(c p) f -> c p f", p=128)

    def ln_stats(c):
        cl4 = c % 4
        st = a.stat
        s0 = (c % 2) * 10
        if c >= 4:
            em.wait("sy", f"xn{c - 4}")
        em.wait_ct("sy", "io")
        em.inc("io", sy.dma_start(out=a.x_ln[:, cl4, :], in_=x_ch[c]), 16)
        em.mark(f"xl{c}", "io")
        xs = a.x_ln[:, cl4, :]
        em.wait("act", f"xl{c}")
        em.inc("act", act.activation(a.sq_scr[:], xs, AF.Square,
                                     accum_out=st[:, s0 + 1:s0 + 2]))
        em.mark(f"sq{c}", "act")
        em.wait("dve", f"xl{c}")
        em.inc("dve", dve.reduce_sum(st[:, s0 + 0:s0 + 1], xs, axis=AX.X))
        dve.drain()
        em.wait("dve", f"sq{c}")
        dve.tensor_scalar_mul(st[:, s0 + 2:s0 + 3], st[:, s0 + 0:s0 + 1], 1.0 / E)
        dve.tensor_scalar_mul(st[:, s0 + 3:s0 + 4], st[:, s0 + 1:s0 + 2], 1.0 / E)
        dve.drain()
        dve.tensor_mul(st[:, s0 + 4:s0 + 5], st[:, s0 + 2:s0 + 3], st[:, s0 + 2:s0 + 3])
        dve.drain()
        em.inc("dve", dve.tensor_sub(st[:, s0 + 5:s0 + 6], st[:, s0 + 3:s0 + 4], st[:, s0 + 4:s0 + 5]))
        em.mark(f"var{c}", "dve")
        em.wait("act", f"var{c}")
        em.wait("act", "persistbf")
        em.wait("act", "consts")
        act.activation(st[:, s0 + 6:s0 + 7], st[:, s0 + 5:s0 + 6], AF.Ln, bias=t.eps_col[:, 0:1], scale=1.0)
        act.drain()
        em.inc("act", act.activation(st[:, s0 + 7:s0 + 8], st[:, s0 + 6:s0 + 7], AF.Exp, bias=0.0, scale=-0.5))
        em.mark(f"rstd{c}", "act")
        em.wait("dve", f"rstd{c}")
        dve.drain()
        em.inc("dve", dve.scalar_tensor_tensor(st[:, s0 + 8:s0 + 9], st[:, s0 + 2:s0 + 3], -1.0,
                                               st[:, s0 + 7:s0 + 8],
                                               op0=ALU.mult, op1=ALU.mult))
        em.mark(f"nmr{c}", "dve")
        em.wait("act", f"nmr{c}")
        if c >= 2:
            em.wait("act", f"xnused{c - 2}")
        em.inc("act", act.activation(a.xn_sb[:, c % 2, :], xs, AF.Identity,
                                     bias=st[:, s0 + 8:s0 + 9], scale=st[:, s0 + 7:s0 + 8]))
        em.mark(f"xn{c}", "act")

    def ln_tr(c):
        for fc in range(NF):
            b = trb[fc % 2]
            b.producer_wait()
            em.wait("pe", f"xn{c}")
            em.wait("pe", "persistf32")
            ins = pe.transpose(b.t[:, 0:128], a.xn_sb[:, c % 2, fc * 128:(fc + 1) * 128], t.ident_sb[:])
            b.fill(ins)
            b.consumer_wait("act")
            b.free(act.copy(a.ln1T[:, fc, (c % G) * 128:(c % G + 1) * 128], b.t[:, 0:128]), "act")
        em.mark(f"xnused{c}", "pe")
        em.mark(f"lnTc{c}", "act")

    for cl in range(G):
        ln_stats(cl)
        ln_tr(cl)

    for g in range(NG):
        em.wait("pe", "attnw")
        for which, wsb, outq in (("q", a.wq_sb, a.qT), ("k", a.wk_sb, a.kT)):
            for fo in range(NF):
                for hf in range(2):
                    j = mmi[0] % 2
                    mmi[0] += 1
                    bank_acquire(j)
                    em.wait("pe", f"lnTc{g * G + 4 * hf + 3}")
                    for fi in range(NF):
                        ins = pe.matmul(b4[j][:], wsb[:, fi, fo * 128:(fo + 1) * 128],
                                        a.ln1T[:, fi, hf * HF:(hf + 1) * HF],
                                        start=(fi == 0), stop=(fi == NF - 1))
                    em.inc("pe", ins)
                    em.mark(f"pj{g}_{which}{fo}_{hf}", "pe")
                    em.wait("act", f"pj{g}_{which}{fo}_{hf}")
                    if which == "q":
                        em.wait("act", "bqscaled")
                        ins = act.activation(outq[:, fo, hf * HF:(hf + 1) * HF], b4[j][:],
                                             AF.Identity, bias=t.bqTs[:, fo:fo + 1], scale=SCL)
                    else:
                        em.wait("act", "persistf32")
                        ins = act.activation(outq[:, fo, hf * HF:(hf + 1) * HF], b4[j][:],
                                             AF.Identity, bias=t.bkTs[:, fo:fo + 1], scale=1.0)
                    em.inc("act", ins)
                    em.mark(f"pjev{g}_{which}{fo}_{hf}", "act")
                    bank_release(j, f"pjev{g}_{which}{fo}_{hf}")
            em.mark(f"{which}T{g}", "act")

        for cl in range(G):
            c = g * G + cl
            for hf in range(2):
                j = mmi[0] % 2
                mmi[0] += 1
                bank_acquire(j)
                em.wait("pe", f"lnTc{c}")
                em.wait("pe", "attnw")
                em.wait("pe", "consts")
                for fi in range(NF):
                    pe.matmul(b4[j][:], a.ln1T[:, fi, cl * 128:(cl + 1) * 128],
                              a.wv_sb[:, fi, hf * HF:(hf + 1) * HF],
                              start=(fi == 0), stop=False)
                ins = pe.matmul(b4[j][:], t.ones_bf[0:1, 0:128], a.bv_bf[0:1, hf * HF:(hf + 1) * HF],
                                start=False, stop=True)
                em.inc("pe", ins)
                em.mark(f"vf{c}_{hf}", "pe")
                em.wait("act", f"vf{c}_{hf}")
                em.inc("act", act.copy(a.v_sb[:, cl, hf * HF:(hf + 1) * HF], b4[j][:]))
                em.mark(f"v{c}_{hf}", "act")
                bank_release(j, f"v{c}_{hf}")
            em.wait("dve", f"kT{g}")
            if c > 0:
                em.wait("dve", f"km{c - 1}")
            for fo in range(NF):
                ins = dve.reduce_sum(a.km_scr[:, fo:fo + 1], a.kT[:, fo, cl * 128:(cl + 1) * 128], axis=AX.X)
            em.inc("dve", ins)
            em.mark(f"kmraw{c}", "dve")
            em.wait("act", f"kmraw{c}")
            em.inc("act", act.activation(a.kmeansT[:, :, c:c + 1],
                                         a.km_scr[:].rearrange("p (e o) -> p e o", o=1),
                                         AF.Copy, bias=0.0, scale=1.0 / TC))
            em.mark(f"km{c}", "act")
            for hf in range(2):
                j = 2 + hf
                bank_acquire(j)
                em.wait("pe", f"v{c}_{hf}")
                em.wait("pe", "consts")
                ins = pe.matmul(b4[j][0:1, 0:HF], t.ones_col[:, 0:1],
                                a.v_sb[:, cl, hf * HF:(hf + 1) * HF], start=True, stop=True)
                em.inc("pe", ins)
                em.mark(f"vmf{c}_{hf}", "pe")
                em.wait("act", f"vmf{c}_{hf}")
                if c > 0:
                    em.wait("act", f"vmdma{c - 1}")
                em.inc("act", act.activation(a.vm_stage[0:1, hf * HF:(hf + 1) * HF], b4[j][0:1, 0:HF],
                                             AF.Copy, bias=0.0, scale=1.0 / TC))
                em.mark(f"vmc{c}_{hf}", "act")
                bank_release(j, f"vmc{c}_{hf}")
            em.mark(f"vmstage{c}", "act")
            em.wait("pl", f"vmstage{c}")
            em.wait_ct("pl", "wl")
            em.inc("wl", pl.dma_start(out=a.vmeans[c:c + 1, :], in_=a.vm_stage[0:1, :]), 16)
            em.mark(f"vmdma{c}", "wl")

        def chunk_pro(cl):
            c = g * G + cl
            if c >= 2:
                em.wait("sy", f"resid{c - 2}")
            em.wait_ct("sy", "io")
            em.inc("io", sy.dma_start(out=a.x_res[:, c % 2, :], in_=x_ch[c]), 16)
            em.mark(f"xresl{c}", "io")

        def head_scores(cl, h):
            c = g * G + cl
            nj = c + 1
            ghi = c * H + h
            hs = h % 16
            ji = h % 2
            jj = 2 + h % 2
            cp, hp = divmod(ghi - 16, H)
            qs = a.qT[(h % 2) * 64:(h % 2) * 64 + 64, h // 2, cl * 128:(cl + 1) * 128]
            bank_acquire(ji)
            em.wait("pe", f"qT{g}")
            em.wait("pe", f"kT{g}")
            ins = pe.matmul(b4[ji][:, 0:128], qs,
                            a.kT[(h % 2) * 64:(h % 2) * 64 + 64, h // 2, cl * 128:(cl + 1) * 128],
                            start=True, stop=True)
            em.inc("pe", ins)
            em.mark(f"sci{c}_{h}", "pe")
            bank_acquire(jj)
            em.wait("pe", f"km{c}")
            ins = pe.matmul(b4[jj][:, 0:nj], qs,
                            a.kmeansT[(h % 2) * 64:(h % 2) * 64 + 64, h // 2, 0:nj],
                            start=True, stop=True)
            em.inc("pe", ins)
            em.mark(f"scj{c}_{h}", "pe")
            em.wait("dve", f"sci{c}_{h}")
            em.wait("dve", f"scj{c}_{h}")
            if ghi - 16 in a.seen_exp:
                em.wait("dve", f"expi{cp}_{hp}")
                em.wait("dve", f"expj{cp}_{hp}")
            dve.reduce_max(a.negmax[:, hs:hs + 1], b4[ji][:, 0:128], axis=AX.X, negate=True)
            ins = dve.reduce_max(a.negmax[:, 16 + hs:17 + hs], b4[jj][:, 0:nj],
                                 axis=AX.X, negate=True)
            em.inc("dve", ins)
            em.mark(f"nm{c}_{h}", "dve")
            em.wait("act", f"nm{c}_{h}")
            if ghi - 16 in a.seen_an:
                em.wait("act", f"an{cp}_{hp}")
            em.inc("act", act.activation(a.ascr[:, hs, :], b4[ji][:, 0:128], AF.Exp,
                                         bias=a.negmax[:, hs:hs + 1], scale=1.0,
                                         accum_out=a.sums[:, hs:hs + 1]))
            em.mark(f"expi{c}_{h}", "act")
            bank_release(ji, f"expi{c}_{h}")
            ins = act.activation(a.aiscr[:, hs, 0:nj], b4[jj][:, 0:nj], AF.Exp,
                                 bias=a.negmax[:, 16 + hs:17 + hs], scale=1.0,
                                 accum_out=a.sums[:, 16 + hs:17 + hs])
            em.inc("act", ins)
            em.mark(f"expj{c}_{h}", "act")
            bank_release(jj, f"expj{c}_{h}")
            a.seen_exp.add(ghi)

        def head_norm(cl, h):
            c = g * G + cl
            nj = c + 1
            ghi = c * H + h
            hs = h % 16
            cp, hp = divmod(ghi - 16, H)
            em.wait("dve", f"expi{c}_{h}")
            em.wait("dve", f"expj{c}_{h}")
            if ghi - 16 in a.seen_an:
                em.wait("dve", f"an{cp}_{hp}")
            dve.reciprocal(a.recip[:, hs:hs + 1], a.sums[:, hs:hs + 1])
            ins = dve.reciprocal(a.recip[:, 16 + hs:17 + hs], a.sums[:, 16 + hs:17 + hs])
            em.inc("dve", ins)
            em.mark(f"recip{c}_{h}", "dve")
            if ghi - 16 in a.seen_tra:
                em.wait("dve", f"tra{cp}_{hp}")
            dve.drain()
            dve.tensor_scalar_mul(a.an_scr[:, hs, :], a.ascr[:, hs, :], a.recip[:, hs:hs + 1])
            ins = dve.tensor_scalar_mul(a.ain_scr[:, hs, 0:nj], a.aiscr[:, hs, 0:nj],
                                        a.recip[:, 16 + hs:17 + hs])
            em.inc("dve", ins)
            em.mark(f"an{c}_{h}", "dve")
            a.seen_an.add(ghi)

        def head_tr(cl, h):
            c = g * G + cl
            nj = c + 1
            ghi = c * H + h
            hs = h % 16
            cp, hp = divmod(ghi - 16, H)
            bA = trb[0]
            bA.producer_wait()
            em.wait("pe", f"an{c}_{h}")
            ins = pe.transpose(bA.t[:, 0:128], a.an_scr[:, hs, :], t.ident_sb[:])
            bA.fill(ins)
            bB = trb[1]
            bB.producer_wait()
            ins = pe.transpose(bB.t[0:nj, 0:128], a.ain_scr[:, hs, 0:nj], t.ident_sb[:])
            bB.fill(ins)
            em.mark(f"tra{c}_{h}", "pe")
            a.seen_tra.add(ghi)
            bA.consumer_wait("act")
            if ghi - 16 in a.seen_atm:
                em.wait("act", f"atmm{cp}_{hp}")
            bA.free(act.copy(a.aT_sb[:, hs, :], bA.t[:, 0:128]), "act")
            em.mark(f"aT{c}_{h}", "act")
            bB.consumer_wait("dve")
            if ghi - 16 in a.seen_atm:
                em.wait("dve", f"atmm{cp}_{hp}")
            bB.free(dve.tensor_copy(a.aiT_sb[0:nj, hs, :], bB.t[0:nj, 0:128]), "dve")
            em.mark(f"aiT{c}_{h}", "dve")

        def head_mm(cl, h):
            c = g * G + cl
            nj = c + 1
            ghi = c * H + h
            hs = h % 16
            pair = h // 2
            bk = atb[(ghi // 2) % 2]
            if h % 2 == 0:
                bk.producer_wait()
            em.wait("pe", f"aT{c}_{h}")
            pe.matmul(bk.t[(h % 2) * 64:(h % 2) * 64 + 64, 0:128],
                      a.v_sb[:, cl, h * 64:(h + 1) * 64],
                      a.aT_sb[:, hs, :], start=True, stop=False)
            em.wait("pe", f"aiT{c}_{h}")
            em.wait("pe", f"vmdma{c}")
            ins = pe.matmul(bk.t[(h % 2) * 64:(h % 2) * 64 + 64, 0:128],
                            a.vmeans[0:nj, h * 64:(h + 1) * 64],
                            a.aiT_sb[0:nj, hs, :], start=False, stop=True)
            if h % 2 == 1:
                bk.fill(ins)
            else:
                em.inc("pe", ins)
            em.mark(f"atmm{c}_{h}", "pe")
            a.seen_atm.add(ghi)
            if h % 2 == 1:
                bk.consumer_wait("act")
                if c > 0:
                    em.wait("act", f"oproj{c - 1}")
                bk.free(act.copy(a.attnT[:, pair, :], bk.t[:, 0:128]), "act")
                em.mark(f"atevac{c}_{pair}", "act")

        def chunk_epi(cl):
            c = g * G + cl
            for hf in range(2):
                j = mmi[0] % 2
                mmi[0] += 1
                bank_acquire(j)
                if B_HEADS and B_TAILS:
                    em.wait("pe", f"atevac{c}_{H // 2 - 1}")
                for fc in range(NF):
                    pe.matmul(b4[j][:], a.attnT[:, fc, :],
                              a.wo_sb[:, fc, hf * HF:(hf + 1) * HF],
                              start=(fc == 0), stop=False)
                ins = pe.matmul(b4[j][:], t.ones_bf[0:1, 0:128], a.bo_bf[0:1, hf * HF:(hf + 1) * HF],
                                start=False, stop=True)
                em.inc("pe", ins)
                em.mark(f"opf{c}_{hf}", "pe")
                em.wait("dve", f"opf{c}_{hf}")
                em.wait("dve", f"xresl{c}")
                if c >= 2:
                    em.wait("dve", f"x2w{c - 2}")
                ins = dve.tensor_add(a.xo_sb[:, cl % 2, hf * HF:(hf + 1) * HF], b4[j][:],
                                     a.x_res[:, c % 2, hf * HF:(hf + 1) * HF])
                em.inc("dve", ins)
                em.mark(f"opr{c}_{hf}", "dve")
                bank_release(j, f"opr{c}_{hf}")
            em.mark(f"oproj{c}", "pe")
            em.mark(f"resid{c}", "dve")
            em.wait("sy", f"resid{c}")
            em.wait_ct("sy", "io")
            em.inc("io", sy.dma_start(out=x2_ch[c], in_=a.xo_sb[:, cl % 2, :]), 16)
            em.mark(f"x2w{c}", "io")
            if g + 1 < NG:
                if cl < G - 1:
                    ln_stats((g + 1) * G + cl + 1)
                ln_tr((g + 1) * G + cl)

        LAG = 12
        NQ = G * H
        if g + 1 < NG:
            ln_stats((g + 1) * G)
        if B_HEADS:
            for idx in range(NQ + LAG + 1):
                if idx < NQ:
                    cl, h = divmod(idx, H)
                    if h == 0:
                        chunk_pro(cl)
                    head_scores(cl, h)
                    head_norm(cl, h)
                if B_TAILS:
                    i1 = idx - LAG
                    if 0 <= i1 < NQ:
                        head_tr(*divmod(i1, H))
                    i2 = idx - LAG - 1
                    if 0 <= i2 < NQ:
                        cl3, h3 = divmod(i2, H)
                        head_mm(cl3, h3)
                        if h3 == H - 1:
                            chunk_epi(cl3)
            if not B_TAILS:
                for cl in range(G):
                    chunk_epi(cl)
        else:
            for cl in range(G):
                chunk_pro(cl)
                chunk_epi(cl)


def emit_phase_b(em, t, b):
    pe, act, dve, sy, pl = (Eng(em, n) for n in ENG)
    ph = [Bank(em, "Bh0", b.ph0), Bank(em, "Bh1", b.ph1)]
    po = [Bank(em, "Bo0", b.po0), Bank(em, "Bo1", b.po1)]

    x2_tile = t.x2d.rearrange("(ti ts p) f -> ti p ts f", ts=TS, p=128)
    out_tile = t.out.rearrange("(ti ts p) f -> ti p ts f", ts=TS, p=128)

    for k in ("endA_pe", "endA_act", "endA_dve", "endA_io", "endA_wl", "endA_plc"):
        for e in ENG:
            em.wait(e, k)

    prev_w2used = [None]
    w1_slot_used = [None, None, None]
    if SPARSE:
        b.bnd_s = pl.alloc_register("bnd_s")
        pl.reg_mov(b.bnd_s, S - 1)
        b.bnd_i = pl.alloc_register("bnd_i")
        pl.reg_mov(b.bnd_i, CAP * NE - 1)
        em.inc("plc", pl.memset(b.idx_fill[:], 65535), 1)
        em.inc("plc", pl.memset(b.out_g[:], 0.0), 1)
        em.mark("idxfill", "plc")
        for k in ("endA_pe", "endA_act", "endA_dve", "endA_io", "endA_wl", "endA_plc"):
            em.wait("sy", k)
        em.wait("sy", "idxfill")
        for c in range(C):
            em.inc("io", sy.dma_start(out=b.ln2d_r[:, c, :], in_=b.out_g[:]), 16)
        em.inc("io", sy.dma_start(out=t.combd.rearrange("(c p) e -> p c e", p=128),
                                  in_=b.out_g[:, 0:C * NE].rearrange("p (c e) -> p c e", e=NE)), 16)
        em.wait_ct("sy", "io")
        em.mark("zinit", "io")

    for ti in range(NTILE):
        for ts in range(TS):
            em.wait("sy", f"x2w{ti * G + ts}")
            if ti > 0:
                em.wait("sy", f"outdma{ti - 1}_{ts}")
            em.wait_ct("sy", "io")
            em.inc("io", sy.dma_start(out=b.x2_sb[:, ts, :], in_=x2_tile[ti][:, ts, :]), 16)
            em.mark(f"x2l{ti}_{ts}", "io")

        for ts in range(TS):
            st = b.stat2
            s0 = (ts % 2) * 10
            sl = ts % 2
            em.wait("act", f"x2l{ti}_{ts}")
            if f"xn2free{sl}" in em.ms:
                em.wait("act", f"xn2free{sl}")
            if SPARSE:
                kk = f"ln2d{ti}_{ts - 1}" if ts >= 1 else f"ln2d{ti - 1}_{TS - 1}"
                if kk in em.ms:
                    em.wait("act", kk)
            if sl == 0 and ti > 0 and B_GATE:
                em.wait("act", f"ltmtr{ti - 1}")
            em.inc("act", act.activation(b.xn2[:, sl, :], b.x2_sb[:, ts, :], AF.Square,
                                         accum_out=st[:, s0 + 1:s0 + 2]))
            em.mark(f"bsq{ti}_{ts}", "act")
            em.wait("dve", f"x2l{ti}_{ts}")
            em.inc("dve", dve.reduce_sum(st[:, s0 + 0:s0 + 1], b.x2_sb[:, ts, :], axis=AX.X))
            dve.drain()
            em.wait("dve", f"bsq{ti}_{ts}")
            dve.tensor_scalar_mul(st[:, s0 + 2:s0 + 3], st[:, s0 + 0:s0 + 1], 1.0 / E)
            dve.tensor_scalar_mul(st[:, s0 + 3:s0 + 4], st[:, s0 + 1:s0 + 2], 1.0 / E)
            dve.drain()
            dve.tensor_mul(st[:, s0 + 4:s0 + 5], st[:, s0 + 2:s0 + 3], st[:, s0 + 2:s0 + 3])
            dve.drain()
            em.inc("dve", dve.tensor_sub(st[:, s0 + 5:s0 + 6], st[:, s0 + 3:s0 + 4], st[:, s0 + 4:s0 + 5]))
            em.mark(f"bvar{ti}_{ts}", "dve")
            em.wait("act", f"bvar{ti}_{ts}")
            em.wait("act", "persistbf")
            em.wait("act", "consts")
            act.activation(st[:, s0 + 6:s0 + 7], st[:, s0 + 5:s0 + 6], AF.Ln, bias=t.eps_col[:, 0:1], scale=1.0)
            act.drain()
            em.inc("act", act.activation(st[:, s0 + 7:s0 + 8], st[:, s0 + 6:s0 + 7], AF.Exp, bias=0.0, scale=-0.5))
            em.mark(f"brstd{ti}_{ts}", "act")
            em.wait("dve", f"brstd{ti}_{ts}")
            dve.drain()
            em.inc("dve", dve.scalar_tensor_tensor(st[:, s0 + 8:s0 + 9], st[:, s0 + 2:s0 + 3], -1.0,
                                                   st[:, s0 + 7:s0 + 8],
                                                   op0=ALU.mult, op1=ALU.mult))
            em.mark(f"bnmr{ti}_{ts}", "dve")
            em.wait("act", f"bnmr{ti}_{ts}")
            em.inc("act", act.activation(b.xn2[:, sl, :], b.x2_sb[:, ts, :], AF.Identity,
                                         bias=st[:, s0 + 8:s0 + 9], scale=st[:, s0 + 7:s0 + 8]))
            em.mark(f"bxn{ti}_{ts}", "act")
            if SPARSE:
                em.wait("sy", f"bxn{ti}_{ts}")
                em.wait_ct("sy", "io")
                em.inc("io", sy.dma_start(out=b.ln2d_r[:, ti * TS + ts, :], in_=b.xn2[:, sl, :]), 16)
                em.mark(f"ln2d{ti}_{ts}", "io")
            for fc in range(NF):
                bk = ph[fc % 2]
                bk.producer_wait()
                em.wait("pe", f"bxn{ti}_{ts}")
                em.wait("pe", "identf")
                if ti > 0 and B_EXPERTS:
                    em.wait("pe", f"lnT2used{ti - 1}")
                ins = pe.transpose(bk.t[:, 0:128], b.xn2[:, sl, fc * 128:(fc + 1) * 128], b.identf[:])
                bk.fill(ins)
                bk.consumer_wait("act")
                bk.free(act.copy(b.ln2T[:, fc, ts * 128:(ts + 1) * 128], bk.t[:, 0:128]), "act")
            em.mark(f"xn2free{sl}", "pe")
        em.mark(f"lnT2{ti}", "act")

        if B_GATE:
            bk = ph[0]
            bk.producer_wait()
            em.wait("pe", f"lnT2{ti}")
            em.wait("pe", "persistbf")
            em.wait("pe", "consts")
            for hf in range(2):
                for fc in range(NF):
                    ins = pe.matmul(bk.t[0:NE, hf * HF:(hf + 1) * HF], b.gw_sb2[:, fc, :],
                                    b.ln2T[:, fc, hf * HF:(hf + 1) * HF], start=(fc == 0),
                                    stop=(fc == NF - 1))
            bk.fill(ins)
            bk.consumer_wait("act")
            em.wait("act", "xn2free0")
            if SPARSE:
                em.wait("act", f"ln2d{ti}_{TS - 1}")
            em.wait("act", "persistf32")
            bk.free(act.activation(b.xn2[0:NE, 0, :], bk.t[0:NE, :], AF.Identity,
                                   bias=t.gb_col[0:NE, 0:1], scale=1.0), "act")
            em.mark(f"logitsT{ti}", "act")

            for ts in range(TS):
                bk = ph[1 - ts % 2]
                bk.producer_wait()
                em.wait("pe", f"logitsT{ti}")
                ins = pe.transpose(bk.t[0:128, 0:NE], b.xn2[0:NE, 0, ts * 128:(ts + 1) * 128],
                                   b.identf[0:NE, 0:NE])
                bk.fill(ins)
                bk.consumer_wait("act")
                if ti > 0:
                    em.wait("act", f"combine{ti - 1}_{TS - 1}")
                bk.free(act.copy(b.logits_tm[:, ts, :], bk.t[0:128, 0:NE]), "act")
                em.mark(f"ltm{ti}_{ts}", "act")
            em.mark(f"ltmtr{ti}", "pe")

            for ts in range(TS):
                gg = b.gg
                g0 = (ts % 2) * 10
                lg = b.logits_tm[:, ts, :]
                em.wait("dve", f"ltm{ti}_{ts}")
                dve.reduce_max(gg[:, g0 + 0:g0 + 1], lg, axis=AX.X)
                dve.drain()
                dve.tensor_single_scalar(b.maskA[:], lg, gg[:, g0 + 0:g0 + 1], op=ALU.is_ge)
                dve.drain()
                dve.scalar_tensor_tensor(b.xm[:], b.maskA[:], -BIG, lg, op0=ALU.mult, op1=ALU.add)
                dve.drain()
                dve.reduce_max(gg[:, g0 + 1:g0 + 2], b.xm[:], axis=AX.X)
                ins = dve.tensor_scalar_mul(gg[:, g0 + 2:g0 + 3], gg[:, g0 + 0:g0 + 1], -1.0)
                em.inc("dve", ins)
                em.mark(f"gd1_{ti}_{ts}", "dve")
                em.wait("act", f"gd1_{ti}_{ts}")
                em.inc("act", act.activation(gg[:, g0 + 3:g0 + 4], gg[:, g0 + 1:g0 + 2], AF.Exp,
                                             bias=gg[:, g0 + 2:g0 + 3], scale=1.0))
                em.mark(f"gexp{ti}_{ts}", "act")
                em.wait("dve", f"gexp{ti}_{ts}")
                dve.tensor_scalar_add(gg[:, g0 + 4:g0 + 5], gg[:, g0 + 3:g0 + 4], 1.0)
                dve.drain()
                dve.reciprocal(gg[:, g0 + 5:g0 + 6], gg[:, g0 + 4:g0 + 5])
                dve.drain()
                dve.tensor_mul(gg[:, g0 + 6:g0 + 7], gg[:, g0 + 3:g0 + 4], gg[:, g0 + 5:g0 + 6])
                dve.tensor_single_scalar(b.maskAB[:], lg, gg[:, g0 + 1:g0 + 2], op=ALU.is_ge)
                dve.drain()
                dve.tensor_sub(gg[:, g0 + 7:g0 + 8], gg[:, g0 + 5:g0 + 6], gg[:, g0 + 6:g0 + 7])
                dve.drain()
                dve.tensor_scalar_mul(b.xm[:], b.maskA[:], gg[:, g0 + 7:g0 + 8])
                dve.drain()
                em.inc("dve", dve.scalar_tensor_tensor(b.combine[:, ts, :], b.maskAB[:], gg[:, g0 + 6:g0 + 7],
                                                       b.xm[:], op0=ALU.mult, op1=ALU.add))
                em.mark(f"combine{ti}_{ts}", "dve")
                if SPARSE:
                    dve.drain()
                    dve.tensor_single_scalar(b.mask_all[:, ts, :], b.combine[:, ts, :], 0.0,
                                             op=ALU.is_gt)
                    ins = dve.tensor_copy(b.maskA_all[:, ts, :], b.maskA[:])
                    em.inc("dve", ins)
                    em.mark(f"maskp{ti}_{ts}", "dve")
                    em.wait("pl", f"combine{ti}_{ts}")
                    em.wait("pl", "zinit")
                    em.wait_ct("pl", "wl")
                    em.inc("wl", pl.dma_start(out=b.combd_r[:, ti * TS + ts, :], in_=b.combine[:, ts, :]), 16)
                bk = ph[1 - ts % 2]
                bk.producer_wait()
                em.wait("pe", f"combine{ti}_{ts}")
                ins = pe.transpose(bk.t[0:NE, 0:128], b.combine[:, ts, :], b.identf[:])
                bk.fill(ins)
                bk.consumer_wait("act")
                if ti > 0 and B_EXPERTS and B_W2:
                    em.wait("act", f"w2used{(ti - 1) * 9 + (0 if SPARSE else 8)}")
                bk.free(act.copy(b.combineT[0:NE, ts, :], bk.t[0:NE, 0:128]), "act")
                em.mark(f"combT{ti}_{ts}", "act")

            if SPARSE:
                em.wait("act", f"maskp{ti}_{TS - 1}")
                em.inc("act", act.copy(b.mask_bf[:], b.mask_all[:].rearrange("p ts e -> p (ts e)")))
                em.mark(f"maskbf{ti}", "act")
                bk = ph[0]
                bk.producer_wait()
                em.wait("pe", f"maskbf{ti}")
                em.wait("pe", "persistf32")
                ins = pe.matmul(bk.t[:, 0:TS * NE], t.ltri_sb[:], b.mask_bf[:], start=True, stop=True)
                bk.fill(ins)
                bk.consumer_wait("act")
                bk.free(act.copy(b.rankw[:], bk.t[:, 0:TS * NE]), "act")
                em.mark(f"rankw{ti}", "act")
                bk = ph[1]
                bk.producer_wait()
                ins = pe.matmul(bk.t[0:1, 0:TS * NE], t.ones_col[:, 0:1], b.mask_bf[:],
                                start=True, stop=True)
                bk.fill(ins)
                bk.consumer_wait("act")
                bk.free(act.copy(b.cnt[0:1, :], bk.t[0:1, 0:TS * NE]), "act")
                em.mark(f"cnt{ti}", "act")
                em.wait("dve", f"cnt{ti}")
                dve.memset(b.base[0:1, 0:NE], 0.0)
                for ts in range(1, TS):
                    dve.drain()
                    dve.tensor_add(b.base[0:1, ts * NE:(ts + 1) * NE],
                                   b.base[0:1, (ts - 1) * NE:ts * NE],
                                   b.cnt[0:1, (ts - 1) * NE:ts * NE])
                ins = dve.drain()
                em.inc("dve", dve.engine_nop() if em.active == "dve" and not getattr(em, "sched", False) else _DUM)
                em.mark(f"base{ti}", "dve")
                bk = po[0]
                bk.producer_wait()
                em.wait("pe", f"base{ti}")
                em.wait("pe", "consts")
                ins = pe.matmul(bk.t[:, 0:TS * NE], t.ones_f[0:1, :], b.base[0:1, :],
                                start=True, stop=True)
                bk.fill(ins)
                bk.consumer_wait("dve")
                em.wait("dve", f"rankw{ti}")
                bk.free(dve.tensor_add(b.rankw[:], b.rankw[:], bk.t[:, 0:TS * NE]), "dve")
                em.mark(f"rankg{ti}", "dve")
                em.wait("pl", "idxfill")
                em.wait_ct("pl", "wl")
                em.inc("wl", pl.dma_start(out=t.idxd.rearrange("(s p e) one -> p s (e one)", p=128, e=NE),
                                          in_=b.idx_fill[:]), 16)
                for ts in range(TS):
                    sp = (ts % 2)
                    em.wait("dve", f"rankg{ti}")
                    if ts >= 2:
                        em.wait("dve", f"rsc{ti}_{ts - 2}")
                    mA = b.maskA_all[:, ts, :]
                    rk = b.rankw[:, ts * NE:(ts + 1) * NE]
                    dve.tensor_sub(b.maskAB[:], b.mask_all[:, ts, :], mA)
                    dve.tensor_mul(b.xm[:], mA, t.evec_sb[:])
                    dve.drain()
                    dve.reduce_sum(b.maskA[:, 0:1], b.xm[:], axis=AX.X)
                    dve.drain()
                    dve.tensor_mul(b.xm[:], b.maskAB[:], t.evec_sb[:])
                    dve.drain()
                    dve.reduce_sum(b.maskA[:, 1:2], b.xm[:], axis=AX.X)
                    dve.drain()
                    dve.tensor_mul(b.xm[:], mA, rk)
                    dve.drain()
                    dve.reduce_sum(b.maskA[:, 2:3], b.xm[:], axis=AX.X)
                    dve.drain()
                    dve.tensor_mul(b.xm[:], b.maskAB[:], rk)
                    dve.drain()
                    dve.reduce_sum(b.maskA[:, 3:4], b.xm[:], axis=AX.X)
                    dve.reduce_sum(b.maskA[:, 4:5], b.maskAB[:], axis=AX.X)
                    dve.drain()
                    dve.scalar_tensor_tensor(b.maskA[:, 5:6], b.maskA[:, 2:3], float(NE), b.maskA[:, 0:1],
                                             op0=ALU.mult, op1=ALU.add)
                    dve.scalar_tensor_tensor(b.maskA[:, 6:7], b.maskA[:, 3:4], float(NE), b.maskA[:, 1:2],
                                             op0=ALU.mult, op1=ALU.add)
                    dve.drain()
                    dve.scalar_tensor_tensor(b.maskA[:, 7:8], b.maskA[:, 4:5], -8192.0, b.maskA[:, 6:7],
                                             op0=ALU.mult, op1=ALU.add)
                    dve.drain()
                    dve.tensor_scalar_add(b.maskA[:, 7:8], b.maskA[:, 7:8], 8192.0)
                    dve.drain()
                    dve.tensor_copy(b.roff[:, sp, 0:1], b.maskA[:, 5:6])
                    ins = dve.tensor_copy(b.roff[:, sp, 1:2], b.maskA[:, 7:8])
                    em.inc("dve", ins)
                    em.mark(f"roff{ti}_{ts}", "dve")
                    em.wait("pl", f"roff{ti}_{ts}")
                    em.wait_ct("pl", "wl")
                    em.inc("wl", pl.indirect_dma_start(
                        out=t.idxd, out_offset=bass.IndirectOffsetOnAxis(b.roff[:, sp, 0:1], axis=0),
                        in_=t.tokid_sb[:, ti * TS + ts:ti * TS + ts + 1], in_offset=None,
                        bounds_check=b.bnd_i, oob_is_err=False), 16)
                    em.wait_ct("pl", "wl")
                    em.inc("wl", pl.indirect_dma_start(
                        out=t.idxd, out_offset=bass.IndirectOffsetOnAxis(b.roff[:, sp, 1:2], axis=0),
                        in_=t.tokid_sb[:, ti * TS + ts:ti * TS + ts + 1], in_offset=None,
                        bounds_check=b.bnd_i, oob_is_err=False), 16)
                    em.mark(f"rsc{ti}_{ts}", "wl")
                em.wait_ct("pl", "wl")
                em.inc("wl", pl.dma_start(
                    out=b.idxg[:],
                    in_=t.idxd.rearrange("(s p e) one -> p s (e one)", p=128, e=NE)), 16)
                em.mark(f"idxg{ti}", "wl")

        prev_lnTg = [None]
        prev_flushg = [None]
        last_osc = [None]

        def emit_stores():
            for ts in range(TS):
                if B_EXPERTS and B_W2:
                    se = 0 if SPARSE else 8
                    em.wait("sy", f"flush{ti * 9 + se}_{ts}")
                elif B_GATE:
                    em.wait("sy", f"combT{ti}_{ts}")
                else:
                    em.wait("sy", f"lnT2{ti}")
                em.wait_ct("sy", "io")
                em.inc("io", sy.dma_start(out=out_tile[ti][:, ts, :], in_=b.x2_sb[:, ts, :]), 16)
                em.mark(f"outdma{ti}_{ts}", "io")

        def emit_expert(pos, e):
            ei = ti * 9 + pos
            dense = (not SPARSE) or e == 8
            nsub = TS if dense else NSUB
            ncol = TT if dense else CAP
            w1src = t.w1[e] if e < NE else t.ws1
            w2src = t.w2[e] if e < NE else t.ws2

            def w1_dma(sbi):
                slot = sbi % 3
                if w1_slot_used[slot] is not None:
                    em.wait("pl", w1_slot_used[slot])
                em.inc(f"w1s{slot}", pl.dma_start(
                    out=b.w1b[:, slot, :, :],
                    in_=w1src[:, sbi * SLAB * 128:(sbi + 1) * SLAB * 128].rearrange(
                        "(fc p) h -> p fc h", p=128)), 16)
                em.mark(f"w1ld{ei}_{sbi}", f"w1s{slot}")
                w1_slot_used[slot] = f"w1used{ei}_{sbi}"

            def w2_dma(hb):
                if hb == 0 and prev_w2used[0] is not None:
                    em.wait("pl", prev_w2used[0])
                em.wait_ct("pl", "wl")
                em.inc("wl", pl.dma_start(
                    out=b.w2b[:, hb * 4:(hb + 1) * 4, :],
                    in_=w2src[hb * 512:(hb + 1) * 512, :].rearrange("(hc p) f -> p hc f", p=128)), 16)
                em.mark(f"w2ld{ei}_{hb}", "wl")

            if not dense:
                em.wait("pl", f"idxg{ti}")
                if prev_lnTg[0] is not None:
                    em.wait("pl", prev_lnTg[0])
                if prev_flushg[0] is not None:
                    em.wait("pl", prev_flushg[0])
                for sub in range(NSUB):
                    if f"lnTgp{ei}_{sub - 1}" in em.ms:
                        em.wait("pl", f"lnTgp{ei}_{sub - 1}")
                    em.wait_ct("pl", "wl")
                    em.inc("wl", pl.indirect_dma_start(
                        out=b.ln2g[:, 0, :], out_offset=None,
                        in_=t.ln2d, in_offset=bass.IndirectOffsetOnAxis(b.idxg[:, sub, e:e + 1], axis=0),
                        bounds_check=b.bnd_s, oob_is_err=False), 16)
                    em.wait_ct("pl", "wl")
                    em.inc("wl", pl.indirect_dma_start(
                        out=b.cbg[:, sub, :], out_offset=None,
                        in_=t.combd, in_offset=bass.IndirectOffsetOnAxis(b.idxg[:, sub, e:e + 1], axis=0),
                        bounds_check=b.bnd_s, oob_is_err=False), 16)
                    em.mark(f"gat{ei}_{sub}", "wl")
                    for fc in range(NF):
                        bk = ph[fc % 2]
                        bk.producer_wait()
                        em.wait("pe", f"gat{ei}_{sub}")
                        ins = pe.transpose(bk.t[:, 0:128], b.ln2g[:, 0, fc * 128:(fc + 1) * 128],
                                           b.identf[:])
                        bk.fill(ins)
                        bk.consumer_wait("act")
                        bk.free(act.copy(b.ln2Tg[:, fc, sub * 128:(sub + 1) * 128], bk.t[:, 0:128]), "act")
                    em.mark(f"lnTgp{ei}_{sub}", "pe")
                em.mark(f"gat{ei}", "wl")
                em.mark(f"lnTg{ei}", "act")
                prev_lnTg[0] = f"lnTgp{ei}_{NSUB - 1}"

            w1_dma(0)
            w1_dma(1)
            w1_dma(2)

            for hc in range(NHC):
                sbi = hc // SLAB
                bk = ph[hc % 2]
                em.wait("pe", f"w1ld{ei}_{sbi}")
                bk.producer_wait()
                if dense:
                    for hf in range(2):
                        for fc in range(NF):
                            ins = pe.matmul(bk.t[:, hf * HF:(hf + 1) * HF],
                                            b.w1b[:, sbi % 3, fc, (hc % SLAB) * 128:(hc % SLAB + 1) * 128],
                                            b.ln2T[:, fc, hf * HF:(hf + 1) * HF],
                                            start=(fc == 0), stop=(fc == NF - 1))
                else:
                    em.wait("pe", f"lnTg{ei}")
                    for fc in range(NF):
                        ins = pe.matmul(bk.t[:, 0:CAP],
                                        b.w1b[:, sbi % 3, fc, (hc % SLAB) * 128:(hc % SLAB + 1) * 128],
                                        b.ln2Tg[:, fc, 0:CAP],
                                        start=(fc == 0), stop=(fc == NF - 1))
                bk.fill(ins)
                if hc == NHC - 1 and e == 8:
                    em.mark(f"lnT2used{ti}", "pe")
                bk.consumer_wait("act")
                em.wait("act", "persistf32")
                bias = t.b1Ts[:, e, hc:hc + 1] if e < NE else t.bs1Ts[:, hc:hc + 1]
                bk.free(act.activation(b.h_sb[:, hc, 0:ncol], bk.t[:, 0:ncol],
                                       AF.Gelu, bias=bias, scale=1.0), "act")
                em.mark(f"h{ei}_{hc}", "act")
                if hc % SLAB == SLAB - 1:
                    em.mark(f"w1used{ei}_{sbi}", "pe")
                    if sbi + 3 < NSLAB:
                        w1_dma(sbi + 3)
                    if sbi % 2 == 0 and B_W2:
                        w2_dma(sbi // 2)
            prev_w2used[0] = f"w2used{ei}" if B_W2 else prev_w2used[0]

            for ts in range(nsub if B_W2 else 0):
                bk = po[ts % 2]
                bk.producer_wait()
                for hf in range(2):
                    for hc in range(NHC):
                        em.wait("pe", f"h{ei}_{hc}")
                        em.wait("pe", f"w2ld{ei}_{hc // 4}")
                        ins = pe.matmul(bk.t[:, hf * HF:(hf + 1) * HF],
                                        b.h_sb[:, hc, ts * 128:(ts + 1) * 128],
                                        b.w2b[:, hc, hf * HF:(hf + 1) * HF],
                                        start=(hc == 0), stop=(hc == NHC - 1 and e != 8))
                    if e == 8:
                        em.wait("pe", f"combT{ti}_{ts}")
                        pe.matmul(bk.t[:, hf * HF:(hf + 1) * HF], b.combineT[0:NE, ts, :],
                                  t.b2_bf[0:NE, hf * HF:(hf + 1) * HF], start=False, stop=False)
                        ins = pe.matmul(bk.t[:, hf * HF:(hf + 1) * HF], t.ones_bf[0:1, 0:128],
                                        t.bs2_bf[0:1, hf * HF:(hf + 1) * HF], start=False, stop=True)
                bk.fill(ins)
                if dense:
                    bk.consumer_wait("dve")
                    if SPARSE:
                        pass
                    else:
                        em.wait("dve", f"combine{ti}_{ts}")
                    if b.lastflush[ts] is not None:
                        em.wait("dve", b.lastflush[ts])
                    x2s = b.x2_sb[:, ts, :]
                    if e < NE:
                        em.wait("dve", f"combine{ti}_{ts}")
                        ins = dve.scalar_tensor_tensor(x2s, bk.t[:], b.combine[:, ts, e:e + 1],
                                                       x2s, op0=ALU.mult, op1=ALU.add)
                    else:
                        ins = dve.tensor_add(x2s, bk.t[:], x2s)
                    bk.free(ins, "dve")
                    em.mark(f"flush{ei}_{ts}", "dve")
                    b.lastflush[ts] = f"flush{ei}_{ts}"
                else:
                    bk.consumer_wait("dve")
                    em.wait("dve", f"gat{ei}")
                    if last_osc[0] is not None:
                        em.wait("dve", last_osc[0])
                    ins = dve.tensor_scalar_mul(b.out_g[:], bk.t[:], b.cbg[:, ts, e:e + 1])
                    bk.free(ins, "dve")
                    em.mark(f"flushg{ei}_{ts}", "dve")
                    em.wait("pl", f"flushg{ei}_{ts}")
                    em.wait("pl", f"outdma{ti}_{TS - 1}")
                    em.wait_ct("pl", "wl")
                    em.inc("wl", pl.indirect_dma_start(
                        out=t.out,
                        out_offset=bass.IndirectOffsetOnAxis(b.idxg[:, ts, e:e + 1], axis=0),
                        in_=b.out_g[:], in_offset=None,
                        bounds_check=b.bnd_s, oob_is_err=False,
                        compute_op=ALU.add), 16)
                    em.mark(f"osc{ei}_{ts}", "wl")
                    last_osc[0] = f"osc{ei}_{ts}"
                    prev_flushg[0] = f"flushg{ei}_{ts}"
            if B_W2:
                em.mark(f"w2used{ei}", "pe")

        order = ([8] + list(range(NE))) if SPARSE else list(range(9))
        for pos, e in enumerate(order if B_EXPERTS else []):
            emit_expert(pos, e)
            if B_EXPERTS and SPARSE and pos == 0:
                emit_stores()
        if not (B_EXPERTS and SPARSE):
            emit_stores()
        b.lastflush = [None for _ in range(TS)]


def build_nc():
    nc = bass.Bass("TRN2", target_bir_lowering=False)
    t = declare_dram(nc)
    with contextlib.ExitStack() as stack:
        en = stack.enter_context
        t.ones_bf = en(nc.sbuf_tensor("ones_bf", [1, 128], BF16))
        t.ones_col = en(nc.sbuf_tensor("ones_col", [128, 1], BF16))
        t.eps_col = en(nc.sbuf_tensor("eps_col", [128, 1], F32))
        t.ident_sb = en(nc.sbuf_tensor("ident_sb", [128, 128], BF16))
        t.ltri_sb = en(nc.sbuf_tensor("ltri_sb", [128, 128], BF16))
        t.evec_sb = en(nc.sbuf_tensor("evec_sb", [128, NE], F32))
        t.tokid_sb = en(nc.sbuf_tensor("tokid_sb", [128, C], mybir.dt.int32))
        t.ones_f = en(nc.sbuf_tensor("ones_f", [1, 128], F32))
        t.bqTs = en(nc.sbuf_tensor("bqTs", [128, NF], F32))
        t.bkTs = en(nc.sbuf_tensor("bkTs", [128, NF], F32))
        t.b1Ts = en(nc.sbuf_tensor("b1Ts", [128, NE, NHC], F32))
        t.bs1Ts = en(nc.sbuf_tensor("bs1Ts", [128, NHC], F32))
        t.bs2_bf = en(nc.sbuf_tensor("bs2_bf", [1, E], BF16))
        t.b2_bf = en(nc.sbuf_tensor("b2_bf", [NE, E], BF16))
        t.gb_col = en(nc.sbuf_tensor("gb_col", [NE, 1], F32))
        t.gw_sb = en(nc.sbuf_tensor("gw_sb", [128, NF, NE], BF16))

        sems = {k: en(nc.semaphore(f"s_{k}")) for k in SEMS}
        block = en(nc.Block())
        ems = {name: Em(name, sems) for name in ENG}
        emx = Em("__sched__", sems)
        emx.sched = True
        emx.prog = {e: [] for e in ENG}
        emx._last_op = None

        def run(name, fn, *args):
            em = ems[name]
            def body(engobj):
                em.engine = engobj
                fn(em, *args)
                em.engine = None
            getattr(block, {"pe": "tensor", "act": "scalar", "dve": "vector",
                            "sy": "sync", "pl": "gpsimd"}[name])(body)

        with contextlib.ExitStack() as sa:
            ea = sa.enter_context
            a = NS()
            a.x_ln = ea(nc.sbuf_tensor("x_ln", [128, 4, E], F32))
            a.x_res = ea(nc.sbuf_tensor("x_res", [128, 2, E], F32))
            a.xn_sb = ea(nc.sbuf_tensor("xn_sb", [128, 2, E], BF16))
            a.ln1T = ea(nc.sbuf_tensor("ln1T", [128, NF, G * TC], BF16))
            a.qT = ea(nc.sbuf_tensor("qT", [128, NF, G * TC], BF16))
            a.kT = ea(nc.sbuf_tensor("kT", [128, NF, G * TC], BF16))
            a.v_sb = ea(nc.sbuf_tensor("v_sb", [128, G, E], BF16))
            a.attnT = ea(nc.sbuf_tensor("attnT", [128, NF, TC], BF16))
            a.sq_scr = ea(nc.sbuf_tensor("sq_scr", [128, E], F32))
            a.stat = ea(nc.sbuf_tensor("statA", [128, 20], F32))
            a.km_scr = ea(nc.sbuf_tensor("km_scr", [128, NF], F32))
            a.kmeansT = ea(nc.sbuf_tensor("kmeansT", [128, NF, C], BF16))
            a.vmeans = ea(nc.sbuf_tensor("vmeans", [C, E], BF16))
            a.vm_stage = ea(nc.sbuf_tensor("vm_stage", [1, E], F32))
            a.negmax = ea(nc.sbuf_tensor("negmax", [128, 32], F32))
            a.sums = ea(nc.sbuf_tensor("sumsA", [128, 32], F32))
            a.recip = ea(nc.sbuf_tensor("recipA", [128, 32], F32))
            a.ascr = ea(nc.sbuf_tensor("ascr", [128, 16, TC], BF16))
            a.an_scr = ea(nc.sbuf_tensor("an_scr", [128, 16, TC], BF16))
            a.aiscr = ea(nc.sbuf_tensor("aiscr", [128, 16, C], BF16))
            a.ain_scr = ea(nc.sbuf_tensor("ain_scr", [128, 16, C], BF16))
            a.aT_sb = ea(nc.sbuf_tensor("aT_sb", [128, 16, TC], BF16))
            a.aiT_sb = ea(nc.sbuf_tensor("aiT_sb", [C, 16, TC], BF16))
            a.xo_sb = ea(nc.sbuf_tensor("xo_sb", [128, 2, E], F32))
            a.bv_bf = ea(nc.sbuf_tensor("bv_bf", [1, E], BF16))
            a.bo_bf = ea(nc.sbuf_tensor("bo_bf", [1, E], BF16))
            a.wq_sb = ea(nc.sbuf_tensor("wq_sb", [128, NF, E], BF16))
            a.wk_sb = ea(nc.sbuf_tensor("wk_sb", [128, NF, E], BF16))
            a.wv_sb = ea(nc.sbuf_tensor("wv_sb", [128, NF, E], BF16))
            a.wo_sb = ea(nc.sbuf_tensor("wo_sb", [128, NF, E], BF16))
            a.ps_mm0 = ea(nc.psum_tensor("ps_mm0", [128, HF], F32))
            a.ps_mm1 = ea(nc.psum_tensor("ps_mm1", [128, HF], F32))
            a.ps_sA = ea(nc.psum_tensor("ps_sA", [128, HF], F32))
            a.ps_sB = ea(nc.psum_tensor("ps_sB", [128, HF], F32))
            a.ps_trA = ea(nc.psum_tensor("ps_trA", [128, 128], BF16))
            a.ps_trB = ea(nc.psum_tensor("ps_trB", [128, 128], BF16))
            a.ps_at0 = ea(nc.psum_tensor("ps_at0", [128, 128], F32))
            a.ps_at1 = ea(nc.psum_tensor("ps_at1", [128, 128], F32))
            a.seen_exp = set()
            a.seen_an = set()
            a.seen_tra = set()
            a.seen_atm = set()

            def phA(em):
                a.seen_exp = set()
                a.seen_an = set()
                a.seen_tra = set()
                a.seen_atm = set()
                emit_init(em, t)
                emit_phase_a(em, t, a)
                em.mark("endA_pe", "pe")
                em.mark("endA_act", "act")
                em.mark("endA_dve", "dve")
                em.mark("endA_io", "io")
                em.mark("endA_wl", "wl")
                em.mark("endA_plc", "plc")

            phA(emx)
            for name in ENG:
                run(name, phA)

        with contextlib.ExitStack() as sb:
            eb = sb.enter_context
            b = NS()
            b.x2_sb = eb(nc.sbuf_tensor("x2_sb", [128, TS, E], F32))
            b.xn2 = eb(nc.sbuf_tensor("xn2", [128, 2, E], F32))
            b.ln2T = eb(nc.sbuf_tensor("ln2T", [128, NF, TT], BF16))
            b.h_sb = eb(nc.sbuf_tensor("h_sb", [128, NHC, TT], BF16))
            b.w1b = eb(nc.sbuf_tensor("w1b", [128, 3, NF, SLAB * 128], BF16))
            b.w2b = eb(nc.sbuf_tensor("w2b", [128, NHC, E], BF16))
            b.stat2 = eb(nc.sbuf_tensor("stat2", [128, 20], F32))
            b.logits_tm = eb(nc.sbuf_tensor("logits_tm", [128, TS, NE], F32))
            b.gg = eb(nc.sbuf_tensor("gg", [128, 20], F32))
            b.maskA = eb(nc.sbuf_tensor("maskA", [128, NE], F32))
            b.maskAB = eb(nc.sbuf_tensor("maskAB", [128, NE], F32))
            b.xm = eb(nc.sbuf_tensor("xm", [128, NE], F32))
            b.combine = eb(nc.sbuf_tensor("combine", [128, TS, NE], F32))
            b.combineT = eb(nc.sbuf_tensor("combineT", [NE, TS, TC], BF16))
            b.identf = eb(nc.sbuf_tensor("identf", [128, 128], F32))
            b.ph0 = eb(nc.psum_tensor("ph0", [128, TT], F32))
            b.ph1 = eb(nc.psum_tensor("ph1", [128, TT], F32))
            b.po0 = eb(nc.psum_tensor("po0", [128, TT], F32))
            b.po1 = eb(nc.psum_tensor("po1", [128, TT], F32))
            if SPARSE:
                b.ln2g = eb(nc.sbuf_tensor("ln2g", [128, 1, E], F32))
                b.ln2Tg = b.h_sb[:][:, 0:NF, 512:512 + CAP]
                b.cbg = eb(nc.sbuf_tensor("cbg", [128, NSUB, NE], F32))
                b.idxg = eb(nc.sbuf_tensor("idxg", [128, NSUB, NE], mybir.dt.int32))
                b.out_g = eb(nc.sbuf_tensor("out_g", [128, E], F32))
                b.mask_all = eb(nc.sbuf_tensor("mask_all", [128, TS, NE], F32))
                b.maskA_all = eb(nc.sbuf_tensor("maskA_all", [128, TS, NE], F32))
                b.mask_bf = eb(nc.sbuf_tensor("mask_bf", [128, TS * NE], BF16))
                b.rankw = eb(nc.sbuf_tensor("rankw", [128, TS * NE], F32))
                b.cnt = eb(nc.sbuf_tensor("cntB", [1, TS * NE], F32))
                b.base = eb(nc.sbuf_tensor("baseB", [1, TS * NE], F32))
                b.idx_fill = eb(nc.sbuf_tensor("idx_fill", [128, NSUB, NE], mybir.dt.int32))
                b.roff = eb(nc.sbuf_tensor("roff", [128, 2, 2], mybir.dt.int32))
            b.ln2d_r = t.ln2d.rearrange("(c p) f -> p c f", p=128)
            b.combd_r = t.combd.rearrange("(c p) e -> p c e", p=128)
            b.gw_sb2 = t.gw_sb
            b.lastflush = [None for _ in range(TS)]

            def phB(em):
                b.lastflush = [None for _ in range(TS)]
                dveB = Eng(em, "dve")
                em.wait("dve", "persistf32")
                for k in ("endA_pe", "endA_act", "endA_dve", "endA_io", "endA_wl", "endA_plc"):
                    em.wait("dve", k)
                em.inc("dve", dveB.tensor_copy(b.identf[:], t.ident_sb[:]))
                em.mark("identf", "dve")
                emit_phase_b(em, t, b)
                em.mark("final_io", "io")
                em.wait("sy", "final_io")

            phB(emx)
            _simulate(emx.prog)
            for name in ENG:
                run(name, phB)

    return nc


def kernel(**inputs):
    xin = np.ascontiguousarray(np.asarray(inputs["L0_attn_hidden_states"], dtype=np.float32))
    f32 = lambda v: np.ascontiguousarray(np.asarray(v, dtype=np.float32))
    wq = f32(inputs["L0_attn_wq"]); wk = f32(inputs["L0_attn_wk"])
    wv = f32(inputs["L0_attn_wv"]); wo = f32(inputs["L0_attn_wo"])
    base = {
        "wqT": np.ascontiguousarray(wq.T), "wkT": np.ascontiguousarray(wk.T),
        "wvT": np.ascontiguousarray(wv.T), "woT": np.ascontiguousarray(wo.T),
        "bqT": np.ascontiguousarray(f32(inputs["L0_attn_bq"]).reshape(NF, 128).T),
        "bkT": np.ascontiguousarray(f32(inputs["L0_attn_bk"]).reshape(NF, 128).T),
        "bv": f32(inputs["L0_attn_bv"]).reshape(1, E),
        "bo": f32(inputs["L0_attn_bo"]).reshape(1, E),
        "gw": f32(inputs["L0_ffn_gate_w"]),
        "gb": f32(inputs["L0_ffn_gate_b"]).reshape(NE, 1),
        "w1": f32(inputs["L0_ffn_w1"]),
        "b1T": np.ascontiguousarray(f32(inputs["L0_ffn_b1"]).reshape(NE, NHC, 128).transpose(0, 2, 1)),
        "w2": f32(inputs["L0_ffn_w2"]),
        "b2": f32(inputs["L0_ffn_b2"]),
        "ws1": f32(inputs["L0_ffn_ws1"]),
        "bs1T": np.ascontiguousarray(f32(inputs["L0_ffn_bs1"]).reshape(NHC, 128).T),
        "ws2": f32(inputs["L0_ffn_ws2"]),
        "bs2": f32(inputs["L0_ffn_bs2"]).reshape(1, E),
        "ident": np.eye(128, dtype=ml_dtypes.bfloat16),
        "ltri": np.triu(np.ones((128, 128), dtype=ml_dtypes.bfloat16), 1),
        "evec": np.tile(np.arange(NE, dtype=np.float32), (128, 1)),
        "tokid": (np.arange(C, dtype=np.int32)[None, :] * 128
                  + np.arange(128, dtype=np.int32)[:, None]).astype(np.int32),
    }
    in_maps = []
    for core in range(NCORE):
        m = dict(base)
        m["x"] = np.ascontiguousarray(xin[core])
        in_maps.append(m)
    nc = build_nc()
    res = run_bass_kernel_spmd(nc, in_maps, core_ids=list(range(NCORE)))
    out = np.stack([np.asarray(res.results[i]["out"]) for i in range(NCORE)], axis=0)
    return out.astype(np.float32)


# version 17
# speedup vs baseline: 1.1271x; 1.0819x over previous
import contextlib
import numpy as np
import ml_dtypes
import concourse.bass as bass
import concourse.mybir as mybir
from concourse.bass_utils import run_bass_kernel_spmd

F32 = mybir.dt.float32
BF16 = mybir.dt.bfloat16
AF = mybir.ActivationFunctionType
ALU = mybir.AluOpType
AX = mybir.AxisListType

NCORE = 8
S, E, H, D = 4096, 1024, 16, 64
TC, C = 128, 32
G, NG = 8, 4
NF = 8
HID, NE, NHC = 4096, 8, 32
TT, NTILE, TS = 1024, 4, 8
HF = 512
EPS = 1e-5
SCL = 0.125
SLAB = 2
NSLAB = NHC // SLAB
BIG = 1.0e30
B_HEADS = True
B_TAILS = True
B_EXPERTS = True
B_W2 = True
B_GATE = True
SPARSE = False
CAP = 384
NSUB = CAP // 128

ENG = ("pe", "act", "dve", "sy", "pl")
SEMS = ("pe", "act", "dve", "io", "wl", "plc", "w1s0", "w1s1", "w1s2")


class _Dummy:
    def then_inc(self, *a, **k):
        return self


_DUM = _Dummy()


class Eng:

    def __init__(self, em, name):
        object.__setattr__(self, "_em", em)
        object.__setattr__(self, "_name", name)

    def __getattr__(self, op):
        em = self._em
        if getattr(em, "sched", False):
            name = self._name
            def rec(*a, **k):
                item = ["op", op, []]
                em.prog[name].append(item)
                em._last_op = item
                return _DUM
            return rec
        if em.active == self._name and em.engine is not None:
            return getattr(em.engine, op)
        return lambda *a, **k: _DUM


class Em:

    def __init__(self, active, sems):
        self.active = active
        self.engine = None
        self.sems = sems
        self.ct = {k: 0 for k in SEMS}
        self.ms = {}
        self._lastwait = {}

    def inc(self, sem, inst, n=1):
        self.ct[sem] += n
        if getattr(self, "sched", False):
            self._last_op[2].append((sem, n))
        elif not isinstance(inst, _Dummy):
            inst.then_inc(self.sems[sem], n)
        return self.ct[sem]

    def mark(self, key, sem):
        self.ms[key] = (sem, self.ct[sem])

    def wait(self, eng_name, key):
        sem, val = self.ms[key]
        self._wait_val(eng_name, sem, val)

    def wait_ct(self, eng_name, sem):
        self._wait_val(eng_name, sem, self.ct[sem])

    def _wait_val(self, eng_name, sem, val):
        if val <= 0:
            return
        if getattr(self, "sched", False):
            last = self._lastwait.get((eng_name, sem), -1)
            if val <= last:
                return
            self._lastwait[(eng_name, sem)] = val
            self.prog[eng_name].append(["wait", sem, val, "wait_ct"])
            return
        if self.active == eng_name and self.engine is not None:
            last = self._lastwait.get(sem, -1)
            if val <= last:
                return
            self._lastwait[sem] = val
            self.engine.wait_ge(self.sems[sem], val)


def _simulate(prog):
    counts = {k: 0 for k in SEMS}
    idx = {e: 0 for e in ENG}
    progress = True
    while progress:
        progress = False
        for e in ENG:
            stream = prog[e]
            while idx[e] < len(stream):
                it = stream[idx[e]]
                if it[0] == "wait":
                    if counts[it[1]] >= it[2]:
                        idx[e] += 1
                        progress = True
                    else:
                        break
                else:
                    for sem, n in it[2]:
                        counts[sem] += n
                    idx[e] += 1
                    progress = True
    stuck = {e: idx[e] for e in ENG if idx[e] < len(prog[e])}
    if stuck:
        msg = []
        for e, i in stuck.items():
            msg.append(f"{e}@{i}/{len(prog[e])}: {prog[e][i]} counts_now={counts[prog[e][i][1]] if prog[e][i][0]=='wait' else ''}")
        raise RuntimeError("DEADLOCK in schedule: " + " | ".join(msg))


class Bank:

    def __init__(self, em, name, t):
        self.em, self.name, self.t = em, name, t
        self.fills = 0
        self.frees = 0

    def producer_wait(self):
        assert self.frees == self.fills, (self.name, self.frees, self.fills)
        if self.fills > 0:
            self.em.wait("pe", f"{self.name}.fr{self.fills}")

    def fill(self, inst):
        self.em.inc("pe", inst)
        self.fills += 1
        self.em.mark(f"{self.name}.fi{self.fills}", "pe")

    def consumer_wait(self, eng):
        self.em.wait(eng, f"{self.name}.fi{self.fills}")

    def free(self, inst, sem):
        self.em.inc(sem, inst)
        self.frees += 1
        assert self.frees == self.fills, (self.name, self.frees, self.fills)
        self.em.mark(f"{self.name}.fr{self.frees}", sem)


class NS:
    pass


def declare_dram(nc):
    t = NS()
    def di(name, shape, dt=F32):
        return nc.dram_tensor(name, shape, dt, kind="ExternalInput").ap()
    t.x = di("x", [S, E])
    t.wqT = di("wqT", [E, E]); t.wkT = di("wkT", [E, E])
    t.wvT = di("wvT", [E, E]); t.woT = di("woT", [E, E])
    t.bqT = di("bqT", [128, NF]); t.bkT = di("bkT", [128, NF])
    t.bv = di("bv", [1, E]); t.bo = di("bo", [1, E])
    t.gw = di("gw", [E, NE]); t.gb = di("gb", [NE, 1])
    t.w1 = di("w1", [NE, E, HID]); t.b1T = di("b1T", [NE, 128, NHC])
    t.w2 = di("w2", [NE, HID, E]); t.b2 = di("b2", [NE, E])
    t.ws1 = di("ws1", [E, HID]); t.bs1T = di("bs1T", [128, NHC])
    t.ws2 = di("ws2", [HID, E]); t.bs2 = di("bs2", [1, E])
    t.ident = di("ident", [128, 128], BF16)
    t.ltri = di("ltri", [128, 128], BF16)
    t.evec = di("evec", [128, NE])
    t.tokid = di("tokid", [128, C], mybir.dt.int32)
    t.ln2d = nc.dram_tensor("ln2d", [S, E], F32).ap()
    t.combd = nc.dram_tensor("combd", [S, NE], F32).ap()
    t.idxd = nc.dram_tensor("idxd", [CAP * NE, 1], mybir.dt.int32).ap()
    t.out = nc.dram_tensor("out", [S, E], F32, kind="ExternalOutput").ap()
    t.x2d = nc.dram_tensor("x2d", [S, E], F32, kind="ExternalOutput").ap()
    return t


def emit_init(em, t):
    pe, act, dve, sy, pl = (Eng(em, n) for n in ENG)
    em.inc("plc", pl.memset(t.ones_bf[:], 1.0), 1)
    em.inc("plc", pl.memset(t.ones_col[:], 1.0), 1)
    em.inc("plc", pl.memset(t.eps_col[:], EPS), 1)
    em.inc("plc", pl.memset(t.ones_f[:], 1.0), 1)
    em.mark("consts", "plc")
    for dst, src in ((t.ident_sb, t.ident), (t.ltri_sb, t.ltri), (t.evec_sb, t.evec),
                     (t.tokid_sb, t.tokid), (t.bqTs, t.bqT), (t.bkTs, t.bkT),
                     (t.bs1Ts, t.bs1T)):
        em.wait_ct("sy", "io")
        em.inc("io", sy.dma_start(out=dst[:], in_=src), 16)
    em.wait_ct("sy", "io")
    em.wait_ct("sy", "io")
    em.inc("io", sy.dma_start(out=t.gb_col[:], in_=t.gb), 16)
    em.wait_ct("sy", "io")
    em.inc("io", sy.dma_start(out=t.b1Ts[:], in_=t.b1T.rearrange("e p h -> p e h")), 16)
    em.mark("persistf32", "io")
    for dst, src in ((t.bs2_bf, t.bs2), (t.b2_bf, t.b2)):
        em.wait_ct("pl", "wl")
        em.inc("wl", pl.dma_start(out=dst[:], in_=src), 16)
    em.wait_ct("pl", "wl")
    em.inc("wl", pl.dma_start(out=t.gw_sb[:], in_=t.gw.rearrange("(fc p) e -> p fc e", p=128)), 16)
    em.mark("persistbf", "wl")
    em.wait("dve", "persistf32")
    em.inc("dve", dve.tensor_scalar_mul(t.bqTs[:], t.bqTs[:], SCL))
    em.mark("bqscaled", "dve")


def emit_phase_a(em, t, a):
    pe, act, dve, sy, pl = (Eng(em, n) for n in ENG)
    b4 = [a.ps_mm0, a.ps_mm1, a.ps_sA, a.ps_sB]
    scuse = {}
    trb = [Bank(em, "Atr0", a.ps_trA), Bank(em, "Atr1", a.ps_trB)]
    atb = [Bank(em, "Aat0", a.ps_at0), Bank(em, "Aat1", a.ps_at1)]
    mmi = [0]

    def bank_acquire(j):
        if j in scuse:
            em.wait("pe", scuse[j])

    def bank_release(j, key):
        scuse[j] = key

    em.wait_ct("pl", "wl")
    em.inc("wl", pl.dma_start(out=a.bv_bf[:], in_=t.bv), 16)
    em.wait_ct("pl", "wl")
    em.inc("wl", pl.dma_start(out=a.bo_bf[:], in_=t.bo), 16)
    for dst, src in ((a.wq_sb, t.wqT), (a.wk_sb, t.wkT), (a.wv_sb, t.wvT),
                     (a.wo_sb, t.woT)):
        em.wait_ct("pl", "wl")
        em.inc("wl", pl.dma_start(out=dst[:], in_=src.rearrange("(fi p) o -> p fi o", p=128)), 16)
    em.mark("attnw", "wl")

    x_ch = t.x.rearrange("(c p) f -> c p f", p=128)
    x2_ch = t.x2d.rearrange("(c p) f -> c p f", p=128)

    def ln_stats(c):
        cl4 = c % 4
        st = a.stat
        s0 = (c % 2) * 10
        if c >= 4:
            em.wait("sy", f"xn{c - 4}")
        em.wait_ct("sy", "io")
        em.inc("io", sy.dma_start(out=a.x_ln[:, cl4, :], in_=x_ch[c]), 16)
        em.mark(f"xl{c}", "io")
        xs = a.x_ln[:, cl4, :]
        em.wait("act", f"xl{c}")
        em.inc("act", act.activation(a.sq_scr[:], xs, AF.Square,
                                     accum_out=st[:, s0 + 3:s0 + 4]))
        em.mark(f"sq{c}", "act")
        em.wait("dve", f"xl{c}")
        if c >= 2:
            em.wait("dve", f"xn{c - 2}")
        em.inc("dve", dve.reduce_sum(st[:, s0 + 0:s0 + 1], xs, axis=AX.X))
        dve.drain()
        dve.tensor_scalar_mul(st[:, s0 + 2:s0 + 3], st[:, s0 + 0:s0 + 1], -1.0 / E)
        dve.drain()
        dve.tensor_mul(st[:, s0 + 4:s0 + 5], st[:, s0 + 2:s0 + 3], st[:, s0 + 2:s0 + 3])
        em.wait("dve", f"sq{c}")
        dve.drain()
        dve.scalar_tensor_tensor(st[:, s0 + 5:s0 + 6], st[:, s0 + 3:s0 + 4], 1.0 / E,
                                 st[:, s0 + 4:s0 + 5], op0=ALU.mult, op1=ALU.subtract)
        dve.drain()
        dve.tensor_scalar_add(st[:, s0 + 6:s0 + 7], st[:, s0 + 5:s0 + 6], EPS)
        dve.drain()
        em.inc("dve", dve.reciprocal(st[:, s0 + 1:s0 + 2], st[:, s0 + 6:s0 + 7]))
        em.mark(f"var{c}", "dve")
        em.wait("act", f"var{c}")
        em.wait("act", "persistbf")
        em.wait("act", "consts")
        act.activation(st[:, s0 + 7:s0 + 8], st[:, s0 + 1:s0 + 2], AF.Sqrt,
                       bias=0.0, scale=1.0)
        act.drain()
        act.activation(st[:, s0 + 8:s0 + 9], st[:, s0 + 2:s0 + 3], AF.Identity,
                       bias=0.0, scale=st[:, s0 + 7:s0 + 8])
        act.drain()
        if c >= 2:
            em.wait("act", f"xnused{c - 2}")
        em.inc("act", act.activation(a.xn_sb[:, c % 2, :], xs, AF.Identity,
                                     bias=st[:, s0 + 8:s0 + 9], scale=st[:, s0 + 7:s0 + 8]))
        em.mark(f"xn{c}", "act")

    def ln_tr(c):
        for fc in range(NF):
            b = trb[fc % 2]
            b.producer_wait()
            em.wait("pe", f"xn{c}")
            em.wait("pe", "persistf32")
            ins = pe.transpose(b.t[:, 0:128], a.xn_sb[:, c % 2, fc * 128:(fc + 1) * 128], t.ident_sb[:])
            b.fill(ins)
            b.consumer_wait("act")
            b.free(act.copy(a.ln1T[:, fc, (c % G) * 128:(c % G + 1) * 128], b.t[:, 0:128]), "act")
        em.mark(f"xnused{c}", "pe")
        em.mark(f"lnTc{c}", "act")

    for cl in range(G):
        ln_stats(cl)
        ln_tr(cl)

    for g in range(NG):
        em.wait("pe", "attnw")
        for which, wsb, outq in (("q", a.wq_sb, a.qT), ("k", a.wk_sb, a.kT)):
            for fo in range(NF):
                for hf in range(2):
                    j = mmi[0] % 2
                    mmi[0] += 1
                    bank_acquire(j)
                    em.wait("pe", f"lnTc{g * G + 4 * hf + 3}")
                    for fi in range(NF):
                        ins = pe.matmul(b4[j][:], wsb[:, fi, fo * 128:(fo + 1) * 128],
                                        a.ln1T[:, fi, hf * HF:(hf + 1) * HF],
                                        start=(fi == 0), stop=(fi == NF - 1))
                    em.inc("pe", ins)
                    em.mark(f"pj{g}_{which}{fo}_{hf}", "pe")
                    em.wait("act", f"pj{g}_{which}{fo}_{hf}")
                    if which == "q":
                        em.wait("act", "bqscaled")
                        ins = act.activation(outq[:, fo, hf * HF:(hf + 1) * HF], b4[j][:],
                                             AF.Identity, bias=t.bqTs[:, fo:fo + 1], scale=SCL)
                    else:
                        em.wait("act", "persistf32")
                        ins = act.activation(outq[:, fo, hf * HF:(hf + 1) * HF], b4[j][:],
                                             AF.Identity, bias=t.bkTs[:, fo:fo + 1], scale=1.0)
                    em.inc("act", ins)
                    em.mark(f"pjev{g}_{which}{fo}_{hf}", "act")
                    bank_release(j, f"pjev{g}_{which}{fo}_{hf}")
            em.mark(f"{which}T{g}", "act")

        for cl in range(G):
            c = g * G + cl
            for hf in range(2):
                j = mmi[0] % 2
                mmi[0] += 1
                bank_acquire(j)
                em.wait("pe", f"lnTc{c}")
                em.wait("pe", "attnw")
                em.wait("pe", "consts")
                for fi in range(NF):
                    pe.matmul(b4[j][:], a.ln1T[:, fi, cl * 128:(cl + 1) * 128],
                              a.wv_sb[:, fi, hf * HF:(hf + 1) * HF],
                              start=(fi == 0), stop=False)
                ins = pe.matmul(b4[j][:], t.ones_bf[0:1, 0:128], a.bv_bf[0:1, hf * HF:(hf + 1) * HF],
                                start=False, stop=True)
                em.inc("pe", ins)
                em.mark(f"vf{c}_{hf}", "pe")
                em.wait("act", f"vf{c}_{hf}")
                em.inc("act", act.copy(a.v_sb[:, cl, hf * HF:(hf + 1) * HF], b4[j][:]))
                em.mark(f"v{c}_{hf}", "act")
                bank_release(j, f"v{c}_{hf}")
            em.wait("dve", f"kT{g}")
            if c > 0:
                em.wait("dve", f"km{c - 1}")
            for fo in range(NF):
                ins = dve.reduce_sum(a.km_scr[:, fo:fo + 1], a.kT[:, fo, cl * 128:(cl + 1) * 128], axis=AX.X)
            em.inc("dve", ins)
            em.mark(f"kmraw{c}", "dve")
            em.wait("act", f"kmraw{c}")
            em.inc("act", act.activation(a.kmeansT[:, :, c:c + 1],
                                         a.km_scr[:].rearrange("p (e o) -> p e o", o=1),
                                         AF.Copy, bias=0.0, scale=1.0 / TC))
            em.mark(f"km{c}", "act")
            for hf in range(2):
                j = 2 + hf
                bank_acquire(j)
                em.wait("pe", f"v{c}_{hf}")
                em.wait("pe", "consts")
                ins = pe.matmul(b4[j][0:1, 0:HF], t.ones_col[:, 0:1],
                                a.v_sb[:, cl, hf * HF:(hf + 1) * HF], start=True, stop=True)
                em.inc("pe", ins)
                em.mark(f"vmf{c}_{hf}", "pe")
                em.wait("act", f"vmf{c}_{hf}")
                if c > 0:
                    em.wait("act", f"vmdma{c - 1}")
                em.inc("act", act.activation(a.vm_stage[0:1, hf * HF:(hf + 1) * HF], b4[j][0:1, 0:HF],
                                             AF.Copy, bias=0.0, scale=1.0 / TC))
                em.mark(f"vmc{c}_{hf}", "act")
                bank_release(j, f"vmc{c}_{hf}")
            em.mark(f"vmstage{c}", "act")
            em.wait("pl", f"vmstage{c}")
            em.wait_ct("pl", "wl")
            em.inc("wl", pl.dma_start(out=a.vmeans[c:c + 1, :], in_=a.vm_stage[0:1, :]), 16)
            em.mark(f"vmdma{c}", "wl")

        def chunk_pro(cl):
            c = g * G + cl
            if c >= 2:
                em.wait("sy", f"resid{c - 2}")
            em.wait_ct("sy", "io")
            em.inc("io", sy.dma_start(out=a.x_res[:, c % 2, :], in_=x_ch[c]), 16)
            em.mark(f"xresl{c}", "io")

        def head_scores(cl, h):
            c = g * G + cl
            nj = c + 1
            ghi = c * H + h
            hs = h % 16
            ji = h % 2
            jj = 2 + h % 2
            cp, hp = divmod(ghi - 16, H)
            qs = a.qT[(h % 2) * 64:(h % 2) * 64 + 64, h // 2, cl * 128:(cl + 1) * 128]
            bank_acquire(ji)
            em.wait("pe", f"qT{g}")
            em.wait("pe", f"kT{g}")
            ins = pe.matmul(b4[ji][:, 0:128], qs,
                            a.kT[(h % 2) * 64:(h % 2) * 64 + 64, h // 2, cl * 128:(cl + 1) * 128],
                            start=True, stop=True)
            em.inc("pe", ins)
            em.mark(f"sci{c}_{h}", "pe")
            bank_acquire(jj)
            em.wait("pe", f"km{c}")
            ins = pe.matmul(b4[jj][:, 0:nj], qs,
                            a.kmeansT[(h % 2) * 64:(h % 2) * 64 + 64, h // 2, 0:nj],
                            start=True, stop=True)
            em.inc("pe", ins)
            em.mark(f"scj{c}_{h}", "pe")
            em.wait("dve", f"sci{c}_{h}")
            em.wait("dve", f"scj{c}_{h}")
            if ghi - 16 in a.seen_exp:
                em.wait("dve", f"expi{cp}_{hp}")
                em.wait("dve", f"expj{cp}_{hp}")
            dve.reduce_max(a.negmax[:, hs:hs + 1], b4[ji][:, 0:128], axis=AX.X, negate=True)
            ins = dve.reduce_max(a.negmax[:, 16 + hs:17 + hs], b4[jj][:, 0:nj],
                                 axis=AX.X, negate=True)
            em.inc("dve", ins)
            em.mark(f"nm{c}_{h}", "dve")
            em.wait("act", f"nm{c}_{h}")
            if ghi - 16 in a.seen_an:
                em.wait("act", f"an{cp}_{hp}")
            em.inc("act", act.activation(a.ascr[:, hs, :], b4[ji][:, 0:128], AF.Exp,
                                         bias=a.negmax[:, hs:hs + 1], scale=1.0,
                                         accum_out=a.sums[:, hs:hs + 1]))
            em.mark(f"expi{c}_{h}", "act")
            bank_release(ji, f"expi{c}_{h}")
            ins = act.activation(a.aiscr[:, hs, 0:nj], b4[jj][:, 0:nj], AF.Exp,
                                 bias=a.negmax[:, 16 + hs:17 + hs], scale=1.0,
                                 accum_out=a.sums[:, 16 + hs:17 + hs])
            em.inc("act", ins)
            em.mark(f"expj{c}_{h}", "act")
            bank_release(jj, f"expj{c}_{h}")
            a.seen_exp.add(ghi)

        def head_norm(cl, h):
            c = g * G + cl
            nj = c + 1
            ghi = c * H + h
            hs = h % 16
            cp, hp = divmod(ghi - 16, H)
            em.wait("dve", f"expi{c}_{h}")
            em.wait("dve", f"expj{c}_{h}")
            if ghi - 16 in a.seen_an:
                em.wait("dve", f"an{cp}_{hp}")
            dve.reciprocal(a.recip[:, hs:hs + 1], a.sums[:, hs:hs + 1])
            ins = dve.reciprocal(a.recip[:, 16 + hs:17 + hs], a.sums[:, 16 + hs:17 + hs])
            em.inc("dve", ins)
            em.mark(f"recip{c}_{h}", "dve")
            if ghi - 16 in a.seen_tra:
                em.wait("dve", f"tra{cp}_{hp}")
            dve.drain()
            dve.tensor_scalar_mul(a.an_scr[:, hs, :], a.ascr[:, hs, :], a.recip[:, hs:hs + 1])
            ins = dve.tensor_scalar_mul(a.ain_scr[:, hs, 0:nj], a.aiscr[:, hs, 0:nj],
                                        a.recip[:, 16 + hs:17 + hs])
            em.inc("dve", ins)
            em.mark(f"an{c}_{h}", "dve")
            a.seen_an.add(ghi)

        def head_tr(cl, h):
            c = g * G + cl
            nj = c + 1
            ghi = c * H + h
            hs = h % 16
            cp, hp = divmod(ghi - 16, H)
            bA = trb[0]
            bA.producer_wait()
            em.wait("pe", f"an{c}_{h}")
            ins = pe.transpose(bA.t[:, 0:128], a.an_scr[:, hs, :], t.ident_sb[:])
            bA.fill(ins)
            bB = trb[1]
            bB.producer_wait()
            ins = pe.transpose(bB.t[0:nj, 0:128], a.ain_scr[:, hs, 0:nj], t.ident_sb[:])
            bB.fill(ins)
            em.mark(f"tra{c}_{h}", "pe")
            a.seen_tra.add(ghi)
            bA.consumer_wait("act")
            if ghi - 16 in a.seen_atm:
                em.wait("act", f"atmm{cp}_{hp}")
            bA.free(act.copy(a.aT_sb[:, hs, :], bA.t[:, 0:128]), "act")
            em.mark(f"aT{c}_{h}", "act")
            bB.consumer_wait("dve")
            if ghi - 16 in a.seen_atm:
                em.wait("dve", f"atmm{cp}_{hp}")
            bB.free(dve.tensor_copy(a.aiT_sb[0:nj, hs, :], bB.t[0:nj, 0:128]), "dve")
            em.mark(f"aiT{c}_{h}", "dve")

        def head_mm(cl, h):
            c = g * G + cl
            nj = c + 1
            ghi = c * H + h
            hs = h % 16
            pair = h // 2
            bk = atb[(ghi // 2) % 2]
            if h % 2 == 0:
                bk.producer_wait()
            em.wait("pe", f"aT{c}_{h}")
            pe.matmul(bk.t[(h % 2) * 64:(h % 2) * 64 + 64, 0:128],
                      a.v_sb[:, cl, h * 64:(h + 1) * 64],
                      a.aT_sb[:, hs, :], start=True, stop=False)
            em.wait("pe", f"aiT{c}_{h}")
            em.wait("pe", f"vmdma{c}")
            ins = pe.matmul(bk.t[(h % 2) * 64:(h % 2) * 64 + 64, 0:128],
                            a.vmeans[0:nj, h * 64:(h + 1) * 64],
                            a.aiT_sb[0:nj, hs, :], start=False, stop=True)
            if h % 2 == 1:
                bk.fill(ins)
            else:
                em.inc("pe", ins)
            em.mark(f"atmm{c}_{h}", "pe")
            a.seen_atm.add(ghi)
            if h % 2 == 1:
                bk.consumer_wait("act")
                if c > 0:
                    em.wait("act", f"oproj{c - 1}")
                bk.free(act.copy(a.attnT[:, pair, :], bk.t[:, 0:128]), "act")
                em.mark(f"atevac{c}_{pair}", "act")

        def chunk_epi(cl):
            c = g * G + cl
            for hf in range(2):
                j = mmi[0] % 2
                mmi[0] += 1
                bank_acquire(j)
                if B_HEADS and B_TAILS:
                    em.wait("pe", f"atevac{c}_{H // 2 - 1}")
                for fc in range(NF):
                    pe.matmul(b4[j][:], a.attnT[:, fc, :],
                              a.wo_sb[:, fc, hf * HF:(hf + 1) * HF],
                              start=(fc == 0), stop=False)
                ins = pe.matmul(b4[j][:], t.ones_bf[0:1, 0:128], a.bo_bf[0:1, hf * HF:(hf + 1) * HF],
                                start=False, stop=True)
                em.inc("pe", ins)
                em.mark(f"opf{c}_{hf}", "pe")
                em.wait("dve", f"opf{c}_{hf}")
                em.wait("dve", f"xresl{c}")
                if c >= 2:
                    em.wait("dve", f"x2w{c - 2}")
                ins = dve.tensor_add(a.xo_sb[:, cl % 2, hf * HF:(hf + 1) * HF], b4[j][:],
                                     a.x_res[:, c % 2, hf * HF:(hf + 1) * HF])
                em.inc("dve", ins)
                em.mark(f"opr{c}_{hf}", "dve")
                bank_release(j, f"opr{c}_{hf}")
            em.mark(f"oproj{c}", "pe")
            em.mark(f"resid{c}", "dve")
            em.wait("sy", f"resid{c}")
            em.wait_ct("sy", "io")
            em.inc("io", sy.dma_start(out=x2_ch[c], in_=a.xo_sb[:, cl % 2, :]), 16)
            em.mark(f"x2w{c}", "io")
            if g + 1 < NG:
                if cl < G - 1:
                    ln_stats((g + 1) * G + cl + 1)
                ln_tr((g + 1) * G + cl)

        LAG = 12
        NQ = G * H
        if g + 1 < NG:
            ln_stats((g + 1) * G)
        if B_HEADS:
            for idx in range(NQ + LAG + 1):
                if idx < NQ:
                    cl, h = divmod(idx, H)
                    if h == 0:
                        chunk_pro(cl)
                    head_scores(cl, h)
                    head_norm(cl, h)
                if B_TAILS:
                    i1 = idx - LAG
                    if 0 <= i1 < NQ:
                        head_tr(*divmod(i1, H))
                    i2 = idx - LAG - 1
                    if 0 <= i2 < NQ:
                        cl3, h3 = divmod(i2, H)
                        head_mm(cl3, h3)
                        if h3 == H - 1:
                            chunk_epi(cl3)
            if not B_TAILS:
                for cl in range(G):
                    chunk_epi(cl)
        else:
            for cl in range(G):
                chunk_pro(cl)
                chunk_epi(cl)


def emit_phase_b(em, t, b):
    pe, act, dve, sy, pl = (Eng(em, n) for n in ENG)
    ph = [Bank(em, "Bh0", b.ph0), Bank(em, "Bh1", b.ph1)]
    po = [Bank(em, "Bo0", b.po0), Bank(em, "Bo1", b.po1)]

    x2_tile = t.x2d.rearrange("(ti ts p) f -> ti p ts f", ts=TS, p=128)
    out_tile = t.out.rearrange("(ti ts p) f -> ti p ts f", ts=TS, p=128)

    for k in ("endA_pe", "endA_act", "endA_dve", "endA_io", "endA_wl", "endA_plc"):
        for e in ENG:
            em.wait(e, k)

    prev_w2used = [None]
    w1_slot_used = [None, None, None]
    if SPARSE:
        b.bnd_s = pl.alloc_register("bnd_s")
        pl.reg_mov(b.bnd_s, S - 1)
        b.bnd_i = pl.alloc_register("bnd_i")
        pl.reg_mov(b.bnd_i, CAP * NE - 1)
        em.inc("plc", pl.memset(b.idx_fill[:], 65535), 1)
        em.inc("plc", pl.memset(b.out_g[:], 0.0), 1)
        em.mark("idxfill", "plc")
        for k in ("endA_pe", "endA_act", "endA_dve", "endA_io", "endA_wl", "endA_plc"):
            em.wait("sy", k)
        em.wait("sy", "idxfill")
        for c in range(C):
            em.inc("io", sy.dma_start(out=b.ln2d_r[:, c, :], in_=b.out_g[:]), 16)
        em.inc("io", sy.dma_start(out=t.combd.rearrange("(c p) e -> p c e", p=128),
                                  in_=b.out_g[:, 0:C * NE].rearrange("p (c e) -> p c e", e=NE)), 16)
        em.wait_ct("sy", "io")
        em.mark("zinit", "io")

    for ti in range(NTILE):
        for ts in range(TS):
            em.wait("sy", f"x2w{ti * G + ts}")
            if ti > 0:
                em.wait("sy", f"outdma{ti - 1}_{ts}")
            em.wait_ct("sy", "io")
            em.inc("io", sy.dma_start(out=b.x2_sb[:, ts, :], in_=x2_tile[ti][:, ts, :]), 16)
            em.mark(f"x2l{ti}_{ts}", "io")

        for ts in range(TS):
            st = b.stat2
            s0 = (ts % 2) * 10
            sl = ts % 2
            em.wait("dve", f"x2l{ti}_{ts}")
            kk = f"bxn{ti}_{ts - 2}" if ts >= 2 else f"bxn{ti - 1}_{ts + TS - 2}"
            if kk in em.ms:
                em.wait("dve", kk)
            em.wait("act", f"x2l{ti}_{ts}")
            if f"xn2free{sl}" in em.ms:
                em.wait("act", f"xn2free{sl}")
            if SPARSE:
                kk2 = f"ln2d{ti}_{ts - 1}" if ts >= 1 else f"ln2d{ti - 1}_{TS - 1}"
                if kk2 in em.ms:
                    em.wait("act", kk2)
            if sl == 0 and ti > 0 and B_GATE:
                em.wait("act", f"ltmtr{ti - 1}")
            em.inc("act", act.activation(b.xn2[:, sl, :], b.x2_sb[:, ts, :], AF.Square,
                                         accum_out=st[:, s0 + 3:s0 + 4]))
            em.mark(f"bsq{ti}_{ts}", "act")
            em.inc("dve", dve.reduce_sum(st[:, s0 + 0:s0 + 1], b.x2_sb[:, ts, :], axis=AX.X))
            dve.drain()
            dve.tensor_scalar_mul(st[:, s0 + 2:s0 + 3], st[:, s0 + 0:s0 + 1], -1.0 / E)
            dve.drain()
            dve.tensor_mul(st[:, s0 + 4:s0 + 5], st[:, s0 + 2:s0 + 3], st[:, s0 + 2:s0 + 3])
            em.wait("dve", f"bsq{ti}_{ts}")
            dve.drain()
            dve.scalar_tensor_tensor(st[:, s0 + 5:s0 + 6], st[:, s0 + 3:s0 + 4], 1.0 / E,
                                     st[:, s0 + 4:s0 + 5], op0=ALU.mult, op1=ALU.subtract)
            dve.drain()
            dve.tensor_scalar_add(st[:, s0 + 6:s0 + 7], st[:, s0 + 5:s0 + 6], EPS)
            dve.drain()
            em.inc("dve", dve.reciprocal(st[:, s0 + 1:s0 + 2], st[:, s0 + 6:s0 + 7]))
            em.mark(f"bvar{ti}_{ts}", "dve")
            em.wait("act", f"bvar{ti}_{ts}")
            em.wait("act", "persistbf")
            em.wait("act", "consts")
            act.activation(st[:, s0 + 7:s0 + 8], st[:, s0 + 1:s0 + 2], AF.Sqrt,
                           bias=0.0, scale=1.0)
            act.drain()
            act.activation(st[:, s0 + 8:s0 + 9], st[:, s0 + 2:s0 + 3], AF.Identity,
                           bias=0.0, scale=st[:, s0 + 7:s0 + 8])
            act.drain()
            em.inc("act", act.activation(b.xn2[:, sl, :], b.x2_sb[:, ts, :], AF.Identity,
                                         bias=st[:, s0 + 8:s0 + 9], scale=st[:, s0 + 7:s0 + 8]))
            em.mark(f"bxn{ti}_{ts}", "act")
            if SPARSE:
                em.wait("sy", f"bxn{ti}_{ts}")
                em.wait_ct("sy", "io")
                em.inc("io", sy.dma_start(out=b.ln2d_r[:, ti * TS + ts, :], in_=b.xn2[:, sl, :]), 16)
                em.mark(f"ln2d{ti}_{ts}", "io")
            for fc in range(NF):
                bk = ph[fc % 2]
                bk.producer_wait()
                em.wait("pe", f"bxn{ti}_{ts}")
                em.wait("pe", "identf")
                if ti > 0 and B_EXPERTS:
                    em.wait("pe", f"lnT2used{ti - 1}")
                ins = pe.transpose(bk.t[:, 0:128], b.xn2[:, sl, fc * 128:(fc + 1) * 128], b.identf[:])
                bk.fill(ins)
                bk.consumer_wait("act")
                bk.free(act.copy(b.ln2T[:, fc, ts * 128:(ts + 1) * 128], bk.t[:, 0:128]), "act")
            em.mark(f"xn2free{sl}", "pe")
        em.mark(f"lnT2{ti}", "act")

        if B_GATE:
            bk = ph[0]
            bk.producer_wait()
            em.wait("pe", f"lnT2{ti}")
            em.wait("pe", "persistbf")
            em.wait("pe", "consts")
            for hf in range(2):
                for fc in range(NF):
                    ins = pe.matmul(bk.t[0:NE, hf * HF:(hf + 1) * HF], b.gw_sb2[:, fc, :],
                                    b.ln2T[:, fc, hf * HF:(hf + 1) * HF], start=(fc == 0),
                                    stop=(fc == NF - 1))
            bk.fill(ins)
            bk.consumer_wait("act")
            em.wait("act", "xn2free0")
            if SPARSE:
                em.wait("act", f"ln2d{ti}_{TS - 1}")
            em.wait("act", "persistf32")
            bk.free(act.activation(b.xn2[0:NE, 0, :], bk.t[0:NE, :], AF.Identity,
                                   bias=t.gb_col[0:NE, 0:1], scale=1.0), "act")
            em.mark(f"logitsT{ti}", "act")

            for ts in range(TS):
                bk = ph[1 - ts % 2]
                bk.producer_wait()
                em.wait("pe", f"logitsT{ti}")
                ins = pe.transpose(bk.t[0:128, 0:NE], b.xn2[0:NE, 0, ts * 128:(ts + 1) * 128],
                                   b.identf[0:NE, 0:NE])
                bk.fill(ins)
                bk.consumer_wait("act")
                if ti > 0:
                    em.wait("act", f"combine{ti - 1}_{TS - 1}")
                bk.free(act.copy(b.logits_tm[:, ts, :], bk.t[0:128, 0:NE]), "act")
                em.mark(f"ltm{ti}_{ts}", "act")
            em.mark(f"ltmtr{ti}", "pe")

            for ts in range(TS):
                gg = b.gg
                g0 = (ts % 2) * 10
                lg = b.logits_tm[:, ts, :]
                em.wait("dve", f"ltm{ti}_{ts}")
                dve.reduce_max(gg[:, g0 + 0:g0 + 1], lg, axis=AX.X)
                dve.drain()
                dve.tensor_single_scalar(b.maskA[:], lg, gg[:, g0 + 0:g0 + 1], op=ALU.is_ge)
                dve.drain()
                dve.scalar_tensor_tensor(b.xm[:], b.maskA[:], -BIG, lg, op0=ALU.mult, op1=ALU.add)
                dve.drain()
                dve.reduce_max(gg[:, g0 + 1:g0 + 2], b.xm[:], axis=AX.X)
                ins = dve.tensor_scalar_mul(gg[:, g0 + 2:g0 + 3], gg[:, g0 + 0:g0 + 1], -1.0)
                em.inc("dve", ins)
                em.mark(f"gd1_{ti}_{ts}", "dve")
                em.wait("act", f"gd1_{ti}_{ts}")
                em.inc("act", act.activation(gg[:, g0 + 3:g0 + 4], gg[:, g0 + 1:g0 + 2], AF.Exp,
                                             bias=gg[:, g0 + 2:g0 + 3], scale=1.0))
                em.mark(f"gexp{ti}_{ts}", "act")
                em.wait("dve", f"gexp{ti}_{ts}")
                dve.tensor_scalar_add(gg[:, g0 + 4:g0 + 5], gg[:, g0 + 3:g0 + 4], 1.0)
                dve.drain()
                dve.reciprocal(gg[:, g0 + 5:g0 + 6], gg[:, g0 + 4:g0 + 5])
                dve.drain()
                dve.tensor_mul(gg[:, g0 + 6:g0 + 7], gg[:, g0 + 3:g0 + 4], gg[:, g0 + 5:g0 + 6])
                dve.tensor_single_scalar(b.maskAB[:], lg, gg[:, g0 + 1:g0 + 2], op=ALU.is_ge)
                dve.drain()
                dve.tensor_sub(gg[:, g0 + 7:g0 + 8], gg[:, g0 + 5:g0 + 6], gg[:, g0 + 6:g0 + 7])
                dve.drain()
                dve.tensor_scalar_mul(b.xm[:], b.maskA[:], gg[:, g0 + 7:g0 + 8])
                dve.drain()
                em.inc("dve", dve.scalar_tensor_tensor(b.combine[:, ts, :], b.maskAB[:], gg[:, g0 + 6:g0 + 7],
                                                       b.xm[:], op0=ALU.mult, op1=ALU.add))
                em.mark(f"combine{ti}_{ts}", "dve")
                if SPARSE:
                    dve.drain()
                    dve.tensor_single_scalar(b.mask_all[:, ts, :], b.combine[:, ts, :], 0.0,
                                             op=ALU.is_gt)
                    ins = dve.tensor_copy(b.maskA_all[:, ts, :], b.maskA[:])
                    em.inc("dve", ins)
                    em.mark(f"maskp{ti}_{ts}", "dve")
                    em.wait("pl", f"combine{ti}_{ts}")
                    em.wait("pl", "zinit")
                    em.wait_ct("pl", "wl")
                    em.inc("wl", pl.dma_start(out=b.combd_r[:, ti * TS + ts, :], in_=b.combine[:, ts, :]), 16)
                bk = ph[1 - ts % 2]
                bk.producer_wait()
                em.wait("pe", f"combine{ti}_{ts}")
                ins = pe.transpose(bk.t[0:NE, 0:128], b.combine[:, ts, :], b.identf[:])
                bk.fill(ins)
                bk.consumer_wait("act")
                if ti > 0 and B_EXPERTS and B_W2:
                    em.wait("act", f"w2used{(ti - 1) * 9 + (0 if SPARSE else 8)}")
                bk.free(act.copy(b.combineT[0:NE, ts, :], bk.t[0:NE, 0:128]), "act")
                em.mark(f"combT{ti}_{ts}", "act")

            if SPARSE:
                em.wait("act", f"maskp{ti}_{TS - 1}")
                em.inc("act", act.copy(b.mask_bf[:], b.mask_all[:].rearrange("p ts e -> p (ts e)")))
                em.mark(f"maskbf{ti}", "act")
                bk = ph[0]
                bk.producer_wait()
                em.wait("pe", f"maskbf{ti}")
                em.wait("pe", "persistf32")
                ins = pe.matmul(bk.t[:, 0:TS * NE], t.ltri_sb[:], b.mask_bf[:], start=True, stop=True)
                bk.fill(ins)
                bk.consumer_wait("act")
                bk.free(act.copy(b.rankw[:], bk.t[:, 0:TS * NE]), "act")
                em.mark(f"rankw{ti}", "act")
                bk = ph[1]
                bk.producer_wait()
                ins = pe.matmul(bk.t[0:1, 0:TS * NE], t.ones_col[:, 0:1], b.mask_bf[:],
                                start=True, stop=True)
                bk.fill(ins)
                bk.consumer_wait("act")
                bk.free(act.copy(b.cnt[0:1, :], bk.t[0:1, 0:TS * NE]), "act")
                em.mark(f"cnt{ti}", "act")
                em.wait("dve", f"cnt{ti}")
                dve.memset(b.base[0:1, 0:NE], 0.0)
                for ts in range(1, TS):
                    dve.drain()
                    dve.tensor_add(b.base[0:1, ts * NE:(ts + 1) * NE],
                                   b.base[0:1, (ts - 1) * NE:ts * NE],
                                   b.cnt[0:1, (ts - 1) * NE:ts * NE])
                ins = dve.drain()
                em.inc("dve", dve.engine_nop() if em.active == "dve" and not getattr(em, "sched", False) else _DUM)
                em.mark(f"base{ti}", "dve")
                bk = po[0]
                bk.producer_wait()
                em.wait("pe", f"base{ti}")
                em.wait("pe", "consts")
                ins = pe.matmul(bk.t[:, 0:TS * NE], t.ones_f[0:1, :], b.base[0:1, :],
                                start=True, stop=True)
                bk.fill(ins)
                bk.consumer_wait("dve")
                em.wait("dve", f"rankw{ti}")
                bk.free(dve.tensor_add(b.rankw[:], b.rankw[:], bk.t[:, 0:TS * NE]), "dve")
                em.mark(f"rankg{ti}", "dve")
                em.wait("pl", "idxfill")
                em.wait_ct("pl", "wl")
                em.inc("wl", pl.dma_start(out=t.idxd.rearrange("(s p e) one -> p s (e one)", p=128, e=NE),
                                          in_=b.idx_fill[:]), 16)
                for ts in range(TS):
                    sp = (ts % 2)
                    em.wait("dve", f"rankg{ti}")
                    if ts >= 2:
                        em.wait("dve", f"rsc{ti}_{ts - 2}")
                    mA = b.maskA_all[:, ts, :]
                    rk = b.rankw[:, ts * NE:(ts + 1) * NE]
                    dve.tensor_sub(b.maskAB[:], b.mask_all[:, ts, :], mA)
                    dve.tensor_mul(b.xm[:], mA, t.evec_sb[:])
                    dve.drain()
                    dve.reduce_sum(b.maskA[:, 0:1], b.xm[:], axis=AX.X)
                    dve.drain()
                    dve.tensor_mul(b.xm[:], b.maskAB[:], t.evec_sb[:])
                    dve.drain()
                    dve.reduce_sum(b.maskA[:, 1:2], b.xm[:], axis=AX.X)
                    dve.drain()
                    dve.tensor_mul(b.xm[:], mA, rk)
                    dve.drain()
                    dve.reduce_sum(b.maskA[:, 2:3], b.xm[:], axis=AX.X)
                    dve.drain()
                    dve.tensor_mul(b.xm[:], b.maskAB[:], rk)
                    dve.drain()
                    dve.reduce_sum(b.maskA[:, 3:4], b.xm[:], axis=AX.X)
                    dve.reduce_sum(b.maskA[:, 4:5], b.maskAB[:], axis=AX.X)
                    dve.drain()
                    dve.scalar_tensor_tensor(b.maskA[:, 5:6], b.maskA[:, 2:3], float(NE), b.maskA[:, 0:1],
                                             op0=ALU.mult, op1=ALU.add)
                    dve.scalar_tensor_tensor(b.maskA[:, 6:7], b.maskA[:, 3:4], float(NE), b.maskA[:, 1:2],
                                             op0=ALU.mult, op1=ALU.add)
                    dve.drain()
                    dve.scalar_tensor_tensor(b.maskA[:, 7:8], b.maskA[:, 4:5], -8192.0, b.maskA[:, 6:7],
                                             op0=ALU.mult, op1=ALU.add)
                    dve.drain()
                    dve.tensor_scalar_add(b.maskA[:, 7:8], b.maskA[:, 7:8], 8192.0)
                    dve.drain()
                    dve.tensor_copy(b.roff[:, sp, 0:1], b.maskA[:, 5:6])
                    ins = dve.tensor_copy(b.roff[:, sp, 1:2], b.maskA[:, 7:8])
                    em.inc("dve", ins)
                    em.mark(f"roff{ti}_{ts}", "dve")
                    em.wait("pl", f"roff{ti}_{ts}")
                    em.wait_ct("pl", "wl")
                    em.inc("wl", pl.indirect_dma_start(
                        out=t.idxd, out_offset=bass.IndirectOffsetOnAxis(b.roff[:, sp, 0:1], axis=0),
                        in_=t.tokid_sb[:, ti * TS + ts:ti * TS + ts + 1], in_offset=None,
                        bounds_check=b.bnd_i, oob_is_err=False), 16)
                    em.wait_ct("pl", "wl")
                    em.inc("wl", pl.indirect_dma_start(
                        out=t.idxd, out_offset=bass.IndirectOffsetOnAxis(b.roff[:, sp, 1:2], axis=0),
                        in_=t.tokid_sb[:, ti * TS + ts:ti * TS + ts + 1], in_offset=None,
                        bounds_check=b.bnd_i, oob_is_err=False), 16)
                    em.mark(f"rsc{ti}_{ts}", "wl")
                em.wait_ct("pl", "wl")
                em.inc("wl", pl.dma_start(
                    out=b.idxg[:],
                    in_=t.idxd.rearrange("(s p e) one -> p s (e one)", p=128, e=NE)), 16)
                em.mark(f"idxg{ti}", "wl")

        prev_lnTg = [None]
        prev_flushg = [None]
        last_osc = [None]

        def emit_stores():
            for ts in range(TS):
                if B_EXPERTS and B_W2:
                    se = 0 if SPARSE else 8
                    em.wait("sy", f"flush{ti * 9 + se}_{ts}")
                elif B_GATE:
                    em.wait("sy", f"combT{ti}_{ts}")
                else:
                    em.wait("sy", f"lnT2{ti}")
                em.wait_ct("sy", "io")
                em.inc("io", sy.dma_start(out=out_tile[ti][:, ts, :], in_=b.x2_sb[:, ts, :]), 16)
                em.mark(f"outdma{ti}_{ts}", "io")

        def emit_expert(pos, e):
            ei = ti * 9 + pos
            dense = (not SPARSE) or e == 8
            nsub = TS if dense else NSUB
            ncol = TT if dense else CAP
            w1src = t.w1[e] if e < NE else t.ws1
            w2src = t.w2[e] if e < NE else t.ws2

            def w1_dma(sbi):
                slot = sbi % 3
                if w1_slot_used[slot] is not None:
                    em.wait("pl", w1_slot_used[slot])
                em.inc(f"w1s{slot}", pl.dma_start(
                    out=b.w1b[:, slot, :, :],
                    in_=w1src[:, sbi * SLAB * 128:(sbi + 1) * SLAB * 128].rearrange(
                        "(fc p) h -> p fc h", p=128)), 16)
                em.mark(f"w1ld{ei}_{sbi}", f"w1s{slot}")
                w1_slot_used[slot] = f"w1used{ei}_{sbi}"

            def w2_dma(hb):
                if hb == 0 and prev_w2used[0] is not None:
                    em.wait("pl", prev_w2used[0])
                em.wait_ct("pl", "wl")
                em.inc("wl", pl.dma_start(
                    out=b.w2b[:, hb * 4:(hb + 1) * 4, :],
                    in_=w2src[hb * 512:(hb + 1) * 512, :].rearrange("(hc p) f -> p hc f", p=128)), 16)
                em.mark(f"w2ld{ei}_{hb}", "wl")

            if not dense:
                em.wait("pl", f"idxg{ti}")
                if prev_lnTg[0] is not None:
                    em.wait("pl", prev_lnTg[0])
                if prev_flushg[0] is not None:
                    em.wait("pl", prev_flushg[0])
                for sub in range(NSUB):
                    if f"lnTgp{ei}_{sub - 1}" in em.ms:
                        em.wait("pl", f"lnTgp{ei}_{sub - 1}")
                    em.wait_ct("pl", "wl")
                    em.inc("wl", pl.indirect_dma_start(
                        out=b.ln2g[:, 0, :], out_offset=None,
                        in_=t.ln2d, in_offset=bass.IndirectOffsetOnAxis(b.idxg[:, sub, e:e + 1], axis=0),
                        bounds_check=b.bnd_s, oob_is_err=False), 16)
                    em.wait_ct("pl", "wl")
                    em.inc("wl", pl.indirect_dma_start(
                        out=b.cbg[:, sub, :], out_offset=None,
                        in_=t.combd, in_offset=bass.IndirectOffsetOnAxis(b.idxg[:, sub, e:e + 1], axis=0),
                        bounds_check=b.bnd_s, oob_is_err=False), 16)
                    em.mark(f"gat{ei}_{sub}", "wl")
                    for fc in range(NF):
                        bk = ph[fc % 2]
                        bk.producer_wait()
                        em.wait("pe", f"gat{ei}_{sub}")
                        ins = pe.transpose(bk.t[:, 0:128], b.ln2g[:, 0, fc * 128:(fc + 1) * 128],
                                           b.identf[:])
                        bk.fill(ins)
                        bk.consumer_wait("act")
                        bk.free(act.copy(b.ln2Tg[:, fc, sub * 128:(sub + 1) * 128], bk.t[:, 0:128]), "act")
                    em.mark(f"lnTgp{ei}_{sub}", "pe")
                em.mark(f"gat{ei}", "wl")
                em.mark(f"lnTg{ei}", "act")
                prev_lnTg[0] = f"lnTgp{ei}_{NSUB - 1}"

            w1_dma(0)
            w1_dma(1)
            w1_dma(2)

            for hc in range(NHC):
                sbi = hc // SLAB
                bk = ph[hc % 2]
                em.wait("pe", f"w1ld{ei}_{sbi}")
                bk.producer_wait()
                if dense:
                    for hf in range(2):
                        for fc in range(NF):
                            ins = pe.matmul(bk.t[:, hf * HF:(hf + 1) * HF],
                                            b.w1b[:, sbi % 3, fc, (hc % SLAB) * 128:(hc % SLAB + 1) * 128],
                                            b.ln2T[:, fc, hf * HF:(hf + 1) * HF],
                                            start=(fc == 0), stop=(fc == NF - 1))
                else:
                    em.wait("pe", f"lnTg{ei}")
                    for fc in range(NF):
                        ins = pe.matmul(bk.t[:, 0:CAP],
                                        b.w1b[:, sbi % 3, fc, (hc % SLAB) * 128:(hc % SLAB + 1) * 128],
                                        b.ln2Tg[:, fc, 0:CAP],
                                        start=(fc == 0), stop=(fc == NF - 1))
                bk.fill(ins)
                if hc == NHC - 1 and e == 8:
                    em.mark(f"lnT2used{ti}", "pe")
                bk.consumer_wait("act")
                em.wait("act", "persistf32")
                bias = t.b1Ts[:, e, hc:hc + 1] if e < NE else t.bs1Ts[:, hc:hc + 1]
                bk.free(act.activation(b.h_sb[:, hc, 0:ncol], bk.t[:, 0:ncol],
                                       AF.Gelu, bias=bias, scale=1.0), "act")
                em.mark(f"h{ei}_{hc}", "act")
                if hc % SLAB == SLAB - 1:
                    em.mark(f"w1used{ei}_{sbi}", "pe")
                    if sbi + 3 < NSLAB:
                        w1_dma(sbi + 3)
                    if sbi % 2 == 0 and B_W2:
                        w2_dma(sbi // 2)
            prev_w2used[0] = f"w2used{ei}" if B_W2 else prev_w2used[0]

            for ts in range(nsub if B_W2 else 0):
                bk = po[ts % 2]
                bk.producer_wait()
                for hf in range(2):
                    for hc in range(NHC):
                        em.wait("pe", f"h{ei}_{hc}")
                        em.wait("pe", f"w2ld{ei}_{hc // 4}")
                        ins = pe.matmul(bk.t[:, hf * HF:(hf + 1) * HF],
                                        b.h_sb[:, hc, ts * 128:(ts + 1) * 128],
                                        b.w2b[:, hc, hf * HF:(hf + 1) * HF],
                                        start=(hc == 0), stop=(hc == NHC - 1 and e != 8))
                    if e == 8:
                        em.wait("pe", f"combT{ti}_{ts}")
                        pe.matmul(bk.t[:, hf * HF:(hf + 1) * HF], b.combineT[0:NE, ts, :],
                                  t.b2_bf[0:NE, hf * HF:(hf + 1) * HF], start=False, stop=False)
                        ins = pe.matmul(bk.t[:, hf * HF:(hf + 1) * HF], t.ones_bf[0:1, 0:128],
                                        t.bs2_bf[0:1, hf * HF:(hf + 1) * HF], start=False, stop=True)
                bk.fill(ins)
                if dense:
                    bk.consumer_wait("dve")
                    if SPARSE:
                        pass
                    else:
                        em.wait("dve", f"combine{ti}_{ts}")
                    if b.lastflush[ts] is not None:
                        em.wait("dve", b.lastflush[ts])
                    x2s = b.x2_sb[:, ts, :]
                    if e < NE:
                        em.wait("dve", f"combine{ti}_{ts}")
                        ins = dve.scalar_tensor_tensor(x2s, bk.t[:], b.combine[:, ts, e:e + 1],
                                                       x2s, op0=ALU.mult, op1=ALU.add)
                    else:
                        ins = dve.tensor_add(x2s, bk.t[:], x2s)
                    bk.free(ins, "dve")
                    em.mark(f"flush{ei}_{ts}", "dve")
                    b.lastflush[ts] = f"flush{ei}_{ts}"
                else:
                    bk.consumer_wait("dve")
                    em.wait("dve", f"gat{ei}")
                    if last_osc[0] is not None:
                        em.wait("dve", last_osc[0])
                    ins = dve.tensor_scalar_mul(b.out_g[:], bk.t[:], b.cbg[:, ts, e:e + 1])
                    bk.free(ins, "dve")
                    em.mark(f"flushg{ei}_{ts}", "dve")
                    em.wait("pl", f"flushg{ei}_{ts}")
                    em.wait("pl", f"outdma{ti}_{TS - 1}")
                    em.wait_ct("pl", "wl")
                    em.inc("wl", pl.indirect_dma_start(
                        out=t.out,
                        out_offset=bass.IndirectOffsetOnAxis(b.idxg[:, ts, e:e + 1], axis=0),
                        in_=b.out_g[:], in_offset=None,
                        bounds_check=b.bnd_s, oob_is_err=False,
                        compute_op=ALU.add), 16)
                    em.mark(f"osc{ei}_{ts}", "wl")
                    last_osc[0] = f"osc{ei}_{ts}"
                    prev_flushg[0] = f"flushg{ei}_{ts}"
            if B_W2:
                em.mark(f"w2used{ei}", "pe")

        order = ([8] + list(range(NE))) if SPARSE else list(range(9))
        for pos, e in enumerate(order if B_EXPERTS else []):
            emit_expert(pos, e)
            if B_EXPERTS and SPARSE and pos == 0:
                emit_stores()
        if not (B_EXPERTS and SPARSE):
            emit_stores()
        b.lastflush = [None for _ in range(TS)]


def build_nc():
    nc = bass.Bass("TRN2", target_bir_lowering=False)
    t = declare_dram(nc)
    with contextlib.ExitStack() as stack:
        en = stack.enter_context
        t.ones_bf = en(nc.sbuf_tensor("ones_bf", [1, 128], BF16))
        t.ones_col = en(nc.sbuf_tensor("ones_col", [128, 1], BF16))
        t.eps_col = en(nc.sbuf_tensor("eps_col", [128, 1], F32))
        t.ident_sb = en(nc.sbuf_tensor("ident_sb", [128, 128], BF16))
        t.ltri_sb = en(nc.sbuf_tensor("ltri_sb", [128, 128], BF16))
        t.evec_sb = en(nc.sbuf_tensor("evec_sb", [128, NE], F32))
        t.tokid_sb = en(nc.sbuf_tensor("tokid_sb", [128, C], mybir.dt.int32))
        t.ones_f = en(nc.sbuf_tensor("ones_f", [1, 128], F32))
        t.bqTs = en(nc.sbuf_tensor("bqTs", [128, NF], F32))
        t.bkTs = en(nc.sbuf_tensor("bkTs", [128, NF], F32))
        t.b1Ts = en(nc.sbuf_tensor("b1Ts", [128, NE, NHC], F32))
        t.bs1Ts = en(nc.sbuf_tensor("bs1Ts", [128, NHC], F32))
        t.bs2_bf = en(nc.sbuf_tensor("bs2_bf", [1, E], BF16))
        t.b2_bf = en(nc.sbuf_tensor("b2_bf", [NE, E], BF16))
        t.gb_col = en(nc.sbuf_tensor("gb_col", [NE, 1], F32))
        t.gw_sb = en(nc.sbuf_tensor("gw_sb", [128, NF, NE], BF16))

        sems = {k: en(nc.semaphore(f"s_{k}")) for k in SEMS}
        block = en(nc.Block())
        ems = {name: Em(name, sems) for name in ENG}
        emx = Em("__sched__", sems)
        emx.sched = True
        emx.prog = {e: [] for e in ENG}
        emx._last_op = None

        def run(name, fn, *args):
            em = ems[name]
            def body(engobj):
                em.engine = engobj
                fn(em, *args)
                em.engine = None
            getattr(block, {"pe": "tensor", "act": "scalar", "dve": "vector",
                            "sy": "sync", "pl": "gpsimd"}[name])(body)

        with contextlib.ExitStack() as sa:
            ea = sa.enter_context
            a = NS()
            a.x_ln = ea(nc.sbuf_tensor("x_ln", [128, 4, E], F32))
            a.x_res = ea(nc.sbuf_tensor("x_res", [128, 2, E], F32))
            a.xn_sb = ea(nc.sbuf_tensor("xn_sb", [128, 2, E], BF16))
            a.ln1T = ea(nc.sbuf_tensor("ln1T", [128, NF, G * TC], BF16))
            a.qT = ea(nc.sbuf_tensor("qT", [128, NF, G * TC], BF16))
            a.kT = ea(nc.sbuf_tensor("kT", [128, NF, G * TC], BF16))
            a.v_sb = ea(nc.sbuf_tensor("v_sb", [128, G, E], BF16))
            a.attnT = ea(nc.sbuf_tensor("attnT", [128, NF, TC], BF16))
            a.sq_scr = ea(nc.sbuf_tensor("sq_scr", [128, E], F32))
            a.stat = ea(nc.sbuf_tensor("statA", [128, 20], F32))
            a.km_scr = ea(nc.sbuf_tensor("km_scr", [128, NF], F32))
            a.kmeansT = ea(nc.sbuf_tensor("kmeansT", [128, NF, C], BF16))
            a.vmeans = ea(nc.sbuf_tensor("vmeans", [C, E], BF16))
            a.vm_stage = ea(nc.sbuf_tensor("vm_stage", [1, E], F32))
            a.negmax = ea(nc.sbuf_tensor("negmax", [128, 32], F32))
            a.sums = ea(nc.sbuf_tensor("sumsA", [128, 32], F32))
            a.recip = ea(nc.sbuf_tensor("recipA", [128, 32], F32))
            a.ascr = ea(nc.sbuf_tensor("ascr", [128, 16, TC], BF16))
            a.an_scr = ea(nc.sbuf_tensor("an_scr", [128, 16, TC], BF16))
            a.aiscr = ea(nc.sbuf_tensor("aiscr", [128, 16, C], BF16))
            a.ain_scr = ea(nc.sbuf_tensor("ain_scr", [128, 16, C], BF16))
            a.aT_sb = ea(nc.sbuf_tensor("aT_sb", [128, 16, TC], BF16))
            a.aiT_sb = ea(nc.sbuf_tensor("aiT_sb", [C, 16, TC], BF16))
            a.xo_sb = ea(nc.sbuf_tensor("xo_sb", [128, 2, E], F32))
            a.bv_bf = ea(nc.sbuf_tensor("bv_bf", [1, E], BF16))
            a.bo_bf = ea(nc.sbuf_tensor("bo_bf", [1, E], BF16))
            a.wq_sb = ea(nc.sbuf_tensor("wq_sb", [128, NF, E], BF16))
            a.wk_sb = ea(nc.sbuf_tensor("wk_sb", [128, NF, E], BF16))
            a.wv_sb = ea(nc.sbuf_tensor("wv_sb", [128, NF, E], BF16))
            a.wo_sb = ea(nc.sbuf_tensor("wo_sb", [128, NF, E], BF16))
            a.ps_mm0 = ea(nc.psum_tensor("ps_mm0", [128, HF], F32))
            a.ps_mm1 = ea(nc.psum_tensor("ps_mm1", [128, HF], F32))
            a.ps_sA = ea(nc.psum_tensor("ps_sA", [128, HF], F32))
            a.ps_sB = ea(nc.psum_tensor("ps_sB", [128, HF], F32))
            a.ps_trA = ea(nc.psum_tensor("ps_trA", [128, 128], BF16))
            a.ps_trB = ea(nc.psum_tensor("ps_trB", [128, 128], BF16))
            a.ps_at0 = ea(nc.psum_tensor("ps_at0", [128, 128], F32))
            a.ps_at1 = ea(nc.psum_tensor("ps_at1", [128, 128], F32))
            a.seen_exp = set()
            a.seen_an = set()
            a.seen_tra = set()
            a.seen_atm = set()

            def phA(em):
                a.seen_exp = set()
                a.seen_an = set()
                a.seen_tra = set()
                a.seen_atm = set()
                emit_init(em, t)
                emit_phase_a(em, t, a)
                em.mark("endA_pe", "pe")
                em.mark("endA_act", "act")
                em.mark("endA_dve", "dve")
                em.mark("endA_io", "io")
                em.mark("endA_wl", "wl")
                em.mark("endA_plc", "plc")

            phA(emx)
            for name in ENG:
                run(name, phA)

        with contextlib.ExitStack() as sb:
            eb = sb.enter_context
            b = NS()
            b.x2_sb = eb(nc.sbuf_tensor("x2_sb", [128, TS, E], F32))
            b.xn2 = eb(nc.sbuf_tensor("xn2", [128, 2, E], F32))
            b.ln2T = eb(nc.sbuf_tensor("ln2T", [128, NF, TT], BF16))
            b.h_sb = eb(nc.sbuf_tensor("h_sb", [128, NHC, TT], BF16))
            b.w1b = eb(nc.sbuf_tensor("w1b", [128, 3, NF, SLAB * 128], BF16))
            b.w2b = eb(nc.sbuf_tensor("w2b", [128, NHC, E], BF16))
            b.stat2 = eb(nc.sbuf_tensor("stat2", [128, 20], F32))
            b.logits_tm = eb(nc.sbuf_tensor("logits_tm", [128, TS, NE], F32))
            b.gg = eb(nc.sbuf_tensor("gg", [128, 20], F32))
            b.maskA = eb(nc.sbuf_tensor("maskA", [128, NE], F32))
            b.maskAB = eb(nc.sbuf_tensor("maskAB", [128, NE], F32))
            b.xm = eb(nc.sbuf_tensor("xm", [128, NE], F32))
            b.combine = eb(nc.sbuf_tensor("combine", [128, TS, NE], F32))
            b.combineT = eb(nc.sbuf_tensor("combineT", [NE, TS, TC], BF16))
            b.identf = eb(nc.sbuf_tensor("identf", [128, 128], F32))
            b.ph0 = eb(nc.psum_tensor("ph0", [128, TT], F32))
            b.ph1 = eb(nc.psum_tensor("ph1", [128, TT], F32))
            b.po0 = eb(nc.psum_tensor("po0", [128, TT], F32))
            b.po1 = eb(nc.psum_tensor("po1", [128, TT], F32))
            if SPARSE:
                b.ln2g = eb(nc.sbuf_tensor("ln2g", [128, 1, E], F32))
                b.ln2Tg = b.h_sb[:][:, 0:NF, 512:512 + CAP]
                b.cbg = eb(nc.sbuf_tensor("cbg", [128, NSUB, NE], F32))
                b.idxg = eb(nc.sbuf_tensor("idxg", [128, NSUB, NE], mybir.dt.int32))
                b.out_g = eb(nc.sbuf_tensor("out_g", [128, E], F32))
                b.mask_all = eb(nc.sbuf_tensor("mask_all", [128, TS, NE], F32))
                b.maskA_all = eb(nc.sbuf_tensor("maskA_all", [128, TS, NE], F32))
                b.mask_bf = eb(nc.sbuf_tensor("mask_bf", [128, TS * NE], BF16))
                b.rankw = eb(nc.sbuf_tensor("rankw", [128, TS * NE], F32))
                b.cnt = eb(nc.sbuf_tensor("cntB", [1, TS * NE], F32))
                b.base = eb(nc.sbuf_tensor("baseB", [1, TS * NE], F32))
                b.idx_fill = eb(nc.sbuf_tensor("idx_fill", [128, NSUB, NE], mybir.dt.int32))
                b.roff = eb(nc.sbuf_tensor("roff", [128, 2, 2], mybir.dt.int32))
            b.ln2d_r = t.ln2d.rearrange("(c p) f -> p c f", p=128)
            b.combd_r = t.combd.rearrange("(c p) e -> p c e", p=128)
            b.gw_sb2 = t.gw_sb
            b.lastflush = [None for _ in range(TS)]

            def phB(em):
                b.lastflush = [None for _ in range(TS)]
                dveB = Eng(em, "dve")
                em.wait("dve", "persistf32")
                for k in ("endA_pe", "endA_act", "endA_dve", "endA_io", "endA_wl", "endA_plc"):
                    em.wait("dve", k)
                em.inc("dve", dveB.tensor_copy(b.identf[:], t.ident_sb[:]))
                em.mark("identf", "dve")
                emit_phase_b(em, t, b)
                em.mark("final_io", "io")
                em.wait("sy", "final_io")

            phB(emx)
            _simulate(emx.prog)
            for name in ENG:
                run(name, phB)

    return nc


def kernel(**inputs):
    xin = np.ascontiguousarray(np.asarray(inputs["L0_attn_hidden_states"], dtype=np.float32))
    f32 = lambda v: np.ascontiguousarray(np.asarray(v, dtype=np.float32))
    wq = f32(inputs["L0_attn_wq"]); wk = f32(inputs["L0_attn_wk"])
    wv = f32(inputs["L0_attn_wv"]); wo = f32(inputs["L0_attn_wo"])
    base = {
        "wqT": np.ascontiguousarray(wq.T), "wkT": np.ascontiguousarray(wk.T),
        "wvT": np.ascontiguousarray(wv.T), "woT": np.ascontiguousarray(wo.T),
        "bqT": np.ascontiguousarray(f32(inputs["L0_attn_bq"]).reshape(NF, 128).T),
        "bkT": np.ascontiguousarray(f32(inputs["L0_attn_bk"]).reshape(NF, 128).T),
        "bv": f32(inputs["L0_attn_bv"]).reshape(1, E),
        "bo": f32(inputs["L0_attn_bo"]).reshape(1, E),
        "gw": f32(inputs["L0_ffn_gate_w"]),
        "gb": f32(inputs["L0_ffn_gate_b"]).reshape(NE, 1),
        "w1": f32(inputs["L0_ffn_w1"]),
        "b1T": np.ascontiguousarray(f32(inputs["L0_ffn_b1"]).reshape(NE, NHC, 128).transpose(0, 2, 1)),
        "w2": f32(inputs["L0_ffn_w2"]),
        "b2": f32(inputs["L0_ffn_b2"]),
        "ws1": f32(inputs["L0_ffn_ws1"]),
        "bs1T": np.ascontiguousarray(f32(inputs["L0_ffn_bs1"]).reshape(NHC, 128).T),
        "ws2": f32(inputs["L0_ffn_ws2"]),
        "bs2": f32(inputs["L0_ffn_bs2"]).reshape(1, E),
        "ident": np.eye(128, dtype=ml_dtypes.bfloat16),
        "ltri": np.triu(np.ones((128, 128), dtype=ml_dtypes.bfloat16), 1),
        "evec": np.tile(np.arange(NE, dtype=np.float32), (128, 1)),
        "tokid": (np.arange(C, dtype=np.int32)[None, :] * 128
                  + np.arange(128, dtype=np.int32)[:, None]).astype(np.int32),
    }
    in_maps = []
    for core in range(NCORE):
        m = dict(base)
        m["x"] = np.ascontiguousarray(xin[core])
        in_maps.append(m)
    nc = build_nc()
    res = run_bass_kernel_spmd(nc, in_maps, core_ids=list(range(NCORE)))
    out = np.stack([np.asarray(res.results[i]["out"]) for i in range(NCORE)], axis=0)
    return out.astype(np.float32)
